# version 25
# baseline (speedup 1.0000x reference)
import numpy as np
import ml_dtypes
from contextlib import ExitStack

import concourse.bass as bass
import concourse.mybir as mybir
from concourse.bass_utils import run_bass_kernel_spmd

F32 = mybir.dt.float32
F16 = mybir.dt.float16
AF = mybir.ActivationFunctionType
ALU = mybir.AluOpType

C = 2048
H = 16
NCH = 16
TC = 2048
TT = 512
NT = TC // TT
NFF = 8192
P = 128
LN_EPS = 1e-5
MODW = 12288 // 4


SERIAL = True
SERIAL_DMA = []
SER_GROUPS = set()


class Q:
    ENGINES = ("sync", "gpsimd", "vector", "scalar", "tensor")

    def __init__(self, nc, stack):
        self.nc = nc
        self.q = {e: [] for e in self.ENGINES}
        self.sems = {}
        self.cnt = {}
        for name in ("st", "sv", "ss", "dsy", "dgp", "scc"):
            self.sems[name] = stack.enter_context(nc.semaphore(name))
            self.cnt[name] = 0

    ser_mark = 0

    def op(self, eng, fn, waits=(), inc=None, amt=1):
        if Q.ser_mark:
            waits = list(waits) + [("dgp", Q.ser_mark)]
        if SERIAL:
            waits = list(waits) + [(s, self.cnt[s]) for s in
                                   ("st", "sv", "ss", "dsy", "dgp")]
        elif SERIAL_DMA:
            ser = [x for x in SERIAL_DMA if True]
            if inc in ser:
                waits = list(waits) + [(s, self.cnt[s]) for s in
                                       ("st", "sv", "ss", "dsy", "dgp")]
            else:
                waits = list(waits) + [(s2, self.cnt[s2]) for s2 in ser]
        waits = [(s, v) for (s, v) in waits if v and v > 0]
        for s, v in waits:
            assert v <= self.cnt[s], f"wait {s}>={v} > emitted {self.cnt[s]} ({eng})"
        if inc is not None:
            self.cnt[inc] += amt
        sems = self.sems

        def body(e, fn=fn, waits=tuple(waits), inc=inc, amt=amt):
            for s, v in waits:
                e.wait_ge(sems[s], v)
            r = fn(e)
            if inc is not None:
                ins = r[-1] if isinstance(r, (list, tuple)) else r
                ins.then_inc(sems[inc], amt)
        self.q[eng].append(body)
        return self.cnt[inc] if inc is not None else None

    def dma(self, eng, out, in_, waits=(), slow=False, group=None):
        sem = "dsy" if eng == "sync" else "dgp"
        if group in SER_GROUPS:
            waits = list(waits) + [(s2, self.cnt[s2]) for s2 in
                                   ("st", "sv", "ss", "dsy", "dgp")]
        r = self.op(eng, lambda e, o=out, i=in_, s=slow: e.dma_start(
            out=o, in_=i, allow_slow_non_contiguous=s), waits, sem, 16)
        if group in SER_GROUPS:
            Q.ser_mark = r
        return r

    def now(self, s):
        return self.cnt[s]

    def finalize(self, block):
        for eng in self.ENGINES:
            fns = self.q[eng]
            if fns:
                getattr(block, eng)(lambda e, fns=fns: [f(e) for f in fns])


class Banks:

    def __init__(self, pA, pB):
        self.sl = [pA[:, i * 512:(i + 1) * 512] for i in range(4)] + \
                  [pB[:, i * 512:(i + 1) * 512] for i in range(4)]
        self.state = [("st", 0)] * 8
        self.i = 0

    def acquire(self):
        b = self.i % 8
        self.i += 1
        return b, self.sl[b], [self.state[b]]

    def release(self, b, sem, val):
        self.state[b] = (sem, val)


def build_nc():
    nc = bass.Bass("TRN2", target_bir_lowering=False, debug=False)

    xfm = nc.dram_tensor("xfm", [C, 3 + TC], F32, kind="ExternalInput").ap()
    wqT = nc.dram_tensor("wqT", [NCH, P, C], F32, kind="ExternalInput").ap()
    wkT = nc.dram_tensor("wkT", [NCH, P, C], F32, kind="ExternalInput").ap()
    wvT = nc.dram_tensor("wvT", [NCH, P, C], F32, kind="ExternalInput").ap()
    wgT = nc.dram_tensor("wgT", [NCH, P, C], F32, kind="ExternalInput").ap()
    woT = nc.dram_tensor("woT", [NCH, P, C], F32, kind="ExternalInput").ap()
    w0T = nc.dram_tensor("w0T", [64, P, C], F32, kind="ExternalInput").ap()
    w1T = nc.dram_tensor("w1T", [64, P, C], F32, kind="ExternalInput").ap()
    wmods = nc.dram_tensor("wmods", [C, MODW], F32, kind="ExternalInput").ap()
    cbv = nc.dram_tensor("cbv", [C], F32, kind="ExternalInput").ap()
    convw = nc.dram_tensor("convw", [C, 4], F32, kind="ExternalInput").ap()
    convb = nc.dram_tensor("convb", [C], F32, kind="ExternalInput").ap()
    gateb = nc.dram_tensor("gateb", [C], F32, kind="ExternalInput").ap()
    lnw = nc.dram_tensor("lnw", [C], F32, kind="ExternalInput").ap()
    lnb = nc.dram_tensor("lnb", [C], F32, kind="ExternalInput").ap()
    gam16 = nc.dram_tensor("gam16", [H], F32, kind="ExternalInput").ap()
    b0v = nc.dram_tensor("b0v", [NFF], F32, kind="ExternalInput").ap()
    b1v = nc.dram_tensor("b1v", [C], F32, kind="ExternalInput").ap()
    msel = nc.dram_tensor("msel", [4], F32, kind="ExternalInput").ap()
    hmask = nc.dram_tensor("hmask", [1], F32, kind="ExternalInput").ap()
    iotar = nc.dram_tensor("iotar", [1, TT], F32, kind="ExternalInput").ap()
    out_fm = nc.dram_tensor("out_fm", [C, TC], F32, kind="ExternalOutput").ap()

    wapair_d = nc.dram_tensor("wapair_d", [64, P, 2 * C], F16).ap()
    w0hi_d = nc.dram_tensor("w0hi_d", [64, P, C], F16).ap()
    w0lo_d = nc.dram_tensor("w0lo_d", [64, P, C], F16).ap()
    w1hi_d = nc.dram_tensor("w1hi_d", [64, P, C], F16).ap()
    w1lo_d = nc.dram_tensor("w1lo_d", [64, P, C], F16).ap()
    gate_d = nc.dram_tensor("gate_d", [C, TT], F32).ap()
    S_d = nc.dram_tensor("S_d", [C, TC], F32).ap()
    G_d = nc.dram_tensor("G_d", [C, TC], F32).ap()
    ag1_in = nc.dram_tensor("ag1_in", [1, MODW], F32).ap()
    ag1_out = nc.dram_tensor("ag1_out", [4, MODW], F32).ap()
    ag2_in = nc.dram_tensor("ag2_in", [P, H], F32).ap()
    ag2_out = nc.dram_tensor("ag2_out", [4 * P, H], F32).ap()

    stack = ExitStack()
    with stack:
        sb = lambda n_, s, d=F32: stack.enter_context(nc.sbuf_tensor(n_, s, d))

        ASZ = 50784
        arena = sb("arena", [P, ASZ])

        def reg(off, sz):
            return arena[:, off:off + sz]

        def reg3(off, k, t):
            return arena[:, off:off + k * t].rearrange("p (k t) -> p k t", k=k)

        def reg3h(off, k, t):
            return arena[:, off:off + (k * t) // 2].bitcast(F16).rearrange(
                "p (k t) -> p k t", k=k)

        O_X0 = 0
        O_X1 = 8240
        O_RB = 16480
        O_RC = 24672
        O_RD = 32864
        O_WBA = 41056
        O_SQ = 45152
        O_POW = 46176
        O_SPL = 46688
        O_SPLO = 48736
        O_H2 = O_SPL
        assert O_SPLO + 2048 <= ASZ

        consts = sb("consts", [P, 736])
        CV_ONEPCA, CV_SA, CV_GA = 0, 16, 32
        CV_ONEPCB, CV_SB, CV_GB = 48, 64, 80
        CV_LNW, CV_LNB, CV_CONVB, CV_GATEB, CV_B1 = 96, 112, 128, 144, 160
        CV_GAM, CV_LNGAM, CV_G2048 = 176, 192, 208
        CV_SIN, CV_WTAU = 224, 240
        CV_CARRY = 256
        CV_TAIL = 272
        CV_CJ = 320
        CV_MSEL = 384
        CV_HM = 388
        CV_CONVW = 392
        CV_B0 = 456
        CV_SILC = 520
        CV_MODF = 536
        CV_GT = 632
        CV_G512P = 664

        rowsb = sb("rowsb", [H, 1024])
        ones128 = sb("ones128", [P, P])
        onesc = sb("onesc", [P, 1])
        riot = sb("riot", [P, TT])
        eps5 = sb("eps5", [1, 4])
        modc = arena[0:1, O_RB:O_RB + MODW]

        pA = stack.enter_context(nc.psum_tensor("pA", [P, 2048], F32))
        pB = stack.enter_context(nc.psum_tensor("pB", [P, 2048], F32))

        q = Q(nc, stack)
        bk = Banks(pA, pB)
        sqg = [("sv", 0), ("sv", 0)]

        def vop(fn, waits=()):
            return q.op("vector", fn, waits, "sv")

        def sop(fn, waits=()):
            return q.op("scalar", fn, waits, "ss")

        def top(fn, waits=()):
            return q.op("tensor", fn, waits, "st")

        def ccol(i):
            return consts[:, i:i + 1]

        q.dma("sync", rowsb[0:1, 0:TT], iotar[:, :])
        q.dma("sync", consts[:, CV_CONVW:CV_CONVW + 64].rearrange(
            "p (k t) -> p k t", k=NCH),
              convw.rearrange("(k p) t -> p k t", p=P))
        for ap_, cv, n in ((convb, CV_CONVB, 16), (gateb, CV_GATEB, 16),
                           (lnw, CV_LNW, 16), (lnb, CV_LNB, 16),
                           (b1v, CV_B1, 16), (b0v, CV_B0, 64),
                           (cbv, CV_SILC, 16)):
            q.dma("sync", consts[:, cv:cv + n], ap_.rearrange("(k p) -> p k", p=P),
                  slow=True)
        for h in range(H):
            q.dma("sync", consts[:, CV_GAM + h:CV_GAM + h + 1],
                  gam16[h:h + 1].unsqueeze(0).broadcast_to([P, 1]))
        for j in range(4):
            q.dma("sync", consts[:, CV_MSEL + j:CV_MSEL + j + 1],
                  msel[j:j + 1].unsqueeze(0).broadcast_to([P, 1]))
        q.dma("sync", consts[:, CV_HM:CV_HM + 1],
              hmask[0:1].unsqueeze(0).broadcast_to([P, 1]))
        ldc = q.now("dsy")

        vop(lambda v: v.memset(ones128[:, :], 1.0))
        vop(lambda v: v.memset(onesc[:, :], 1.0))
        vop(lambda v: v.memset(eps5[:, 0:1], LN_EPS))
        sv_ones = vop(lambda v: v.memset(consts[:, CV_CARRY:CV_CARRY + 16], 0.0))

        s1 = sop(lambda s: s.activation(consts[:, CV_SILC:CV_SILC + 16],
                                        consts[:, CV_SILC:CV_SILC + 16], AF.Silu),
                 waits=[("dsy", ldc)])
        ss_lng = sop(lambda s: s.activation(consts[:, CV_LNGAM:CV_LNGAM + 16],
                                            consts[:, CV_GAM:CV_GAM + 16], AF.Ln))
        v1 = vop(lambda v: v.tensor_scalar(consts[:, CV_GT:CV_GT + 16],
                                           consts[:, CV_LNGAM:CV_LNGAM + 16],
                                           2048.0, None, ALU.mult),
                 waits=[("ss", ss_lng)])
        sop(lambda s: s.activation(consts[:, CV_G2048:CV_G2048 + 16],
                                   consts[:, CV_GT:CV_GT + 16], AF.Exp),
            waits=[("sv", v1)])
        vop(lambda v: v.memset(consts[:, CV_G512P:CV_G512P + 16], 1.0))
        prev_exp = 0
        for tau in range(1, NT):
            v2 = vop(lambda v, tau=tau: v.tensor_scalar(
                consts[:, CV_GT + 16:CV_GT + 32],
                consts[:, CV_LNGAM:CV_LNGAM + 16],
                512.0 * tau, None, ALU.mult),
                waits=[("ss", ss_lng), ("ss", prev_exp)])
            prev_exp = sop(lambda s, tau=tau: s.activation(
                consts[:, CV_G512P + 16 * tau:CV_G512P + 16 * (tau + 1)],
                consts[:, CV_GT + 16:CV_GT + 32], AF.Exp), waits=[("sv", v2)])
        ss_gam = q.now("ss")

        b_, psl, w_ = bk.acquire()
        tb = top(lambda t, psl=psl: t.matmul(psl, ones128[0:1, :],
                                             rowsb[0:1, 0:TT],
                                             start=True, stop=True),
                 waits=[("dsy", ldc), ("sv", sv_ones)] + w_)
        riot_sv = vop(lambda v, psl=psl: v.tensor_copy(riot[:, :], psl),
                      waits=[("st", tb)])
        bk.release(b_, "sv", riot_sv)

        split_units = []
        for src_t, hi_t, lo_t in ((w0T, w0hi_d, w0lo_d), (w1T, w1hi_d, w1lo_d)):
            for i in range(64):
                for hh in range(2):
                    c0, c1 = hh * 1024, (hh + 1) * 1024
                    split_units.append((src_t[i, :, c0:c1], hi_t[i, :, c0:c1],
                                        lo_t[i, :, c0:c1]))
        spl = {"i": 0, "store": [0, 0]}
        spl_in = [reg(O_SPL, 1024), reg(O_SPL + 1024, 1024)]
        spl_hi = [reg(O_SPLO, 512).bitcast(F16), reg(O_SPLO + 512, 512).bitcast(F16)]
        spl_lo = [reg(O_SPLO + 1024, 512).bitcast(F16),
                  reg(O_SPLO + 1536, 512).bitcast(F16)]

        def drip(n):
            for _ in range(n):
                i = spl["i"]
                if i >= len(split_units):
                    return
                spl["i"] += 1
                src, hd, ld_ = split_units[i]
                sl = i % 2
                d = q.dma("sync", spl_in[sl], src,
                          waits=[("dgp", spl["store"][sl])])
                s_ = sop(lambda s, sl=sl: s.copy(spl_hi[sl], spl_in[sl]),
                         waits=[("dsy", d)])
                v_ = vop(lambda v, sl=sl: v.tensor_tensor(
                    spl_lo[sl], spl_in[sl], spl_hi[sl], ALU.subtract),
                    waits=[("ss", s_)])
                q.dma("gpsimd", hd, spl_hi[sl], waits=[("sv", v_)])
                spl["store"][sl] = q.dma("gpsimd", ld_, spl_lo[sl])

        asp_in = [reg(O_RD, 2048), reg(O_RD + 2048, 2048)]
        asp_out = [reg(O_RD + 4096, 2048), reg(O_RD + 6144, 2048)]
        asp = {"i": 0, "store": [0, 0], "ready": {}}
        _amats = (wkT, wvT, wqT, woT)

        def adrip(n):
            for _ in range(n):
                i = asp["i"]
                if i >= 64:
                    return
                asp["i"] += 1
                mat, o = _amats[i // 16], i % 16
                sl = i % 2
                d = q.dma("sync", asp_in[sl], mat[o, :, :],
                          waits=[("dgp", asp["store"][sl])])
                hi = asp_out[sl][:, 0:1024].bitcast(F16)
                lo = asp_out[sl][:, 1024:2048].bitcast(F16)
                s_ = sop(lambda s, sl=sl, hi=hi: s.copy(hi, asp_in[sl]),
                         waits=[("dsy", d)])
                v_ = vop(lambda v, sl=sl, hi=hi, lo=lo: v.tensor_tensor(
                    lo, asp_in[sl], hi, ALU.subtract), waits=[("ss", s_)])
                asp["store"][sl] = q.dma("gpsimd", wapair_d[i, :, :],
                                         asp_out[sl].bitcast(F16),
                                         waits=[("sv", v_)])
                asp["ready"][i] = asp["store"][sl]

        wmbufs = [reg3(O_RC, NCH, TT), reg3(O_RC, NCH, TT)]
        NJ = MODW // TT
        ss_sil = s1
        prev_cp = 0
        wm_mm = [0, 0]
        for j in range(NJ):
            wb = wmbufs[j % 2]
            dj = q.dma("sync", wb[:, :, :],
                       wmods[:, j * TT:(j + 1) * TT].rearrange("(k p) t -> p k t", p=P),
                       waits=[("st", wm_mm[j % 2])])
            b_, psl, w_ = bk.acquire()
            tm = None
            for k in range(NCH):
                tm = top(lambda t, psl=psl, k=k, wb=wb: t.matmul(
                    psl[0:1, :], consts[:, CV_SILC + k:CV_SILC + k + 1],
                    wb[:, k, :], start=(k == 0), stop=(k == NCH - 1)),
                    waits=([("dsy", dj), ("ss", ss_sil)] + w_) if k == 0 else [])
            wm_mm[j % 2] = tm
            prev_cp = vop(lambda v, psl=psl, j=j: v.tensor_copy(
                modc[:, j * TT:(j + 1) * TT], psl[0:1, :]), waits=[("st", tm)])
            bk.release(b_, "sv", prev_cp)
        g1 = q.dma("gpsimd", ag1_in[:, :], modc[:, :], waits=[("sv", prev_cp)])
        cc1 = q.op("gpsimd", lambda g: g.collective_compute(
            "AllGather", ALU.bypass,
            replica_groups=[[0, 1, 2, 3], [4, 5, 6, 7]],
            ins=[ag1_in.opt()], outs=[ag1_out.opt()]),
            waits=[("dgp", g1)], inc="scc")
        gmod = 0
        for j in range(4):
            gmod = q.dma("gpsimd", consts[:, CV_MODF + 24 * j:CV_MODF + 24 * (j + 1)],
                         ag1_out[j, :].rearrange("(k p) -> p k", p=P),
                         waits=[("scc", cc1)] if j == 0 else [], slow=True)
        MF = CV_MODF
        vop(lambda v: v.tensor_copy(consts[:, CV_SA:CV_SA + 16],
                                    consts[:, MF:MF + 16]), waits=[("dgp", gmod)])
        vop(lambda v: v.tensor_scalar(consts[:, CV_ONEPCA:CV_ONEPCA + 16],
                                      consts[:, MF + 16:MF + 32], 1.0, None, ALU.add))
        vop(lambda v: v.tensor_copy(consts[:, CV_SB:CV_SB + 16],
                                    consts[:, MF + 48:MF + 64]))
        vop(lambda v: v.tensor_scalar(consts[:, CV_ONEPCB:CV_ONEPCB + 16],
                                      consts[:, MF + 64:MF + 80], 1.0, None, ALU.add))
        sv_e = 0
        for src, dst in ((MF + 32, CV_GA), (MF + 80, CV_GB)):
            sv_e = vop(lambda v, src=src: v.tensor_scalar(
                consts[:, CV_GT:CV_GT + 16], consts[:, src:src + 16],
                0.0, None, ALU.is_ge), waits=[("ss", ss_gam), ("sv", sv_e)])
            sv_e = vop(lambda v: v.tensor_scalar(
                consts[:, CV_GT:CV_GT + 16], consts[:, CV_GT:CV_GT + 16],
                2.0, -1.0, ALU.mult, ALU.add), waits=[("sv", sv_e)])
            sv_e = vop(lambda v, src=src: v.tensor_scalar(
                consts[:, CV_GT + 16:CV_GT + 32], consts[:, src:src + 16],
                -1.0, None, ALU.mult), waits=[("sv", sv_e)])
            sv_e = vop(lambda v, src=src: v.tensor_tensor(
                consts[:, CV_GT + 16:CV_GT + 32], consts[:, src:src + 16],
                consts[:, CV_GT + 16:CV_GT + 32], ALU.max), waits=[("sv", sv_e)])
            sv_e = vop(lambda v: v.tensor_scalar(
                consts[:, CV_GT + 16:CV_GT + 32], consts[:, CV_GT + 16:CV_GT + 32],
                0.25, None, ALU.max), waits=[("sv", sv_e)])
            sv_e = vop(lambda v, dst=dst: v.tensor_tensor(
                consts[:, dst:dst + 16], consts[:, CV_GT:CV_GT + 16],
                consts[:, CV_GT + 16:CV_GT + 32], ALU.mult), waits=[("sv", sv_e)])
        sv_mod = sv_e
        phA_st = q.now("st")

        def emit_ln(src3, n_t, dep_waits, xc_dst3=None):
            dst3 = src3 if xc_dst3 is None else xc_dst3
            b1_, ps1, w1_ = bk.acquire()
            b3_, ps3, w3_ = bk.acquire()
            tm = None
            tm3 = None
            prev_mm = 0
            for k in range(NCH):
                tm = top(lambda t, ps1=ps1, k=k: t.matmul(
                    ps1[0:1, 0:n_t], onesc[:, 0:1], src3[:, k, :],
                    start=(k == 0), stop=(k == NCH - 1)),
                    waits=(list(dep_waits) + w1_) if k == 0 else [])
                sq = reg(O_SQ, n_t) if k % 2 == 0 else reg(O_SQ + 512, n_t)
                vs = vop(lambda v, k=k, sq=sq: v.tensor_tensor(
                    sq, src3[:, k, :], src3[:, k, :], ALU.mult),
                    waits=(list(dep_waits) if k == 0 else []) +
                          ([("st", prev_mm)] if k >= 2 else []) +
                          ([sqg[k % 2]] if k < 2 else []))
                tm3 = top(lambda t, ps3=ps3, k=k, sq=sq: t.matmul(
                    ps3[0:1, 0:n_t], onesc[:, 0:1], sq,
                    start=(k == 0), stop=(k == NCH - 1)),
                    waits=[("sv", vs)] + (w3_ if k == 0 else []))
                prev_mm = tm3
            mu = rowsb[0:1, 0:n_t]
            e2 = rowsb[0:1, 512:512 + n_t]
            musq = reg(O_SQ, n_t)[0:1, :]
            vv = vop(lambda v, ps1=ps1: v.tensor_scalar(
                mu, ps1[0:1, 0:n_t], 1.0 / C, None, ALU.mult), waits=[("st", tm)])
            bk.release(b1_, "sv", vv)
            vv = vop(lambda v, ps3=ps3: v.tensor_scalar(
                e2, ps3[0:1, 0:n_t], 1.0 / C, None, ALU.mult),
                waits=[("st", tm3), ("sv", vv)])
            bk.release(b3_, "sv", vv)
            vv = vop(lambda v: v.tensor_tensor(musq, mu, mu, ALU.mult),
                     waits=[("sv", vv)])
            vv = vop(lambda v: v.tensor_tensor(e2, e2, musq, ALU.subtract),
                     waits=[("sv", vv)])
            sqg[0] = ("sv", vv)
            sqg[1] = ("st", tm3)
            sr = sop(lambda s: s.activation(e2, e2, AF.Sqrt, bias=eps5[0:1, 0:1]),
                     waits=[("sv", vv)])
            vv3 = vop(lambda v: v.reciprocal(e2, e2), waits=[("ss", sr)])
            vv3 = vop(lambda v: v.tensor_tensor(mu, mu, e2, ALU.mult),
                      waits=[("sv", vv3)])
            b4_, ps4, w4_ = bk.acquire()
            b5_, ps5, w5_ = bk.acquire()
            t4 = top(lambda t, ps4=ps4: t.matmul(ps4[:, 0:n_t], ones128[0:1, :], e2,
                                                 start=True, stop=True),
                     waits=[("sv", vv3)] + w4_)
            t5 = top(lambda t, ps5=ps5: t.matmul(ps5[:, 0:n_t], ones128[0:1, :], mu,
                                                 start=True, stop=True),
                     waits=[("sv", vv3)] + w5_)
            last = 0
            for k in range(NCH):
                last = vop(lambda v, k=k, ps4=ps4: v.tensor_tensor(
                    dst3[:, k, :], src3[:, k, :], ps4[:, 0:n_t], ALU.mult),
                    waits=[("st", t4)] if k == 0 else [])
            for k in range(NCH):
                last = vop(lambda v, k=k, ps5=ps5: v.tensor_tensor(
                    dst3[:, k, :], dst3[:, k, :], ps5[:, 0:n_t], ALU.subtract),
                    waits=[("st", t5)] if k == 0 else [])
            bk.release(b4_, "sv", last)
            bk.release(b5_, "sv", last)
            return last

        wba = [reg3(O_WBA, NCH, P), reg3(O_WBA + 2048, NCH, P)]
        wba_mm = [0, 0]

        def emit_attn_mm(wT_ap, rhs3, rhs_waits, evac, drip_n=0):
            for o in range(NCH):
                if drip_n:
                    drip(drip_n)
                sl = o % 2
                d = q.dma("sync", wba[sl][:, :, :],
                          wT_ap[o, :, :].rearrange("p (k o) -> p k o", k=NCH),
                          waits=[("st", wba_mm[sl])])
                b_, psl, w_ = bk.acquire()
                tm = None
                for k in range(NCH):
                    tm = top(lambda t, psl=psl, k=k, sl=sl: t.matmul(
                        psl, wba[sl][:, k, :], rhs3[:, k, :],
                        start=(k == 0), stop=(k == NCH - 1)),
                        waits=([("dsy", d)] + list(rhs_waits) + w_) if k == 0 else [])
                wba_mm[sl] = tm
                sem, val = evac(o, psl, tm)
                bk.release(b_, sem, val)

        wba_pair = [reg(O_WBA, 2048).bitcast(F16),
                    reg(O_WBA + 2048, 2048).bitcast(F16)]
        wba_h = [reg3h(O_WBA, NCH, P), reg3h(O_WBA + 2048, NCH, P)]
        wba_l = [reg3h(O_WBA + 1024, NCH, P), reg3h(O_WBA + 3072, NCH, P)]

        def emit_attn_mm_f16(mat_i, rhs_h, rhs_l, rhs_waits, evac, drip_n=0):
            for o in range(NCH):
                img = mat_i * 16 + o
                while img not in asp["ready"]:
                    adrip(1)
                if drip_n:
                    drip(drip_n)
                sl = o % 2
                d = q.dma("sync", wba_pair[sl], wapair_d[img, :, :],
                          waits=[("st", wba_mm[sl]), ("dgp", asp["ready"][img])])
                b_, psl, w_ = bk.acquire()
                tm = None
                for k in range(NCH):
                    w0 = ([("dsy", d)] + list(rhs_waits) + w_) if k == 0 else []
                    tm = top(lambda t, psl=psl, k=k, sl=sl: t.matmul(
                        psl, wba_h[sl][:, k, :], rhs_h[:, k, :],
                        start=(k == 0), stop=False), waits=w0)
                    tm = top(lambda t, psl=psl, k=k, sl=sl: t.matmul(
                        psl, wba_h[sl][:, k, :], rhs_l[:, k, :],
                        start=False, stop=False))
                    tm = top(lambda t, psl=psl, k=k, sl=sl: t.matmul(
                        psl, wba_l[sl][:, k, :], rhs_h[:, k, :],
                        start=False, stop=(k == NCH - 1)))
                wba_mm[sl] = tm
                sem, val = evac(o, psl, tm)
                bk.release(b_, sem, val)

        def emit_pairs(src3, dst_h, dst_l, waits):
            sv = 0
            for k in range(NCH):
                sv = vop(lambda v, k=k, src3=src3, dst_h=dst_h: v.tensor_copy(
                    dst_h[:, k, :], src3[:, k, :]),
                    waits=list(waits) if k == 0 else [])
            for k in range(NCH):
                sv = vop(lambda v, k=k, src3=src3, dst_h=dst_h, dst_l=dst_l:
                         v.tensor_tensor(dst_l[:, k, :], src3[:, k, :],
                                         dst_h[:, k, :], ALU.subtract))
            return sv

        def emit_norms(src3, ev_dep):
            prev_mm = [0, 0]
            last = 0
            for k in range(NCH):
                j = k % 2
                sq = reg(O_SQ, TT) if j == 0 else reg(O_SQ + 512, TT)
                vs = vop(lambda v, k=k, sq=sq, src3=src3: v.tensor_tensor(
                    sq, src3[:, k, :], src3[:, k, :], ALU.mult),
                    waits=([("ss", ev_dep)] if k == 0 else []) +
                          [("st", prev_mm[j]), sqg[j]])
                bn_, psn, wn_ = bk.acquire()
                tmn = top(lambda t, psn=psn, sq=sq: t.matmul(
                    psn, ones128[:, :], sq, start=True, stop=True),
                    waits=[("sv", vs)] + wn_)
                prev_mm[j] = tmn
                sq_s = sop(lambda s, psn=psn, sq=sq: s.activation(
                    sq, psn, AF.Sqrt), waits=[("st", tmn)])
                bk.release(bn_, "ss", sq_s)
                vn1 = vop(lambda v, sq=sq: v.tensor_scalar(
                    sq, sq, 1e-12, None, ALU.max), waits=[("ss", sq_s)])
                vn2 = vop(lambda v, sq=sq: v.reciprocal(sq, sq), waits=[("sv", vn1)])
                last = vop(lambda v, k=k, sq=sq, src3=src3: v.tensor_tensor(
                    src3[:, k, :], src3[:, k, :], sq, ALU.mult),
                    waits=[("sv", vn2)])
                sqg[j] = ("sv", last)
            return last

        XB = [reg3(O_X0, NCH, 3 + TT), reg3(O_X1, NCH, 3 + TT)]
        RB = reg3(O_RB, NCH, TT)
        RC = reg3(O_RC, NCH, TT)
        RD = reg3(O_RD, NCH, TT)
        ph_h = reg3h(O_RD, NCH, TT)
        ph_l = reg3h(O_RD + 4096, NCH, TT)
        powb = reg(O_SQ, TT)
        gstg = reg(O_POW, TT)
        tail3 = consts[:, CV_TAIL:CV_TAIL + 48].rearrange("p (k t) -> p k t", k=NCH)

        sin_cell = [0]

        def emit_ag2(scan_done):
            g2 = q.dma("gpsimd", ag2_in[:, :], consts[:, CV_CARRY:CV_CARRY + 16],
                       waits=[("sv", scan_done)])
            cc2 = q.op("gpsimd", lambda g: g.collective_compute(
                "AllGather", ALU.bypass,
                replica_groups=[[0, 1, 2, 3], [4, 5, 6, 7]],
                ins=[ag2_in.opt()], outs=[ag2_out.opt()]),
                waits=[("dgp", g2)], inc="scc")
            gc = 0
            for j in range(4):
                gc = q.dma("gpsimd", consts[:, CV_CJ + 16 * j:CV_CJ + 16 * (j + 1)],
                           ag2_out[j * P:(j + 1) * P, :],
                           waits=[("scc", cc2)] if j == 0 else [])
            sv_a = vop(lambda v: v.memset(consts[:, CV_SIN:CV_SIN + 16], 0.0),
                       waits=[("dgp", gc)])
            sv_a = vop(lambda v: v.tensor_copy(consts[:, CV_GT:CV_GT + 16],
                                               consts[:, CV_CJ:CV_CJ + 16]),
                       waits=[("sv", sv_a)])
            for j in range(4):
                sv_a = vop(lambda v, j=j: v.scalar_tensor_tensor(
                    consts[:, CV_SIN:CV_SIN + 16], consts[:, CV_GT:CV_GT + 16],
                    ccol(CV_MSEL + j), consts[:, CV_SIN:CV_SIN + 16],
                    ALU.mult, ALU.add), waits=[("sv", sv_a)])
                if j < 3:
                    sv_a = vop(lambda v: v.tensor_tensor(
                        consts[:, CV_GT:CV_GT + 16], consts[:, CV_GT:CV_GT + 16],
                        consts[:, CV_G2048:CV_G2048 + 16], ALU.mult),
                        waits=[("sv", sv_a)])
                    sv_a = vop(lambda v, j=j: v.tensor_tensor(
                        consts[:, CV_GT:CV_GT + 16], consts[:, CV_GT:CV_GT + 16],
                        consts[:, CV_CJ + 16 * (j + 1):CV_CJ + 16 * (j + 2)],
                        ALU.add), waits=[("sv", sv_a)])
            sin_cell[0] = sv_a

        x_pref = [0, 0]
        x_pref[0] = q.dma("gpsimd", XB[0][:, :, :],
                          xfm[:, 0:3 + TT].rearrange("(k p) t -> p k t", p=P))
        qmm_st = [0, 0]
        az_store = 0
        prev_gq = 0

        for tau in range(NT):
            buf = XB[tau % 2]
            bufm = buf[:, :, 3:3 + TT]
            if tau + 1 < NT:
                x_pref[(tau + 1) % 2] = q.dma(
                    "gpsimd", XB[(tau + 1) % 2][:, :, 3:3 + TT],
                    xfm[:, 3 + (tau + 1) * TT:3 + (tau + 2) * TT].rearrange(
                        "(k p) t -> p k t", p=P),
                    waits=[("st", qmm_st[(tau + 1) % 2]), ("scc", cc1)])
            if tau == 0:
                hsv = emit_ln(buf[:, :, 0:3], 3, [("dgp", x_pref[0]), ("sv", sv_mod)])
                for k in range(NCH):
                    hsv = vop(lambda v, k=k, buf=buf: v.tensor_scalar(
                        buf[:, k, 0:3], buf[:, k, 0:3],
                        ccol(CV_ONEPCA + k), ccol(CV_SA + k), ALU.mult, ALU.add),
                        waits=[("sv", hsv)] if k == 0 else [])
                hm = 0
                for k in range(NCH):
                    hm = vop(lambda v, k=k, buf=buf: v.tensor_scalar(
                        buf[:, k, 0:3], buf[:, k, 0:3], ccol(CV_HM), None, ALU.mult),
                        waits=[("sv", hsv)] if k == 0 else [])
                dep = [("sv", hm)]
            else:
                tl = vop(lambda v, buf=buf: v.tensor_copy(buf[:, :, 0:3], tail3),
                         waits=[("dgp", x_pref[tau % 2])])
                dep = [("sv", tl)]
            lsv = emit_ln(bufm, TT, dep + [("dgp", x_pref[tau % 2]), ("sv", sv_mod)])
            h1_sv = 0
            for k in range(NCH):
                h1_sv = vop(lambda v, k=k, bufm=bufm: v.tensor_scalar(
                    bufm[:, k, :], bufm[:, k, :],
                    ccol(CV_ONEPCA + k), ccol(CV_SA + k), ALU.mult, ALU.add),
                    waits=[("sv", lsv)] if k == 0 else [])
            tail_sv = vop(lambda v, buf=buf: v.tensor_copy(
                tail3, buf[:, :, TT:TT + 3]), waits=[("sv", h1_sv)])
            if tau == 0:
                adrip(64)
                asp_done = max(asp["store"])
                pair_dep = [("sv", h1_sv), ("dgp", asp_done)]
            else:
                pair_dep = [("sv", h1_sv), ("sv", prev_gq)]
            pair_sv = emit_pairs(bufm, ph_h, ph_l, pair_dep)

            def evac_to(dst3, extra):
                def f(o, psl, tm):
                    c = sop(lambda s, o=o, psl=psl, dst3=dst3: s.copy(
                        dst3[:, o, :], psl),
                        waits=[("st", tm)] + (extra if o == 0 else []))
                    return ("ss", c)
                return f

            emit_attn_mm_f16(0, ph_h, ph_l, [("sv", pair_sv)],
                             evac_to(RC, [("dgp", az_store), ("st", phA_st)]),
                             drip_n=1)
            k_ev = q.now("ss")

            csv = 0
            for k in range(NCH):
                csv = vop(lambda v, k=k, buf=buf: v.tensor_scalar(
                    RB[:, k, :], buf[:, k, 0:TT], ccol(CV_CONVW + 4 * k),
                    None, ALU.mult),
                    waits=[("sv", h1_sv), ("dgp", az_store)] if k == 0 else [])
                for kk in range(1, 4):
                    csv = vop(lambda v, k=k, kk=kk, buf=buf: v.scalar_tensor_tensor(
                        RB[:, k, :], buf[:, k, kk:kk + TT],
                        ccol(CV_CONVW + 4 * k + kk), RB[:, k, :], ALU.mult, ALU.add))
            ssv = 0
            for k in range(NCH):
                ssv = sop(lambda s, k=k: s.activation(
                    RB[:, k, :], RB[:, k, :], AF.Silu, bias=ccol(CV_CONVB + k)),
                    waits=[("sv", csv)] if k == 0 else [])
            kn_sv = emit_norms(RC, k_ev)

            def evac_u(o, psl, tm):
                c = vop(lambda v, o=o, psl=psl: v.tensor_tensor(
                    RC[:, o, :], RC[:, o, :], psl, ALU.mult),
                    waits=[("st", tm)] + ([("sv", kn_sv)] if o == 0 else []))
                return ("sv", c)

            emit_attn_mm_f16(1, ph_h, ph_l, [("sv", pair_sv)], evac_u, drip_n=1)
            u_sv = q.now("sv")

            gate_st = [0]

            def evac_gate(o, psl, tm):
                c = sop(lambda s, o=o, psl=psl: s.activation(
                    gstg, psl, AF.Sigmoid, bias=ccol(CV_GATEB + o)),
                    waits=[("st", tm), ("dgp", gate_st[0])])
                gate_st[0] = q.dma("gpsimd", gate_d[o * P:(o + 1) * P, :], gstg,
                                   waits=[("ss", c)])
                return ("ss", c)

            emit_attn_mm(wgT, RB, [("ss", ssv)], evac_gate, drip_n=1)
            gate_dma = gate_st[0]
            gate_mm_st = q.now("st")

            scan_k = []
            scan_sv = 0
            for k in range(NCH):
                scan_sv = vop(lambda v, k=k: v.tensor_tensor_scan(
                    RB[:, k, :], ccol(CV_GAM + k).broadcast_to([P, TT]),
                    RC[:, k, :], ccol(CV_CARRY + k), ALU.mult, ALU.add),
                    waits=([("sv", u_sv), ("st", gate_mm_st)] if k == 0 else []))
                scan_k.append(scan_sv)
                scan_sv = vop(lambda v, k=k: v.tensor_copy(
                    ccol(CV_CARRY + k), RB[:, k, TT - 1:TT]),
                    waits=[("sv", scan_sv)])
            if tau == NT - 1:
                emit_ag2(scan_sv)

            def evac_q(o, psl, tm):
                c = sop(lambda s, o=o, psl=psl: s.copy(RC[:, o, :], psl),
                        waits=[("st", tm), ("sv", scan_k[o])])
                return ("ss", c)

            emit_attn_mm_f16(2, ph_h, ph_l, [("sv", pair_sv)], evac_q, drip_n=1)
            q_ev = q.now("ss")
            qmm_st[tau % 2] = q.now("st")
            qn_sv = emit_norms(RC, q_ev)

            gq_sv = 0
            grel = [0, 0]
            for k in range(NCH):
                j = k % 2
                gb_buf = reg(O_SQ, TT) if j == 0 else reg(O_SQ + 512, TT)
                grel[j] = q.dma("gpsimd", gb_buf,
                                gate_d[k * P:(k + 1) * P, :],
                                waits=[("dgp", gate_dma), sqg[j]])
                gq_sv = vop(lambda v, k=k, gb_buf=gb_buf: v.tensor_tensor(
                    RC[:, k, :], RC[:, k, :], gb_buf, ALU.mult),
                    waits=[("dgp", grel[j])] +
                          ([("sv", qn_sv)] if k == 0 else []))
                sqg[j] = ("sv", gq_sv)
            prev_gq = gq_sv
            q.dma("gpsimd", S_d[:, tau * TT:(tau + 1) * TT].rearrange(
                "(k p) t -> p k t", p=P), RB[:, :, :], waits=[("sv", scan_sv)])
            az_store = q.dma("gpsimd", G_d[:, tau * TT:(tau + 1) * TT].rearrange(
                "(k p) t -> p k t", p=P), RC[:, :, :], waits=[("sv", gq_sv)])

        drip(len(split_units))
        assert spl["i"] == len(split_units)
        spl_done = q.now("dgp")
        phB_sv = q.now("sv")
        phB_ss = q.now("ss")
        phB_st = q.now("st")

        XPs = [XB[0][:, :, 0:TT], XB[1][:, :, 0:TT]]
        RBY = RB
        D16 = reg3h(O_RD, 32, TT)
        H2 = reg3h(O_H2, NCH, TT)
        wfflat = [arena[:, O_RC + 2048 * i:O_RC + 2048 * (i + 1)].bitcast(F16)
                  for i in range(4)]
        wf = [wfflat[i].rearrange("p (o k c) -> p o k c", o=2, k=NCH)
              for i in range(4)]
        wf_mm = [0, 0, 0, 0]
        e16 = [reg(O_POW, 256).bitcast(F16), reg(O_POW + 256, 256).bitcast(F16)]

        out_store = [0, 0]
        final_sv = phB_sv
        last_w0_mm = phB_st
        last_w1_mm = phB_st
        corr_sv = [0, 0]
        dZ = 0

        def emit_corr(tau, extra_waits):
            XP = XPs[tau % 2]
            dA = q.dma("gpsimd", XP[:, :, :],
                       S_d[:, tau * TT:(tau + 1) * TT].rearrange(
                           "(k p) t -> p k t", p=P),
                       waits=[("dgp", az_store), ("dgp", out_store[tau % 2])] +
                             list(extra_waits))
            wv = vop(lambda v, tau=tau: v.tensor_tensor(
                consts[:, CV_WTAU:CV_WTAU + 16], consts[:, CV_SIN:CV_SIN + 16],
                consts[:, CV_G512P + 16 * tau:CV_G512P + 16 * (tau + 1)], ALU.mult),
                waits=[("sv", sin_cell[0])])
            sv = 0
            pe = 0
            for k in range(NCH):
                pe = sop(lambda s, k=k: s.activation(
                    powb, riot[:, :], AF.Exp, scale=ccol(CV_LNGAM + k)),
                    waits=([("ss", ss_gam), ("sv", wv), ("dgp", dA), sqg[0]]
                           if k == 0 else [("sv", sv)]))
                sv = vop(lambda v, k=k, XP=XP: v.scalar_tensor_tensor(
                    XP[:, k, :], powb, ccol(CV_WTAU + k), XP[:, k, :],
                    ALU.mult, ALU.add),
                    waits=[("ss", pe)] + ([("sv", wv), ("dgp", dA)] if k == 0 else []))
            sqg[0] = ("sv", sv)
            corr_sv[tau % 2] = sv
            return sv

        for tau in range(NT):
            XP = XPs[tau % 2]
            if tau == 0:
                emit_corr(0, [("sv", final_sv), ("ss", phB_ss), ("st", phB_st)])
                dZ = q.dma("gpsimd", RBY[:, :, :],
                           G_d[:, 0:TT].rearrange("(k p) t -> p k t", p=P),
                           waits=[("dgp", az_store), ("sv", final_sv)])

            o_sv = 0
            for k in range(NCH):
                o_sv = vop(lambda v, k=k, XP=XP: v.tensor_tensor(
                    XP[:, k, :], XP[:, k, :], RBY[:, k, :], ALU.mult),
                    waits=[("dgp", dZ), ("sv", corr_sv[tau % 2])]
                    if k == 0 else [])

            pair_d = emit_pairs(XP, ph_h, ph_l,
                                [("sv", o_sv), ("st", last_w1_mm),
                                 ("st", qmm_st[0]), ("st", qmm_st[1])])

            def evac_y(o, psl, tm):
                c = sop(lambda s, o=o, psl=psl: s.copy(RBY[:, o, :], psl),
                        waits=[("st", tm)])
                return ("ss", c)

            emit_attn_mm_f16(3, ph_h, ph_l, [("sv", pair_d)], evac_y)
            wo_st = q.now("st")
            y_ev = q.now("ss")
            l2 = emit_ln(RBY, TT, [("ss", y_ev)])
            dx = q.dma("gpsimd", XP[:, :, :],
                       xfm[:, 3 + tau * TT:3 + (tau + 1) * TT].rearrange(
                           "(k p) t -> p k t", p=P),
                       waits=[("st", wo_st)])
            for k in range(NCH):
                l2 = vop(lambda v, k=k: v.tensor_scalar(
                    RBY[:, k, :], RBY[:, k, :], ccol(CV_LNW + k), ccol(CV_LNB + k),
                    ALU.mult, ALU.add), waits=[("sv", l2)] if k == 0 else [])
            xp_sv = 0
            for k in range(NCH):
                xp_sv = vop(lambda v, k=k, XP=XP: v.scalar_tensor_tensor(
                    XP[:, k, :], RBY[:, k, :], ccol(CV_GA + k), XP[:, k, :],
                    ALU.mult, ALU.add),
                    waits=[("sv", l2), ("dgp", dx)] if k == 0 else [])
            l3 = emit_ln(XP, TT, [("sv", xp_sv)], xc_dst3=RBY)
            h2_sv = 0
            h2_k = []
            for k in range(NCH):
                h2_sv = vop(lambda v, k=k: v.tensor_scalar(
                    H2[:, k, :], RBY[:, k, :], ccol(CV_ONEPCB + k), ccol(CV_SB + k),
                    ALU.mult, ALU.add),
                    waits=([("sv", l3)] + ([("st", last_w0_mm)] if tau > 0 else []) +
                           ([("dgp", spl_done)] if tau == 0 else []))
                    if k == 0 else [])
                h2_k.append(h2_sv)

            ge_k = [0] * 32
            for half in range(2):
                w0dep = [("dgp", spl_done)] + \
                    ([("st", last_w1_mm)] if (half == 1 or tau > 0) else [])
                for og in range(16):
                    o0 = half * 32 + og * 2
                    slh, sll = (og % 2) * 2, (og % 2) * 2 + 1
                    for oo in range(2):
                        q.dma("sync", wfflat[slh][:, oo * 2048:(oo + 1) * 2048],
                              w0hi_d[o0 + oo, :, :],
                              waits=[("st", wf_mm[slh]), ("dgp", spl_done),
                                     ("dgp", az_store)])
                        dwl = q.dma(
                            "sync", wfflat[sll][:, oo * 2048:(oo + 1) * 2048],
                            w0lo_d[o0 + oo, :, :],
                            waits=[("st", wf_mm[sll]), ("dgp", spl_done),
                                   ("dgp", az_store)])
                    for oo in range(2):
                        b_, psl, w_ = bk.acquire()
                        tm = None
                        for k in range(NCH):
                            tm = top(lambda t, psl=psl, k=k, slh=slh, oo=oo: t.matmul(
                                psl, wf[slh][:, oo, k, :],
                                H2[:, k, :], start=(k == 0), stop=False),
                                waits=([("dsy", dwl), ("sv", h2_k[k])] + w0dep + w_)
                                if (k == 0 and oo == 0) else
                                ([("dsy", dwl), ("sv", h2_k[k])] if k == 0 else
                                 [("sv", h2_k[k])] if oo == 0 else []))
                            tm = top(lambda t, psl=psl, k=k, sll=sll, oo=oo: t.matmul(
                                psl, wf[sll][:, oo, k, :],
                                H2[:, k, :], start=False, stop=(k == NCH - 1)))
                        wf_mm[slh] = tm
                        wf_mm[sll] = tm
                        last_w0_mm = tm
                        ge = sop(lambda s, psl=psl, og=og, oo=oo, o0=o0: s.activation(
                            D16[:, og * 2 + oo, :], psl, AF.Gelu,
                            bias=ccol(CV_B0 + o0 + oo)), waits=[("st", tm)])
                        ge_k[og * 2 + oo] = ge
                        bk.release(b_, "ss", ge)
                d16_ss = q.now("ss")
                if half == 1 and tau + 1 < NT:
                    emit_corr(tau + 1, [])
                for op_ in range(8):
                    banks = []
                    for oo in range(2):
                        banks.append(bk.acquire())
                    dmas = []
                    for kb in range(2):
                        kbg = half * 2 + kb
                        slh, sll = (kb % 2) * 2, (kb % 2) * 2 + 1
                        for oo in range(2):
                            q.dma("sync", wfflat[slh][:, oo * 2048:(oo + 1) * 2048],
                                  w1hi_d[kbg * 16 + op_ * 2 + oo, :, :],
                                  waits=[("st", wf_mm[slh]), ("dgp", spl_done)])
                            dl = q.dma(
                                "sync", wfflat[sll][:, oo * 2048:(oo + 1) * 2048],
                                w1lo_d[kbg * 16 + op_ * 2 + oo, :, :],
                                waits=[("st", wf_mm[sll]), ("dgp", spl_done)])
                        dmas.append((dl, slh, sll))
                    kb_last = [0, 0]
                    for kb in range(2):
                        dl, slh, sll = dmas[kb]
                        for k in range(NCH):
                            for oo in range(2):
                                bix, psl, wb_ = banks[oo]
                                w01 = [("dsy", dl)] if k == 0 else []
                                w01 = w01 + ([("ss", ge_k[kb * 16 + k])]
                                             if oo == 0 else [])
                                if k == 0 and kb == 0:
                                    w01 = w01 + wb_
                                tm = top(lambda t, psl=psl, k=k, kb=kb, oo=oo,
                                         slh=slh: t.matmul(
                                    psl, wf[slh][:, oo, k, :],
                                    D16[:, kb * 16 + k, :],
                                    start=(kb == 0 and k == 0), stop=False),
                                    waits=w01)
                                tm = top(lambda t, psl=psl, k=k, kb=kb, oo=oo,
                                         sll=sll: t.matmul(
                                    psl, wf[sll][:, oo, k, :],
                                    D16[:, kb * 16 + k, :],
                                    start=False, stop=(kb == 1 and k == NCH - 1)))
                                kb_last[kb] = tm
                                if kb == 1 and k == NCH - 1:
                                    last_w1_mm = tm
                        wf_mm[slh] = kb_last[kb]
                        wf_mm[sll] = kb_last[kb]
                    for oo in range(2):
                        bix, psl, wb_ = banks[oo]
                        e = op_ * 2 + oo
                        if half == 0:
                            ev = sop(lambda s, psl=psl, e=e: s.activation(
                                RBY[:, e, :], psl, AF.Identity, bias=ccol(CV_B1 + e)),
                                waits=[("st", last_w1_mm)])
                            bk.release(bix, "ss", ev)
                        else:
                            ev = vop(lambda v, psl=psl, e=e: v.tensor_tensor(
                                e16[e % 2], RBY[:, e, :], psl, ALU.add),
                                waits=[("st", last_w1_mm)] +
                                      ([("sv", final_sv)] if e >= 2 else []))
                            bk.release(bix, "sv", ev)
                            final_sv = vop(lambda v, e=e, XP=XP: v.scalar_tensor_tensor(
                                XP[:, e, :], e16[e % 2], ccol(CV_GB + e),
                                XP[:, e, :], ALU.mult, ALU.add),
                                waits=[("sv", ev)])
            out_store[tau % 2] = q.dma(
                "gpsimd", out_fm[:, tau * TT:(tau + 1) * TT].rearrange(
                    "(k p) t -> p k t", p=P), XP[:, :, :], waits=[("sv", final_sv)])
            if tau + 1 < NT:
                dZ = q.dma("gpsimd", RBY[:, :, :],
                           G_d[:, (tau + 1) * TT:(tau + 2) * TT].rearrange(
                               "(k p) t -> p k t", p=P),
                           waits=[("sv", final_sv)])

        q.op("sync", lambda e: None, waits=[("dgp", q.now("dgp")),
                                            ("dsy", q.now("dsy"))])

        block = stack.enter_context(nc.Block())
        q.finalize(block)
    return nc
def kernel(**inputs):
    x = np.ascontiguousarray(inputs["L0_attn_x"], dtype=np.float32)
    def attn_img(w):
        return np.ascontiguousarray(
            np.asarray(w, np.float32).T.reshape(16, 128, 16, 128)
            .transpose(2, 1, 0, 3).reshape(16, 128, 2048))

    WqT = attn_img(inputs["L0_attn_Wq"])
    WkT = attn_img(inputs["L0_attn_Wk"])
    WvT = attn_img(inputs["L0_attn_Wv"])
    WgT = attn_img(inputs["L0_attn_gate_w"])
    WoT = attn_img(inputs["L0_attn_Wo"])
    W0T = np.ascontiguousarray(
        np.asarray(inputs["L0_ffn_W0"], np.float32).T.reshape(16, 128, 64, 128)
        .transpose(2, 1, 0, 3).reshape(64, 128, 2048))
    W1T = np.ascontiguousarray(
        np.asarray(inputs["L0_ffn_W1"], np.float32).T.reshape(4, 16, 128, 16, 128)
        .transpose(0, 3, 2, 1, 4).reshape(64, 128, 2048))
    wmod = np.asarray(inputs["L0_wmod"], dtype=np.float32)
    cin = np.asarray(inputs["L0_c"], dtype=np.float32)
    conv_w = np.ascontiguousarray(inputs["L0_attn_conv_w"][:, 0, :], dtype=np.float32)
    convb = np.ascontiguousarray(inputs["L0_attn_conv_b"], dtype=np.float32)
    gateb = np.ascontiguousarray(inputs["L0_attn_gate_b"], dtype=np.float32)
    lnw = np.ascontiguousarray(inputs["L0_attn_ln_w"], dtype=np.float32)
    lnb = np.ascontiguousarray(inputs["L0_attn_ln_b"], dtype=np.float32)
    gam = np.ascontiguousarray(inputs["L0_attn_gamma"].reshape(H), dtype=np.float32)
    b0 = np.ascontiguousarray(inputs["L0_ffn_bias0"], dtype=np.float32)
    b1 = np.ascontiguousarray(inputs["L0_ffn_bias1"], dtype=np.float32)

    iota = np.arange(1, TT + 1, dtype=np.float32).reshape(1, TT)

    nc = build_nc()
    in_maps = []
    for r in range(8):
        b, pos = r // 4, r % 4
        t0 = TC * pos
        if t0 == 0:
            halo = np.zeros((C, 3), np.float32)
        else:
            halo = np.ascontiguousarray(x[b, t0 - 3:t0].T)
        xfm = np.ascontiguousarray(
            np.concatenate([halo, x[b, t0:t0 + TC].T], axis=1))
        ms = np.zeros(4, np.float32)
        if pos > 0:
            ms[pos - 1] = 1.0
        in_maps.append({
            "xfm": xfm, "wqT": WqT, "wkT": WkT, "wvT": WvT, "wgT": WgT,
            "woT": WoT, "w0T": W0T, "w1T": W1T,
            "wmods": np.ascontiguousarray(wmod[:, pos * MODW:(pos + 1) * MODW]),
            "cbv": np.ascontiguousarray(cin[b]),
            "convw": conv_w, "convb": convb, "gateb": gateb,
            "lnw": lnw, "lnb": lnb, "gam16": gam, "b0v": b0, "b1v": b1,
            "msel": ms, "hmask": np.array([0.0 if pos == 0 else 1.0], np.float32),
            "iotar": iota,
        })
    res = run_bass_kernel_spmd(nc, in_maps, core_ids=list(range(8)))
    out = np.empty((2, 8192, C), np.float32)
    for r in range(8):
        b, pos = r // 4, r % 4
        out[b, TC * pos:TC * (pos + 1)] = res.results[r]["out_fm"].T
    return out


# version 27
# speedup vs baseline: 2.2194x; 2.2194x over previous
import numpy as np
import ml_dtypes
from contextlib import ExitStack

import concourse.bass as bass
import concourse.mybir as mybir
from concourse.bass_utils import run_bass_kernel_spmd

F32 = mybir.dt.float32
F16 = mybir.dt.float16
AF = mybir.ActivationFunctionType
ALU = mybir.AluOpType

C = 2048
H = 16
NCH = 16
TC = 2048
TT = 512
NT = TC // TT
NFF = 8192
P = 128
LN_EPS = 1e-5
MODW = 12288 // 4


SERIAL = False
SERIAL_DMA = ["dgp", "dsy"]
SER_GROUPS = set()


class Q:
    ENGINES = ("sync", "gpsimd", "vector", "scalar", "tensor")

    def __init__(self, nc, stack):
        self.nc = nc
        self.q = {e: [] for e in self.ENGINES}
        self.sems = {}
        self.cnt = {}
        for name in ("st", "sv", "ss", "dsy", "dgp", "scc"):
            self.sems[name] = stack.enter_context(nc.semaphore(name))
            self.cnt[name] = 0

    ser_mark = 0

    def op(self, eng, fn, waits=(), inc=None, amt=1):
        if Q.ser_mark:
            waits = list(waits) + [("dgp", Q.ser_mark)]
        if SERIAL:
            waits = list(waits) + [(s, self.cnt[s]) for s in
                                   ("st", "sv", "ss", "dsy", "dgp")]
        elif SERIAL_DMA:
            ser = [x for x in SERIAL_DMA if True]
            if inc in ser:
                waits = list(waits) + [(s, self.cnt[s]) for s in
                                       ("st", "sv", "ss", "dsy", "dgp")]
            else:
                waits = list(waits) + [(s2, self.cnt[s2]) for s2 in ser]
        waits = [(s, v) for (s, v) in waits if v and v > 0]
        for s, v in waits:
            assert v <= self.cnt[s], f"wait {s}>={v} > emitted {self.cnt[s]} ({eng})"
        if inc is not None:
            self.cnt[inc] += amt
        sems = self.sems

        def body(e, fn=fn, waits=tuple(waits), inc=inc, amt=amt):
            for s, v in waits:
                e.wait_ge(sems[s], v)
            r = fn(e)
            if inc is not None:
                ins = r[-1] if isinstance(r, (list, tuple)) else r
                ins.then_inc(sems[inc], amt)
        self.q[eng].append(body)
        return self.cnt[inc] if inc is not None else None

    def dma(self, eng, out, in_, waits=(), slow=False, group=None):
        sem = "dsy" if eng == "sync" else "dgp"
        if group in SER_GROUPS:
            waits = list(waits) + [(s2, self.cnt[s2]) for s2 in
                                   ("st", "sv", "ss", "dsy", "dgp")]
        r = self.op(eng, lambda e, o=out, i=in_, s=slow: e.dma_start(
            out=o, in_=i, allow_slow_non_contiguous=s), waits, sem, 16)
        if group in SER_GROUPS:
            Q.ser_mark = r
        return r

    def now(self, s):
        return self.cnt[s]

    def finalize(self, block):
        for eng in self.ENGINES:
            fns = self.q[eng]
            if fns:
                getattr(block, eng)(lambda e, fns=fns: [f(e) for f in fns])


class Banks:

    def __init__(self, pA, pB):
        self.sl = [pA[:, i * 512:(i + 1) * 512] for i in range(4)] + \
                  [pB[:, i * 512:(i + 1) * 512] for i in range(4)]
        self.state = [("st", 0)] * 8
        self.i = 0

    def acquire(self):
        b = self.i % 8
        self.i += 1
        return b, self.sl[b], [self.state[b]]

    def release(self, b, sem, val):
        self.state[b] = (sem, val)


def build_nc():
    nc = bass.Bass("TRN2", target_bir_lowering=False, debug=False)

    xfm = nc.dram_tensor("xfm", [C, 3 + TC], F32, kind="ExternalInput").ap()
    wqT = nc.dram_tensor("wqT", [NCH, P, C], F32, kind="ExternalInput").ap()
    wkT = nc.dram_tensor("wkT", [NCH, P, C], F32, kind="ExternalInput").ap()
    wvT = nc.dram_tensor("wvT", [NCH, P, C], F32, kind="ExternalInput").ap()
    wgT = nc.dram_tensor("wgT", [NCH, P, C], F32, kind="ExternalInput").ap()
    woT = nc.dram_tensor("woT", [NCH, P, C], F32, kind="ExternalInput").ap()
    w0T = nc.dram_tensor("w0T", [64, P, C], F32, kind="ExternalInput").ap()
    w1T = nc.dram_tensor("w1T", [64, P, C], F32, kind="ExternalInput").ap()
    wmods = nc.dram_tensor("wmods", [C, MODW], F32, kind="ExternalInput").ap()
    cbv = nc.dram_tensor("cbv", [C], F32, kind="ExternalInput").ap()
    convw = nc.dram_tensor("convw", [C, 4], F32, kind="ExternalInput").ap()
    convb = nc.dram_tensor("convb", [C], F32, kind="ExternalInput").ap()
    gateb = nc.dram_tensor("gateb", [C], F32, kind="ExternalInput").ap()
    lnw = nc.dram_tensor("lnw", [C], F32, kind="ExternalInput").ap()
    lnb = nc.dram_tensor("lnb", [C], F32, kind="ExternalInput").ap()
    gam16 = nc.dram_tensor("gam16", [H], F32, kind="ExternalInput").ap()
    b0v = nc.dram_tensor("b0v", [NFF], F32, kind="ExternalInput").ap()
    b1v = nc.dram_tensor("b1v", [C], F32, kind="ExternalInput").ap()
    msel = nc.dram_tensor("msel", [4], F32, kind="ExternalInput").ap()
    hmask = nc.dram_tensor("hmask", [1], F32, kind="ExternalInput").ap()
    iotar = nc.dram_tensor("iotar", [1, TT], F32, kind="ExternalInput").ap()
    out_fm = nc.dram_tensor("out_fm", [C, TC], F32, kind="ExternalOutput").ap()

    wapair_d = nc.dram_tensor("wapair_d", [64, P, 2 * C], F16).ap()
    w0hi_d = nc.dram_tensor("w0hi_d", [64, P, C], F16).ap()
    w0lo_d = nc.dram_tensor("w0lo_d", [64, P, C], F16).ap()
    w1hi_d = nc.dram_tensor("w1hi_d", [64, P, C], F16).ap()
    w1lo_d = nc.dram_tensor("w1lo_d", [64, P, C], F16).ap()
    gate_d = nc.dram_tensor("gate_d", [C, TT], F32).ap()
    S_d = nc.dram_tensor("S_d", [C, TC], F32).ap()
    G_d = nc.dram_tensor("G_d", [C, TC], F32).ap()
    ag1_in = nc.dram_tensor("ag1_in", [1, MODW], F32).ap()
    ag1_out = nc.dram_tensor("ag1_out", [4, MODW], F32).ap()
    ag2_in = nc.dram_tensor("ag2_in", [P, H], F32).ap()
    ag2_out = nc.dram_tensor("ag2_out", [4 * P, H], F32).ap()

    stack = ExitStack()
    with stack:
        sb = lambda n_, s, d=F32: stack.enter_context(nc.sbuf_tensor(n_, s, d))

        ASZ = 50784
        arena = sb("arena", [P, ASZ])

        def reg(off, sz):
            return arena[:, off:off + sz]

        def reg3(off, k, t):
            return arena[:, off:off + k * t].rearrange("p (k t) -> p k t", k=k)

        def reg3h(off, k, t):
            return arena[:, off:off + (k * t) // 2].bitcast(F16).rearrange(
                "p (k t) -> p k t", k=k)

        O_X0 = 0
        O_X1 = 8240
        O_RB = 16480
        O_RC = 24672
        O_RD = 32864
        O_WBA = 41056
        O_SQ = 45152
        O_POW = 46176
        O_SPL = 46688
        O_SPLO = 48736
        O_H2 = O_SPL
        assert O_SPLO + 2048 <= ASZ

        consts = sb("consts", [P, 736])
        CV_ONEPCA, CV_SA, CV_GA = 0, 16, 32
        CV_ONEPCB, CV_SB, CV_GB = 48, 64, 80
        CV_LNW, CV_LNB, CV_CONVB, CV_GATEB, CV_B1 = 96, 112, 128, 144, 160
        CV_GAM, CV_LNGAM, CV_G2048 = 176, 192, 208
        CV_SIN, CV_WTAU = 224, 240
        CV_CARRY = 256
        CV_TAIL = 272
        CV_CJ = 320
        CV_MSEL = 384
        CV_HM = 388
        CV_CONVW = 392
        CV_B0 = 456
        CV_SILC = 520
        CV_MODF = 536
        CV_GT = 632
        CV_G512P = 664

        rowsb = sb("rowsb", [H, 1024])
        ones128 = sb("ones128", [P, P])
        onesc = sb("onesc", [P, 1])
        riot = sb("riot", [P, TT])
        eps5 = sb("eps5", [1, 4])
        modc = arena[0:1, O_RB:O_RB + MODW]

        pA = stack.enter_context(nc.psum_tensor("pA", [P, 2048], F32))
        pB = stack.enter_context(nc.psum_tensor("pB", [P, 2048], F32))

        q = Q(nc, stack)
        bk = Banks(pA, pB)
        sqg = [("sv", 0), ("sv", 0)]

        def vop(fn, waits=()):
            return q.op("vector", fn, waits, "sv")

        def sop(fn, waits=()):
            return q.op("scalar", fn, waits, "ss")

        def top(fn, waits=()):
            return q.op("tensor", fn, waits, "st")

        def ccol(i):
            return consts[:, i:i + 1]

        q.dma("sync", rowsb[0:1, 0:TT], iotar[:, :])
        q.dma("sync", consts[:, CV_CONVW:CV_CONVW + 64].rearrange(
            "p (k t) -> p k t", k=NCH),
              convw.rearrange("(k p) t -> p k t", p=P))
        for ap_, cv, n in ((convb, CV_CONVB, 16), (gateb, CV_GATEB, 16),
                           (lnw, CV_LNW, 16), (lnb, CV_LNB, 16),
                           (b1v, CV_B1, 16), (b0v, CV_B0, 64),
                           (cbv, CV_SILC, 16)):
            q.dma("sync", consts[:, cv:cv + n], ap_.rearrange("(k p) -> p k", p=P),
                  slow=True)
        for h in range(H):
            q.dma("sync", consts[:, CV_GAM + h:CV_GAM + h + 1],
                  gam16[h:h + 1].unsqueeze(0).broadcast_to([P, 1]))
        for j in range(4):
            q.dma("sync", consts[:, CV_MSEL + j:CV_MSEL + j + 1],
                  msel[j:j + 1].unsqueeze(0).broadcast_to([P, 1]))
        q.dma("sync", consts[:, CV_HM:CV_HM + 1],
              hmask[0:1].unsqueeze(0).broadcast_to([P, 1]))
        ldc = q.now("dsy")

        vop(lambda v: v.memset(ones128[:, :], 1.0))
        vop(lambda v: v.memset(onesc[:, :], 1.0))
        vop(lambda v: v.memset(eps5[:, 0:1], LN_EPS))
        sv_ones = vop(lambda v: v.memset(consts[:, CV_CARRY:CV_CARRY + 16], 0.0))

        s1 = sop(lambda s: s.activation(consts[:, CV_SILC:CV_SILC + 16],
                                        consts[:, CV_SILC:CV_SILC + 16], AF.Silu),
                 waits=[("dsy", ldc)])
        ss_lng = sop(lambda s: s.activation(consts[:, CV_LNGAM:CV_LNGAM + 16],
                                            consts[:, CV_GAM:CV_GAM + 16], AF.Ln))
        v1 = vop(lambda v: v.tensor_scalar(consts[:, CV_GT:CV_GT + 16],
                                           consts[:, CV_LNGAM:CV_LNGAM + 16],
                                           2048.0, None, ALU.mult),
                 waits=[("ss", ss_lng)])
        sop(lambda s: s.activation(consts[:, CV_G2048:CV_G2048 + 16],
                                   consts[:, CV_GT:CV_GT + 16], AF.Exp),
            waits=[("sv", v1)])
        vop(lambda v: v.memset(consts[:, CV_G512P:CV_G512P + 16], 1.0))
        prev_exp = 0
        for tau in range(1, NT):
            v2 = vop(lambda v, tau=tau: v.tensor_scalar(
                consts[:, CV_GT + 16:CV_GT + 32],
                consts[:, CV_LNGAM:CV_LNGAM + 16],
                512.0 * tau, None, ALU.mult),
                waits=[("ss", ss_lng), ("ss", prev_exp)])
            prev_exp = sop(lambda s, tau=tau: s.activation(
                consts[:, CV_G512P + 16 * tau:CV_G512P + 16 * (tau + 1)],
                consts[:, CV_GT + 16:CV_GT + 32], AF.Exp), waits=[("sv", v2)])
        ss_gam = q.now("ss")

        b_, psl, w_ = bk.acquire()
        tb = top(lambda t, psl=psl: t.matmul(psl, ones128[0:1, :],
                                             rowsb[0:1, 0:TT],
                                             start=True, stop=True),
                 waits=[("dsy", ldc), ("sv", sv_ones)] + w_)
        riot_sv = vop(lambda v, psl=psl: v.tensor_copy(riot[:, :], psl),
                      waits=[("st", tb)])
        bk.release(b_, "sv", riot_sv)

        split_units = []
        for src_t, hi_t, lo_t in ((w0T, w0hi_d, w0lo_d), (w1T, w1hi_d, w1lo_d)):
            for i in range(64):
                for hh in range(2):
                    c0, c1 = hh * 1024, (hh + 1) * 1024
                    split_units.append((src_t[i, :, c0:c1], hi_t[i, :, c0:c1],
                                        lo_t[i, :, c0:c1]))
        spl = {"i": 0, "store": [0, 0]}
        spl_in = [reg(O_SPL, 1024), reg(O_SPL + 1024, 1024)]
        spl_hi = [reg(O_SPLO, 512).bitcast(F16), reg(O_SPLO + 512, 512).bitcast(F16)]
        spl_lo = [reg(O_SPLO + 1024, 512).bitcast(F16),
                  reg(O_SPLO + 1536, 512).bitcast(F16)]

        def drip(n):
            for _ in range(n):
                i = spl["i"]
                if i >= len(split_units):
                    return
                spl["i"] += 1
                src, hd, ld_ = split_units[i]
                sl = i % 2
                d = q.dma("sync", spl_in[sl], src,
                          waits=[("dgp", spl["store"][sl])])
                s_ = sop(lambda s, sl=sl: s.copy(spl_hi[sl], spl_in[sl]),
                         waits=[("dsy", d)])
                v_ = vop(lambda v, sl=sl: v.tensor_tensor(
                    spl_lo[sl], spl_in[sl], spl_hi[sl], ALU.subtract),
                    waits=[("ss", s_)])
                q.dma("gpsimd", hd, spl_hi[sl], waits=[("sv", v_)])
                spl["store"][sl] = q.dma("gpsimd", ld_, spl_lo[sl])

        asp_in = [reg(O_RD, 2048), reg(O_RD + 2048, 2048)]
        asp_out = [reg(O_RD + 4096, 2048), reg(O_RD + 6144, 2048)]
        asp = {"i": 0, "store": [0, 0], "ready": {}}
        _amats = (wkT, wvT, wqT, woT)

        def adrip(n):
            for _ in range(n):
                i = asp["i"]
                if i >= 64:
                    return
                asp["i"] += 1
                mat, o = _amats[i // 16], i % 16
                sl = i % 2
                d = q.dma("sync", asp_in[sl], mat[o, :, :],
                          waits=[("dgp", asp["store"][sl])])
                hi = asp_out[sl][:, 0:1024].bitcast(F16)
                lo = asp_out[sl][:, 1024:2048].bitcast(F16)
                s_ = sop(lambda s, sl=sl, hi=hi: s.copy(hi, asp_in[sl]),
                         waits=[("dsy", d)])
                v_ = vop(lambda v, sl=sl, hi=hi, lo=lo: v.tensor_tensor(
                    lo, asp_in[sl], hi, ALU.subtract), waits=[("ss", s_)])
                asp["store"][sl] = q.dma("gpsimd", wapair_d[i, :, :],
                                         asp_out[sl].bitcast(F16),
                                         waits=[("sv", v_)])
                asp["ready"][i] = asp["store"][sl]

        wmbufs = [reg3(O_RC, NCH, TT), reg3(O_RC, NCH, TT)]
        NJ = MODW // TT
        ss_sil = s1
        prev_cp = 0
        wm_mm = [0, 0]
        for j in range(NJ):
            wb = wmbufs[j % 2]
            dj = q.dma("sync", wb[:, :, :],
                       wmods[:, j * TT:(j + 1) * TT].rearrange("(k p) t -> p k t", p=P),
                       waits=[("st", wm_mm[j % 2])])
            b_, psl, w_ = bk.acquire()
            tm = None
            for k in range(NCH):
                tm = top(lambda t, psl=psl, k=k, wb=wb: t.matmul(
                    psl[0:1, :], consts[:, CV_SILC + k:CV_SILC + k + 1],
                    wb[:, k, :], start=(k == 0), stop=(k == NCH - 1)),
                    waits=([("dsy", dj), ("ss", ss_sil)] + w_) if k == 0 else [])
            wm_mm[j % 2] = tm
            prev_cp = vop(lambda v, psl=psl, j=j: v.tensor_copy(
                modc[:, j * TT:(j + 1) * TT], psl[0:1, :]), waits=[("st", tm)])
            bk.release(b_, "sv", prev_cp)
        g1 = q.dma("gpsimd", ag1_in[:, :], modc[:, :], waits=[("sv", prev_cp)])
        cc1 = q.op("gpsimd", lambda g: g.collective_compute(
            "AllGather", ALU.bypass,
            replica_groups=[[0, 1, 2, 3], [4, 5, 6, 7]],
            ins=[ag1_in.opt()], outs=[ag1_out.opt()]),
            waits=[("dgp", g1)], inc="scc")
        gmod = 0
        for j in range(4):
            gmod = q.dma("gpsimd", consts[:, CV_MODF + 24 * j:CV_MODF + 24 * (j + 1)],
                         ag1_out[j, :].rearrange("(k p) -> p k", p=P),
                         waits=[("scc", cc1)] if j == 0 else [], slow=True)
        MF = CV_MODF
        vop(lambda v: v.tensor_copy(consts[:, CV_SA:CV_SA + 16],
                                    consts[:, MF:MF + 16]), waits=[("dgp", gmod)])
        vop(lambda v: v.tensor_scalar(consts[:, CV_ONEPCA:CV_ONEPCA + 16],
                                      consts[:, MF + 16:MF + 32], 1.0, None, ALU.add))
        vop(lambda v: v.tensor_copy(consts[:, CV_SB:CV_SB + 16],
                                    consts[:, MF + 48:MF + 64]))
        vop(lambda v: v.tensor_scalar(consts[:, CV_ONEPCB:CV_ONEPCB + 16],
                                      consts[:, MF + 64:MF + 80], 1.0, None, ALU.add))
        sv_e = 0
        for src, dst in ((MF + 32, CV_GA), (MF + 80, CV_GB)):
            sv_e = vop(lambda v, src=src: v.tensor_scalar(
                consts[:, CV_GT:CV_GT + 16], consts[:, src:src + 16],
                0.0, None, ALU.is_ge), waits=[("ss", ss_gam), ("sv", sv_e)])
            sv_e = vop(lambda v: v.tensor_scalar(
                consts[:, CV_GT:CV_GT + 16], consts[:, CV_GT:CV_GT + 16],
                2.0, -1.0, ALU.mult, ALU.add), waits=[("sv", sv_e)])
            sv_e = vop(lambda v, src=src: v.tensor_scalar(
                consts[:, CV_GT + 16:CV_GT + 32], consts[:, src:src + 16],
                -1.0, None, ALU.mult), waits=[("sv", sv_e)])
            sv_e = vop(lambda v, src=src: v.tensor_tensor(
                consts[:, CV_GT + 16:CV_GT + 32], consts[:, src:src + 16],
                consts[:, CV_GT + 16:CV_GT + 32], ALU.max), waits=[("sv", sv_e)])
            sv_e = vop(lambda v: v.tensor_scalar(
                consts[:, CV_GT + 16:CV_GT + 32], consts[:, CV_GT + 16:CV_GT + 32],
                0.25, None, ALU.max), waits=[("sv", sv_e)])
            sv_e = vop(lambda v, dst=dst: v.tensor_tensor(
                consts[:, dst:dst + 16], consts[:, CV_GT:CV_GT + 16],
                consts[:, CV_GT + 16:CV_GT + 32], ALU.mult), waits=[("sv", sv_e)])
        sv_mod = sv_e
        phA_st = q.now("st")

        def emit_ln(src3, n_t, dep_waits, xc_dst3=None):
            dst3 = src3 if xc_dst3 is None else xc_dst3
            b1_, ps1, w1_ = bk.acquire()
            b3_, ps3, w3_ = bk.acquire()
            tm = None
            tm3 = None
            prev_mm = 0
            for k in range(NCH):
                tm = top(lambda t, ps1=ps1, k=k: t.matmul(
                    ps1[0:1, 0:n_t], onesc[:, 0:1], src3[:, k, :],
                    start=(k == 0), stop=(k == NCH - 1)),
                    waits=(list(dep_waits) + w1_) if k == 0 else [])
                sq = reg(O_SQ, n_t) if k % 2 == 0 else reg(O_SQ + 512, n_t)
                vs = vop(lambda v, k=k, sq=sq: v.tensor_tensor(
                    sq, src3[:, k, :], src3[:, k, :], ALU.mult),
                    waits=(list(dep_waits) if k == 0 else []) +
                          ([("st", prev_mm)] if k >= 2 else []) +
                          ([sqg[k % 2]] if k < 2 else []))
                tm3 = top(lambda t, ps3=ps3, k=k, sq=sq: t.matmul(
                    ps3[0:1, 0:n_t], onesc[:, 0:1], sq,
                    start=(k == 0), stop=(k == NCH - 1)),
                    waits=[("sv", vs)] + (w3_ if k == 0 else []))
                prev_mm = tm3
            mu = rowsb[0:1, 0:n_t]
            e2 = rowsb[0:1, 512:512 + n_t]
            musq = reg(O_SQ, n_t)[0:1, :]
            vv = vop(lambda v, ps1=ps1: v.tensor_scalar(
                mu, ps1[0:1, 0:n_t], 1.0 / C, None, ALU.mult), waits=[("st", tm)])
            bk.release(b1_, "sv", vv)
            vv = vop(lambda v, ps3=ps3: v.tensor_scalar(
                e2, ps3[0:1, 0:n_t], 1.0 / C, None, ALU.mult),
                waits=[("st", tm3), ("sv", vv)])
            bk.release(b3_, "sv", vv)
            vv = vop(lambda v: v.tensor_tensor(musq, mu, mu, ALU.mult),
                     waits=[("sv", vv)])
            vv = vop(lambda v: v.tensor_tensor(e2, e2, musq, ALU.subtract),
                     waits=[("sv", vv)])
            sqg[0] = ("sv", vv)
            sqg[1] = ("st", tm3)
            sr = sop(lambda s: s.activation(e2, e2, AF.Sqrt, bias=eps5[0:1, 0:1]),
                     waits=[("sv", vv)])
            vv3 = vop(lambda v: v.reciprocal(e2, e2), waits=[("ss", sr)])
            vv3 = vop(lambda v: v.tensor_tensor(mu, mu, e2, ALU.mult),
                      waits=[("sv", vv3)])
            b4_, ps4, w4_ = bk.acquire()
            b5_, ps5, w5_ = bk.acquire()
            t4 = top(lambda t, ps4=ps4: t.matmul(ps4[:, 0:n_t], ones128[0:1, :], e2,
                                                 start=True, stop=True),
                     waits=[("sv", vv3)] + w4_)
            t5 = top(lambda t, ps5=ps5: t.matmul(ps5[:, 0:n_t], ones128[0:1, :], mu,
                                                 start=True, stop=True),
                     waits=[("sv", vv3)] + w5_)
            last = 0
            for k in range(NCH):
                last = vop(lambda v, k=k, ps4=ps4: v.tensor_tensor(
                    dst3[:, k, :], src3[:, k, :], ps4[:, 0:n_t], ALU.mult),
                    waits=[("st", t4)] if k == 0 else [])
            for k in range(NCH):
                last = vop(lambda v, k=k, ps5=ps5: v.tensor_tensor(
                    dst3[:, k, :], dst3[:, k, :], ps5[:, 0:n_t], ALU.subtract),
                    waits=[("st", t5)] if k == 0 else [])
            bk.release(b4_, "sv", last)
            bk.release(b5_, "sv", last)
            return last

        wba = [reg3(O_WBA, NCH, P), reg3(O_WBA + 2048, NCH, P)]
        wba_mm = [0, 0]

        def emit_attn_mm(wT_ap, rhs3, rhs_waits, evac, drip_n=0):
            for o in range(NCH):
                if drip_n:
                    drip(drip_n)
                sl = o % 2
                d = q.dma("sync", wba[sl][:, :, :],
                          wT_ap[o, :, :].rearrange("p (k o) -> p k o", k=NCH),
                          waits=[("st", wba_mm[sl])])
                b_, psl, w_ = bk.acquire()
                tm = None
                for k in range(NCH):
                    tm = top(lambda t, psl=psl, k=k, sl=sl: t.matmul(
                        psl, wba[sl][:, k, :], rhs3[:, k, :],
                        start=(k == 0), stop=(k == NCH - 1)),
                        waits=([("dsy", d)] + list(rhs_waits) + w_) if k == 0 else [])
                wba_mm[sl] = tm
                sem, val = evac(o, psl, tm)
                bk.release(b_, sem, val)

        wba_pair = [reg(O_WBA, 2048).bitcast(F16),
                    reg(O_WBA + 2048, 2048).bitcast(F16)]
        wba_h = [reg3h(O_WBA, NCH, P), reg3h(O_WBA + 2048, NCH, P)]
        wba_l = [reg3h(O_WBA + 1024, NCH, P), reg3h(O_WBA + 3072, NCH, P)]

        def emit_attn_mm_f16(mat_i, rhs_h, rhs_l, rhs_waits, evac, drip_n=0):
            for o in range(NCH):
                img = mat_i * 16 + o
                while img not in asp["ready"]:
                    adrip(1)
                if drip_n:
                    drip(drip_n)
                sl = o % 2
                d = q.dma("sync", wba_pair[sl], wapair_d[img, :, :],
                          waits=[("st", wba_mm[sl]), ("dgp", asp["ready"][img])])
                b_, psl, w_ = bk.acquire()
                tm = None
                for k in range(NCH):
                    w0 = ([("dsy", d)] + list(rhs_waits) + w_) if k == 0 else []
                    tm = top(lambda t, psl=psl, k=k, sl=sl: t.matmul(
                        psl, wba_h[sl][:, k, :], rhs_h[:, k, :],
                        start=(k == 0), stop=False), waits=w0)
                    tm = top(lambda t, psl=psl, k=k, sl=sl: t.matmul(
                        psl, wba_h[sl][:, k, :], rhs_l[:, k, :],
                        start=False, stop=False))
                    tm = top(lambda t, psl=psl, k=k, sl=sl: t.matmul(
                        psl, wba_l[sl][:, k, :], rhs_h[:, k, :],
                        start=False, stop=(k == NCH - 1)))
                wba_mm[sl] = tm
                sem, val = evac(o, psl, tm)
                bk.release(b_, sem, val)

        def emit_pairs(src3, dst_h, dst_l, waits):
            sv = 0
            for k in range(NCH):
                sv = vop(lambda v, k=k, src3=src3, dst_h=dst_h: v.tensor_copy(
                    dst_h[:, k, :], src3[:, k, :]),
                    waits=list(waits) if k == 0 else [])
            for k in range(NCH):
                sv = vop(lambda v, k=k, src3=src3, dst_h=dst_h, dst_l=dst_l:
                         v.tensor_tensor(dst_l[:, k, :], src3[:, k, :],
                                         dst_h[:, k, :], ALU.subtract))
            return sv

        def emit_norms(src3, ev_dep):
            prev_mm = [0, 0]
            last = 0
            for k in range(NCH):
                j = k % 2
                sq = reg(O_SQ, TT) if j == 0 else reg(O_SQ + 512, TT)
                vs = vop(lambda v, k=k, sq=sq, src3=src3: v.tensor_tensor(
                    sq, src3[:, k, :], src3[:, k, :], ALU.mult),
                    waits=([("ss", ev_dep)] if k == 0 else []) +
                          [("st", prev_mm[j]), sqg[j]])
                bn_, psn, wn_ = bk.acquire()
                tmn = top(lambda t, psn=psn, sq=sq: t.matmul(
                    psn, ones128[:, :], sq, start=True, stop=True),
                    waits=[("sv", vs)] + wn_)
                prev_mm[j] = tmn
                sq_s = sop(lambda s, psn=psn, sq=sq: s.activation(
                    sq, psn, AF.Sqrt), waits=[("st", tmn)])
                bk.release(bn_, "ss", sq_s)
                vn1 = vop(lambda v, sq=sq: v.tensor_scalar(
                    sq, sq, 1e-12, None, ALU.max), waits=[("ss", sq_s)])
                vn2 = vop(lambda v, sq=sq: v.reciprocal(sq, sq), waits=[("sv", vn1)])
                last = vop(lambda v, k=k, sq=sq, src3=src3: v.tensor_tensor(
                    src3[:, k, :], src3[:, k, :], sq, ALU.mult),
                    waits=[("sv", vn2)])
                sqg[j] = ("sv", last)
            return last

        XB = [reg3(O_X0, NCH, 3 + TT), reg3(O_X1, NCH, 3 + TT)]
        RB = reg3(O_RB, NCH, TT)
        RC = reg3(O_RC, NCH, TT)
        RD = reg3(O_RD, NCH, TT)
        ph_h = reg3h(O_RD, NCH, TT)
        ph_l = reg3h(O_RD + 4096, NCH, TT)
        powb = reg(O_SQ, TT)
        gstg = reg(O_POW, TT)
        tail3 = consts[:, CV_TAIL:CV_TAIL + 48].rearrange("p (k t) -> p k t", k=NCH)

        sin_cell = [0]

        def emit_ag2(scan_done):
            g2 = q.dma("gpsimd", ag2_in[:, :], consts[:, CV_CARRY:CV_CARRY + 16],
                       waits=[("sv", scan_done)])
            cc2 = q.op("gpsimd", lambda g: g.collective_compute(
                "AllGather", ALU.bypass,
                replica_groups=[[0, 1, 2, 3], [4, 5, 6, 7]],
                ins=[ag2_in.opt()], outs=[ag2_out.opt()]),
                waits=[("dgp", g2)], inc="scc")
            gc = 0
            for j in range(4):
                gc = q.dma("gpsimd", consts[:, CV_CJ + 16 * j:CV_CJ + 16 * (j + 1)],
                           ag2_out[j * P:(j + 1) * P, :],
                           waits=[("scc", cc2)] if j == 0 else [])
            sv_a = vop(lambda v: v.memset(consts[:, CV_SIN:CV_SIN + 16], 0.0),
                       waits=[("dgp", gc)])
            sv_a = vop(lambda v: v.tensor_copy(consts[:, CV_GT:CV_GT + 16],
                                               consts[:, CV_CJ:CV_CJ + 16]),
                       waits=[("sv", sv_a)])
            for j in range(4):
                sv_a = vop(lambda v, j=j: v.scalar_tensor_tensor(
                    consts[:, CV_SIN:CV_SIN + 16], consts[:, CV_GT:CV_GT + 16],
                    ccol(CV_MSEL + j), consts[:, CV_SIN:CV_SIN + 16],
                    ALU.mult, ALU.add), waits=[("sv", sv_a)])
                if j < 3:
                    sv_a = vop(lambda v: v.tensor_tensor(
                        consts[:, CV_GT:CV_GT + 16], consts[:, CV_GT:CV_GT + 16],
                        consts[:, CV_G2048:CV_G2048 + 16], ALU.mult),
                        waits=[("sv", sv_a)])
                    sv_a = vop(lambda v, j=j: v.tensor_tensor(
                        consts[:, CV_GT:CV_GT + 16], consts[:, CV_GT:CV_GT + 16],
                        consts[:, CV_CJ + 16 * (j + 1):CV_CJ + 16 * (j + 2)],
                        ALU.add), waits=[("sv", sv_a)])
            sin_cell[0] = sv_a

        x_pref = [0, 0]
        x_pref[0] = q.dma("gpsimd", XB[0][:, :, :],
                          xfm[:, 0:3 + TT].rearrange("(k p) t -> p k t", p=P))
        qmm_st = [0, 0]
        az_store = 0
        prev_gq = 0

        for tau in range(NT):
            buf = XB[tau % 2]
            bufm = buf[:, :, 3:3 + TT]
            if tau + 1 < NT:
                x_pref[(tau + 1) % 2] = q.dma(
                    "gpsimd", XB[(tau + 1) % 2][:, :, 3:3 + TT],
                    xfm[:, 3 + (tau + 1) * TT:3 + (tau + 2) * TT].rearrange(
                        "(k p) t -> p k t", p=P),
                    waits=[("st", qmm_st[(tau + 1) % 2]), ("scc", cc1)])
            if tau == 0:
                hsv = emit_ln(buf[:, :, 0:3], 3, [("dgp", x_pref[0]), ("sv", sv_mod)])
                for k in range(NCH):
                    hsv = vop(lambda v, k=k, buf=buf: v.tensor_scalar(
                        buf[:, k, 0:3], buf[:, k, 0:3],
                        ccol(CV_ONEPCA + k), ccol(CV_SA + k), ALU.mult, ALU.add),
                        waits=[("sv", hsv)] if k == 0 else [])
                hm = 0
                for k in range(NCH):
                    hm = vop(lambda v, k=k, buf=buf: v.tensor_scalar(
                        buf[:, k, 0:3], buf[:, k, 0:3], ccol(CV_HM), None, ALU.mult),
                        waits=[("sv", hsv)] if k == 0 else [])
                dep = [("sv", hm)]
            else:
                tl = vop(lambda v, buf=buf: v.tensor_copy(buf[:, :, 0:3], tail3),
                         waits=[("dgp", x_pref[tau % 2])])
                dep = [("sv", tl)]
            lsv = emit_ln(bufm, TT, dep + [("dgp", x_pref[tau % 2]), ("sv", sv_mod)])
            h1_sv = 0
            for k in range(NCH):
                h1_sv = vop(lambda v, k=k, bufm=bufm: v.tensor_scalar(
                    bufm[:, k, :], bufm[:, k, :],
                    ccol(CV_ONEPCA + k), ccol(CV_SA + k), ALU.mult, ALU.add),
                    waits=[("sv", lsv)] if k == 0 else [])
            tail_sv = vop(lambda v, buf=buf: v.tensor_copy(
                tail3, buf[:, :, TT:TT + 3]), waits=[("sv", h1_sv)])
            if tau == 0:
                adrip(64)
                asp_done = max(asp["store"])
                pair_dep = [("sv", h1_sv), ("dgp", asp_done)]
            else:
                pair_dep = [("sv", h1_sv), ("sv", prev_gq)]
            pair_sv = emit_pairs(bufm, ph_h, ph_l, pair_dep)

            def evac_to(dst3, extra):
                def f(o, psl, tm):
                    c = sop(lambda s, o=o, psl=psl, dst3=dst3: s.copy(
                        dst3[:, o, :], psl),
                        waits=[("st", tm)] + (extra if o == 0 else []))
                    return ("ss", c)
                return f

            emit_attn_mm_f16(0, ph_h, ph_l, [("sv", pair_sv)],
                             evac_to(RC, [("dgp", az_store), ("st", phA_st)]),
                             drip_n=1)
            k_ev = q.now("ss")

            csv = 0
            for k in range(NCH):
                csv = vop(lambda v, k=k, buf=buf: v.tensor_scalar(
                    RB[:, k, :], buf[:, k, 0:TT], ccol(CV_CONVW + 4 * k),
                    None, ALU.mult),
                    waits=[("sv", h1_sv), ("dgp", az_store)] if k == 0 else [])
                for kk in range(1, 4):
                    csv = vop(lambda v, k=k, kk=kk, buf=buf: v.scalar_tensor_tensor(
                        RB[:, k, :], buf[:, k, kk:kk + TT],
                        ccol(CV_CONVW + 4 * k + kk), RB[:, k, :], ALU.mult, ALU.add))
            ssv = 0
            for k in range(NCH):
                ssv = sop(lambda s, k=k: s.activation(
                    RB[:, k, :], RB[:, k, :], AF.Silu, bias=ccol(CV_CONVB + k)),
                    waits=[("sv", csv)] if k == 0 else [])
            kn_sv = emit_norms(RC, k_ev)

            def evac_u(o, psl, tm):
                c = vop(lambda v, o=o, psl=psl: v.tensor_tensor(
                    RC[:, o, :], RC[:, o, :], psl, ALU.mult),
                    waits=[("st", tm)] + ([("sv", kn_sv)] if o == 0 else []))
                return ("sv", c)

            emit_attn_mm_f16(1, ph_h, ph_l, [("sv", pair_sv)], evac_u, drip_n=1)
            u_sv = q.now("sv")

            gate_st = [0]

            def evac_gate(o, psl, tm):
                c = sop(lambda s, o=o, psl=psl: s.activation(
                    gstg, psl, AF.Sigmoid, bias=ccol(CV_GATEB + o)),
                    waits=[("st", tm), ("dgp", gate_st[0])])
                gate_st[0] = q.dma("gpsimd", gate_d[o * P:(o + 1) * P, :], gstg,
                                   waits=[("ss", c)])
                return ("ss", c)

            emit_attn_mm(wgT, RB, [("ss", ssv)], evac_gate, drip_n=1)
            gate_dma = gate_st[0]
            gate_mm_st = q.now("st")

            scan_k = []
            scan_sv = 0
            for k in range(NCH):
                scan_sv = vop(lambda v, k=k: v.tensor_tensor_scan(
                    RB[:, k, :], ccol(CV_GAM + k).broadcast_to([P, TT]),
                    RC[:, k, :], ccol(CV_CARRY + k), ALU.mult, ALU.add),
                    waits=([("sv", u_sv), ("st", gate_mm_st)] if k == 0 else []))
                scan_k.append(scan_sv)
                scan_sv = vop(lambda v, k=k: v.tensor_copy(
                    ccol(CV_CARRY + k), RB[:, k, TT - 1:TT]),
                    waits=[("sv", scan_sv)])
            if tau == NT - 1:
                emit_ag2(scan_sv)

            def evac_q(o, psl, tm):
                c = sop(lambda s, o=o, psl=psl: s.copy(RC[:, o, :], psl),
                        waits=[("st", tm), ("sv", scan_k[o])])
                return ("ss", c)

            emit_attn_mm_f16(2, ph_h, ph_l, [("sv", pair_sv)], evac_q, drip_n=1)
            q_ev = q.now("ss")
            qmm_st[tau % 2] = q.now("st")
            qn_sv = emit_norms(RC, q_ev)

            gq_sv = 0
            grel = [0, 0]
            for k in range(NCH):
                j = k % 2
                gb_buf = reg(O_SQ, TT) if j == 0 else reg(O_SQ + 512, TT)
                grel[j] = q.dma("gpsimd", gb_buf,
                                gate_d[k * P:(k + 1) * P, :],
                                waits=[("dgp", gate_dma), sqg[j]])
                gq_sv = vop(lambda v, k=k, gb_buf=gb_buf: v.tensor_tensor(
                    RC[:, k, :], RC[:, k, :], gb_buf, ALU.mult),
                    waits=[("dgp", grel[j])] +
                          ([("sv", qn_sv)] if k == 0 else []))
                sqg[j] = ("sv", gq_sv)
            prev_gq = gq_sv
            q.dma("gpsimd", S_d[:, tau * TT:(tau + 1) * TT].rearrange(
                "(k p) t -> p k t", p=P), RB[:, :, :], waits=[("sv", scan_sv)])
            az_store = q.dma("gpsimd", G_d[:, tau * TT:(tau + 1) * TT].rearrange(
                "(k p) t -> p k t", p=P), RC[:, :, :], waits=[("sv", gq_sv)])

        drip(len(split_units))
        assert spl["i"] == len(split_units)
        spl_done = q.now("dgp")
        phB_sv = q.now("sv")
        phB_ss = q.now("ss")
        phB_st = q.now("st")

        XPs = [XB[0][:, :, 0:TT], XB[1][:, :, 0:TT]]
        RBY = RB
        D16 = reg3h(O_RD, 32, TT)
        H2 = reg3h(O_H2, NCH, TT)
        wfflat = [arena[:, O_RC + 2048 * i:O_RC + 2048 * (i + 1)].bitcast(F16)
                  for i in range(4)]
        wf = [wfflat[i].rearrange("p (o k c) -> p o k c", o=2, k=NCH)
              for i in range(4)]
        wf_mm = [0, 0, 0, 0]
        e16 = [reg(O_POW, 256).bitcast(F16), reg(O_POW + 256, 256).bitcast(F16)]

        out_store = [0, 0]
        final_sv = phB_sv
        last_w0_mm = phB_st
        last_w1_mm = phB_st
        corr_sv = [0, 0]
        dZ = 0

        def emit_corr(tau, extra_waits):
            XP = XPs[tau % 2]
            dA = q.dma("gpsimd", XP[:, :, :],
                       S_d[:, tau * TT:(tau + 1) * TT].rearrange(
                           "(k p) t -> p k t", p=P),
                       waits=[("dgp", az_store), ("dgp", out_store[tau % 2])] +
                             list(extra_waits))
            wv = vop(lambda v, tau=tau: v.tensor_tensor(
                consts[:, CV_WTAU:CV_WTAU + 16], consts[:, CV_SIN:CV_SIN + 16],
                consts[:, CV_G512P + 16 * tau:CV_G512P + 16 * (tau + 1)], ALU.mult),
                waits=[("sv", sin_cell[0])])
            sv = 0
            pe = 0
            for k in range(NCH):
                pe = sop(lambda s, k=k: s.activation(
                    powb, riot[:, :], AF.Exp, scale=ccol(CV_LNGAM + k)),
                    waits=([("ss", ss_gam), ("sv", wv), ("dgp", dA), sqg[0]]
                           if k == 0 else [("sv", sv)]))
                sv = vop(lambda v, k=k, XP=XP: v.scalar_tensor_tensor(
                    XP[:, k, :], powb, ccol(CV_WTAU + k), XP[:, k, :],
                    ALU.mult, ALU.add),
                    waits=[("ss", pe)] + ([("sv", wv), ("dgp", dA)] if k == 0 else []))
            sqg[0] = ("sv", sv)
            corr_sv[tau % 2] = sv
            return sv

        for tau in range(NT):
            XP = XPs[tau % 2]
            if tau == 0:
                emit_corr(0, [("sv", final_sv), ("ss", phB_ss), ("st", phB_st)])
                dZ = q.dma("gpsimd", RBY[:, :, :],
                           G_d[:, 0:TT].rearrange("(k p) t -> p k t", p=P),
                           waits=[("dgp", az_store), ("sv", final_sv)])

            o_sv = 0
            for k in range(NCH):
                o_sv = vop(lambda v, k=k, XP=XP: v.tensor_tensor(
                    XP[:, k, :], XP[:, k, :], RBY[:, k, :], ALU.mult),
                    waits=[("dgp", dZ), ("sv", corr_sv[tau % 2])]
                    if k == 0 else [])

            pair_d = emit_pairs(XP, ph_h, ph_l,
                                [("sv", o_sv), ("st", last_w1_mm),
                                 ("st", qmm_st[0]), ("st", qmm_st[1])])

            def evac_y(o, psl, tm):
                c = sop(lambda s, o=o, psl=psl: s.copy(RBY[:, o, :], psl),
                        waits=[("st", tm)])
                return ("ss", c)

            emit_attn_mm_f16(3, ph_h, ph_l, [("sv", pair_d)], evac_y)
            wo_st = q.now("st")
            y_ev = q.now("ss")
            l2 = emit_ln(RBY, TT, [("ss", y_ev)])
            dx = q.dma("gpsimd", XP[:, :, :],
                       xfm[:, 3 + tau * TT:3 + (tau + 1) * TT].rearrange(
                           "(k p) t -> p k t", p=P),
                       waits=[("st", wo_st)])
            for k in range(NCH):
                l2 = vop(lambda v, k=k: v.tensor_scalar(
                    RBY[:, k, :], RBY[:, k, :], ccol(CV_LNW + k), ccol(CV_LNB + k),
                    ALU.mult, ALU.add), waits=[("sv", l2)] if k == 0 else [])
            xp_sv = 0
            for k in range(NCH):
                xp_sv = vop(lambda v, k=k, XP=XP: v.scalar_tensor_tensor(
                    XP[:, k, :], RBY[:, k, :], ccol(CV_GA + k), XP[:, k, :],
                    ALU.mult, ALU.add),
                    waits=[("sv", l2), ("dgp", dx)] if k == 0 else [])
            l3 = emit_ln(XP, TT, [("sv", xp_sv)], xc_dst3=RBY)
            h2_sv = 0
            h2_k = []
            for k in range(NCH):
                h2_sv = vop(lambda v, k=k: v.tensor_scalar(
                    H2[:, k, :], RBY[:, k, :], ccol(CV_ONEPCB + k), ccol(CV_SB + k),
                    ALU.mult, ALU.add),
                    waits=([("sv", l3)] + ([("st", last_w0_mm)] if tau > 0 else []) +
                           ([("dgp", spl_done)] if tau == 0 else []))
                    if k == 0 else [])
                h2_k.append(h2_sv)

            ge_k = [0] * 32
            for half in range(2):
                w0dep = [("dgp", spl_done)] + \
                    ([("st", last_w1_mm)] if (half == 1 or tau > 0) else [])
                for og in range(16):
                    o0 = half * 32 + og * 2
                    slh, sll = (og % 2) * 2, (og % 2) * 2 + 1
                    for oo in range(2):
                        q.dma("sync", wfflat[slh][:, oo * 2048:(oo + 1) * 2048],
                              w0hi_d[o0 + oo, :, :],
                              waits=[("st", wf_mm[slh]), ("dgp", spl_done),
                                     ("dgp", az_store)])
                        dwl = q.dma(
                            "sync", wfflat[sll][:, oo * 2048:(oo + 1) * 2048],
                            w0lo_d[o0 + oo, :, :],
                            waits=[("st", wf_mm[sll]), ("dgp", spl_done),
                                   ("dgp", az_store)])
                    for oo in range(2):
                        b_, psl, w_ = bk.acquire()
                        tm = None
                        for k in range(NCH):
                            tm = top(lambda t, psl=psl, k=k, slh=slh, oo=oo: t.matmul(
                                psl, wf[slh][:, oo, k, :],
                                H2[:, k, :], start=(k == 0), stop=False),
                                waits=([("dsy", dwl), ("sv", h2_k[k])] + w0dep + w_)
                                if (k == 0 and oo == 0) else
                                ([("dsy", dwl), ("sv", h2_k[k])] if k == 0 else
                                 [("sv", h2_k[k])] if oo == 0 else []))
                            tm = top(lambda t, psl=psl, k=k, sll=sll, oo=oo: t.matmul(
                                psl, wf[sll][:, oo, k, :],
                                H2[:, k, :], start=False, stop=(k == NCH - 1)))
                        wf_mm[slh] = tm
                        wf_mm[sll] = tm
                        last_w0_mm = tm
                        ge = sop(lambda s, psl=psl, og=og, oo=oo, o0=o0: s.activation(
                            D16[:, og * 2 + oo, :], psl, AF.Gelu,
                            bias=ccol(CV_B0 + o0 + oo)), waits=[("st", tm)])
                        ge_k[og * 2 + oo] = ge
                        bk.release(b_, "ss", ge)
                d16_ss = q.now("ss")
                if half == 1 and tau + 1 < NT:
                    emit_corr(tau + 1, [])
                for op_ in range(8):
                    banks = []
                    for oo in range(2):
                        banks.append(bk.acquire())
                    dmas = []
                    for kb in range(2):
                        kbg = half * 2 + kb
                        slh, sll = (kb % 2) * 2, (kb % 2) * 2 + 1
                        for oo in range(2):
                            q.dma("sync", wfflat[slh][:, oo * 2048:(oo + 1) * 2048],
                                  w1hi_d[kbg * 16 + op_ * 2 + oo, :, :],
                                  waits=[("st", wf_mm[slh]), ("dgp", spl_done)])
                            dl = q.dma(
                                "sync", wfflat[sll][:, oo * 2048:(oo + 1) * 2048],
                                w1lo_d[kbg * 16 + op_ * 2 + oo, :, :],
                                waits=[("st", wf_mm[sll]), ("dgp", spl_done)])
                        dmas.append((dl, slh, sll))
                    kb_last = [0, 0]
                    for kb in range(2):
                        dl, slh, sll = dmas[kb]
                        for k in range(NCH):
                            for oo in range(2):
                                bix, psl, wb_ = banks[oo]
                                w01 = [("dsy", dl)] if k == 0 else []
                                w01 = w01 + ([("ss", ge_k[kb * 16 + k])]
                                             if oo == 0 else [])
                                if k == 0 and kb == 0:
                                    w01 = w01 + wb_
                                tm = top(lambda t, psl=psl, k=k, kb=kb, oo=oo,
                                         slh=slh: t.matmul(
                                    psl, wf[slh][:, oo, k, :],
                                    D16[:, kb * 16 + k, :],
                                    start=(kb == 0 and k == 0), stop=False),
                                    waits=w01)
                                tm = top(lambda t, psl=psl, k=k, kb=kb, oo=oo,
                                         sll=sll: t.matmul(
                                    psl, wf[sll][:, oo, k, :],
                                    D16[:, kb * 16 + k, :],
                                    start=False, stop=(kb == 1 and k == NCH - 1)))
                                kb_last[kb] = tm
                                if kb == 1 and k == NCH - 1:
                                    last_w1_mm = tm
                        wf_mm[slh] = kb_last[kb]
                        wf_mm[sll] = kb_last[kb]
                    for oo in range(2):
                        bix, psl, wb_ = banks[oo]
                        e = op_ * 2 + oo
                        if half == 0:
                            ev = sop(lambda s, psl=psl, e=e: s.activation(
                                RBY[:, e, :], psl, AF.Identity, bias=ccol(CV_B1 + e)),
                                waits=[("st", last_w1_mm)])
                            bk.release(bix, "ss", ev)
                        else:
                            ev = vop(lambda v, psl=psl, e=e: v.tensor_tensor(
                                e16[e % 2], RBY[:, e, :], psl, ALU.add),
                                waits=[("st", last_w1_mm)] +
                                      ([("sv", final_sv)] if e >= 2 else []))
                            bk.release(bix, "sv", ev)
                            final_sv = vop(lambda v, e=e, XP=XP: v.scalar_tensor_tensor(
                                XP[:, e, :], e16[e % 2], ccol(CV_GB + e),
                                XP[:, e, :], ALU.mult, ALU.add),
                                waits=[("sv", ev)])
            out_store[tau % 2] = q.dma(
                "gpsimd", out_fm[:, tau * TT:(tau + 1) * TT].rearrange(
                    "(k p) t -> p k t", p=P), XP[:, :, :], waits=[("sv", final_sv)])
            if tau + 1 < NT:
                dZ = q.dma("gpsimd", RBY[:, :, :],
                           G_d[:, (tau + 1) * TT:(tau + 2) * TT].rearrange(
                               "(k p) t -> p k t", p=P),
                           waits=[("sv", final_sv)])

        q.op("sync", lambda e: None, waits=[("dgp", q.now("dgp")),
                                            ("dsy", q.now("dsy"))])

        block = stack.enter_context(nc.Block())
        q.finalize(block)
    return nc
def kernel(**inputs):
    x = np.ascontiguousarray(inputs["L0_attn_x"], dtype=np.float32)
    def attn_img(w):
        return np.ascontiguousarray(
            np.asarray(w, np.float32).T.reshape(16, 128, 16, 128)
            .transpose(2, 1, 0, 3).reshape(16, 128, 2048))

    WqT = attn_img(inputs["L0_attn_Wq"])
    WkT = attn_img(inputs["L0_attn_Wk"])
    WvT = attn_img(inputs["L0_attn_Wv"])
    WgT = attn_img(inputs["L0_attn_gate_w"])
    WoT = attn_img(inputs["L0_attn_Wo"])
    W0T = np.ascontiguousarray(
        np.asarray(inputs["L0_ffn_W0"], np.float32).T.reshape(16, 128, 64, 128)
        .transpose(2, 1, 0, 3).reshape(64, 128, 2048))
    W1T = np.ascontiguousarray(
        np.asarray(inputs["L0_ffn_W1"], np.float32).T.reshape(4, 16, 128, 16, 128)
        .transpose(0, 3, 2, 1, 4).reshape(64, 128, 2048))
    wmod = np.asarray(inputs["L0_wmod"], dtype=np.float32)
    cin = np.asarray(inputs["L0_c"], dtype=np.float32)
    conv_w = np.ascontiguousarray(inputs["L0_attn_conv_w"][:, 0, :], dtype=np.float32)
    convb = np.ascontiguousarray(inputs["L0_attn_conv_b"], dtype=np.float32)
    gateb = np.ascontiguousarray(inputs["L0_attn_gate_b"], dtype=np.float32)
    lnw = np.ascontiguousarray(inputs["L0_attn_ln_w"], dtype=np.float32)
    lnb = np.ascontiguousarray(inputs["L0_attn_ln_b"], dtype=np.float32)
    gam = np.ascontiguousarray(inputs["L0_attn_gamma"].reshape(H), dtype=np.float32)
    b0 = np.ascontiguousarray(inputs["L0_ffn_bias0"], dtype=np.float32)
    b1 = np.ascontiguousarray(inputs["L0_ffn_bias1"], dtype=np.float32)

    iota = np.arange(1, TT + 1, dtype=np.float32).reshape(1, TT)

    nc = build_nc()
    in_maps = []
    for r in range(8):
        b, pos = r // 4, r % 4
        t0 = TC * pos
        if t0 == 0:
            halo = np.zeros((C, 3), np.float32)
        else:
            halo = np.ascontiguousarray(x[b, t0 - 3:t0].T)
        xfm = np.ascontiguousarray(
            np.concatenate([halo, x[b, t0:t0 + TC].T], axis=1))
        ms = np.zeros(4, np.float32)
        if pos > 0:
            ms[pos - 1] = 1.0
        in_maps.append({
            "xfm": xfm, "wqT": WqT, "wkT": WkT, "wvT": WvT, "wgT": WgT,
            "woT": WoT, "w0T": W0T, "w1T": W1T,
            "wmods": np.ascontiguousarray(wmod[:, pos * MODW:(pos + 1) * MODW]),
            "cbv": np.ascontiguousarray(cin[b]),
            "convw": conv_w, "convb": convb, "gateb": gateb,
            "lnw": lnw, "lnb": lnb, "gam16": gam, "b0v": b0, "b1v": b1,
            "msel": ms, "hmask": np.array([0.0 if pos == 0 else 1.0], np.float32),
            "iotar": iota,
        })
    res = run_bass_kernel_spmd(nc, in_maps, core_ids=list(range(8)))
    out = np.empty((2, 8192, C), np.float32)
    for r in range(8):
        b, pos = r // 4, r % 4
        out[b, TC * pos:TC * (pos + 1)] = res.results[r]["out_fm"].T
    return out


# version 28
# speedup vs baseline: 4.4814x; 2.0192x over previous
import numpy as np
import ml_dtypes
from contextlib import ExitStack

import concourse.bass as bass
import concourse.mybir as mybir
from concourse.bass_utils import run_bass_kernel_spmd

F32 = mybir.dt.float32
F16 = mybir.dt.float16
AF = mybir.ActivationFunctionType
ALU = mybir.AluOpType

C = 2048
H = 16
NCH = 16
TC = 2048
TT = 512
NT = TC // TT
NFF = 8192
P = 128
LN_EPS = 1e-5
MODW = 12288 // 4


SERIAL = False
SERIAL_DMA = []
SER_GROUPS = set()


class Q:
    ENGINES = ("sync", "gpsimd", "vector", "scalar", "tensor")

    def __init__(self, nc, stack):
        self.nc = nc
        self.q = {e: [] for e in self.ENGINES}
        self.sems = {}
        self.cnt = {}
        for name in ("st", "sv", "ss", "dsy", "dgp", "scc"):
            self.sems[name] = stack.enter_context(nc.semaphore(name))
            self.cnt[name] = 0

    ser_mark = 0

    def op(self, eng, fn, waits=(), inc=None, amt=1):
        if Q.ser_mark:
            waits = list(waits) + [("dgp", Q.ser_mark)]
        if SERIAL:
            waits = list(waits) + [(s, self.cnt[s]) for s in
                                   ("st", "sv", "ss", "dsy", "dgp")]
        elif SERIAL_DMA:
            ser = [x for x in SERIAL_DMA if True]
            if inc in ser:
                waits = list(waits) + [(s, self.cnt[s]) for s in
                                       ("st", "sv", "ss", "dsy", "dgp")]
            else:
                waits = list(waits) + [(s2, self.cnt[s2]) for s2 in ser]
        waits = [(s, v) for (s, v) in waits if v and v > 0]
        for s, v in waits:
            assert v <= self.cnt[s], f"wait {s}>={v} > emitted {self.cnt[s]} ({eng})"
        if inc is not None:
            self.cnt[inc] += amt
        sems = self.sems

        def body(e, fn=fn, waits=tuple(waits), inc=inc, amt=amt):
            for s, v in waits:
                e.wait_ge(sems[s], v)
            r = fn(e)
            if inc is not None:
                ins = r[-1] if isinstance(r, (list, tuple)) else r
                ins.then_inc(sems[inc], amt)
        self.q[eng].append(body)
        return self.cnt[inc] if inc is not None else None

    def dma(self, eng, out, in_, waits=(), slow=False, group=None):
        sem = "dsy" if eng == "sync" else "dgp"
        if group in SER_GROUPS:
            waits = list(waits) + [(s2, self.cnt[s2]) for s2 in
                                   ("st", "sv", "ss", "dsy", "dgp")]
        r = self.op(eng, lambda e, o=out, i=in_, s=slow: e.dma_start(
            out=o, in_=i, allow_slow_non_contiguous=s), waits, sem, 16)
        if group in SER_GROUPS:
            Q.ser_mark = r
        return r

    def now(self, s):
        return self.cnt[s]

    def finalize(self, block):
        for eng in self.ENGINES:
            fns = self.q[eng]
            if fns:
                getattr(block, eng)(lambda e, fns=fns: [f(e) for f in fns])


class Banks:

    def __init__(self, pA, pB):
        self.sl = [pA[:, i * 512:(i + 1) * 512] for i in range(4)] + \
                  [pB[:, i * 512:(i + 1) * 512] for i in range(4)]
        self.state = [("st", 0)] * 8
        self.i = 0

    def acquire(self):
        b = self.i % 8
        self.i += 1
        return b, self.sl[b], [self.state[b]]

    def release(self, b, sem, val):
        self.state[b] = (sem, val)


def build_nc():
    nc = bass.Bass("TRN2", target_bir_lowering=False, debug=False)

    xfm = nc.dram_tensor("xfm", [C, 3 + TC], F32, kind="ExternalInput").ap()
    wqT = nc.dram_tensor("wqT", [NCH, P, C], F32, kind="ExternalInput").ap()
    wkT = nc.dram_tensor("wkT", [NCH, P, C], F32, kind="ExternalInput").ap()
    wvT = nc.dram_tensor("wvT", [NCH, P, C], F32, kind="ExternalInput").ap()
    wgT = nc.dram_tensor("wgT", [NCH, P, C], F32, kind="ExternalInput").ap()
    woT = nc.dram_tensor("woT", [NCH, P, C], F32, kind="ExternalInput").ap()
    w0T = nc.dram_tensor("w0T", [64, P, C], F32, kind="ExternalInput").ap()
    w1T = nc.dram_tensor("w1T", [64, P, C], F32, kind="ExternalInput").ap()
    wmods = nc.dram_tensor("wmods", [C, MODW], F32, kind="ExternalInput").ap()
    cbv = nc.dram_tensor("cbv", [C], F32, kind="ExternalInput").ap()
    convw = nc.dram_tensor("convw", [C, 4], F32, kind="ExternalInput").ap()
    convb = nc.dram_tensor("convb", [C], F32, kind="ExternalInput").ap()
    gateb = nc.dram_tensor("gateb", [C], F32, kind="ExternalInput").ap()
    lnw = nc.dram_tensor("lnw", [C], F32, kind="ExternalInput").ap()
    lnb = nc.dram_tensor("lnb", [C], F32, kind="ExternalInput").ap()
    gam16 = nc.dram_tensor("gam16", [H], F32, kind="ExternalInput").ap()
    b0v = nc.dram_tensor("b0v", [NFF], F32, kind="ExternalInput").ap()
    b1v = nc.dram_tensor("b1v", [C], F32, kind="ExternalInput").ap()
    msel = nc.dram_tensor("msel", [4], F32, kind="ExternalInput").ap()
    hmask = nc.dram_tensor("hmask", [1], F32, kind="ExternalInput").ap()
    iotar = nc.dram_tensor("iotar", [1, TT], F32, kind="ExternalInput").ap()
    out_fm = nc.dram_tensor("out_fm", [C, TC], F32, kind="ExternalOutput").ap()

    wapair_d = nc.dram_tensor("wapair_d", [64, P, 2 * C], F16).ap()
    w0hi_d = nc.dram_tensor("w0hi_d", [64, P, C], F16).ap()
    w0lo_d = nc.dram_tensor("w0lo_d", [64, P, C], F16).ap()
    w1hi_d = nc.dram_tensor("w1hi_d", [64, P, C], F16).ap()
    w1lo_d = nc.dram_tensor("w1lo_d", [64, P, C], F16).ap()
    gate_d = nc.dram_tensor("gate_d", [C, TT], F32).ap()
    S_d = nc.dram_tensor("S_d", [C, TC], F32).ap()
    G_d = nc.dram_tensor("G_d", [C, TC], F32).ap()
    ag1_in = nc.dram_tensor("ag1_in", [1, MODW], F32).ap()
    ag1_out = nc.dram_tensor("ag1_out", [4, MODW], F32).ap()
    ag2_in = nc.dram_tensor("ag2_in", [P, H], F32).ap()
    ag2_out = nc.dram_tensor("ag2_out", [4 * P, H], F32).ap()

    stack = ExitStack()
    with stack:
        sb = lambda n_, s, d=F32: stack.enter_context(nc.sbuf_tensor(n_, s, d))

        ASZ = 50784
        arena = sb("arena", [P, ASZ])

        def reg(off, sz):
            return arena[:, off:off + sz]

        def reg3(off, k, t):
            return arena[:, off:off + k * t].rearrange("p (k t) -> p k t", k=k)

        def reg3h(off, k, t):
            return arena[:, off:off + (k * t) // 2].bitcast(F16).rearrange(
                "p (k t) -> p k t", k=k)

        O_X0 = 0
        O_X1 = 8240
        O_RB = 16480
        O_RC = 24672
        O_RD = 32864
        O_WBA = 41056
        O_SQ = 45152
        O_POW = 46176
        O_SPL = 46688
        O_SPLO = 48736
        O_H2 = O_SPL
        assert O_SPLO + 2048 <= ASZ

        consts = sb("consts", [P, 736])
        CV_ONEPCA, CV_SA, CV_GA = 0, 16, 32
        CV_ONEPCB, CV_SB, CV_GB = 48, 64, 80
        CV_LNW, CV_LNB, CV_CONVB, CV_GATEB, CV_B1 = 96, 112, 128, 144, 160
        CV_GAM, CV_LNGAM, CV_G2048 = 176, 192, 208
        CV_SIN, CV_WTAU = 224, 240
        CV_CARRY = 256
        CV_TAIL = 272
        CV_CJ = 320
        CV_MSEL = 384
        CV_HM = 388
        CV_CONVW = 392
        CV_B0 = 456
        CV_SILC = 520
        CV_MODF = 536
        CV_GT = 632
        CV_G512P = 664

        rowsb = sb("rowsb", [H, 1024])
        ones128 = sb("ones128", [P, P])
        onesc = sb("onesc", [P, 1])
        riot = sb("riot", [P, TT])
        eps5 = sb("eps5", [1, 4])
        modc = arena[0:1, O_RB:O_RB + MODW]

        pA = stack.enter_context(nc.psum_tensor("pA", [P, 2048], F32))
        pB = stack.enter_context(nc.psum_tensor("pB", [P, 2048], F32))

        q = Q(nc, stack)
        bk = Banks(pA, pB)
        sqg = [("sv", 0), ("sv", 0)]

        def vop(fn, waits=()):
            return q.op("vector", fn, waits, "sv")

        def sop(fn, waits=()):
            return q.op("scalar", fn, waits, "ss")

        def top(fn, waits=()):
            return q.op("tensor", fn, waits, "st")

        def ccol(i):
            return consts[:, i:i + 1]

        q.dma("sync", rowsb[0:1, 0:TT], iotar[:, :])
        q.dma("sync", consts[:, CV_CONVW:CV_CONVW + 64].rearrange(
            "p (k t) -> p k t", k=NCH),
              convw.rearrange("(k p) t -> p k t", p=P))
        for ap_, cv, n in ((convb, CV_CONVB, 16), (gateb, CV_GATEB, 16),
                           (lnw, CV_LNW, 16), (lnb, CV_LNB, 16),
                           (b1v, CV_B1, 16), (b0v, CV_B0, 64),
                           (cbv, CV_SILC, 16)):
            q.dma("sync", consts[:, cv:cv + n], ap_.rearrange("(k p) -> p k", p=P),
                  slow=True)
        for h in range(H):
            q.dma("sync", consts[:, CV_GAM + h:CV_GAM + h + 1],
                  gam16[h:h + 1].unsqueeze(0).broadcast_to([P, 1]))
        for j in range(4):
            q.dma("sync", consts[:, CV_MSEL + j:CV_MSEL + j + 1],
                  msel[j:j + 1].unsqueeze(0).broadcast_to([P, 1]))
        q.dma("sync", consts[:, CV_HM:CV_HM + 1],
              hmask[0:1].unsqueeze(0).broadcast_to([P, 1]))
        ldc = q.now("dsy")

        vop(lambda v: v.memset(ones128[:, :], 1.0))
        vop(lambda v: v.memset(onesc[:, :], 1.0))
        vop(lambda v: v.memset(eps5[:, 0:1], LN_EPS))
        sv_ones = vop(lambda v: v.memset(consts[:, CV_CARRY:CV_CARRY + 16], 0.0))

        s1 = sop(lambda s: s.activation(consts[:, CV_SILC:CV_SILC + 16],
                                        consts[:, CV_SILC:CV_SILC + 16], AF.Silu),
                 waits=[("dsy", ldc)])
        ss_lng = sop(lambda s: s.activation(consts[:, CV_LNGAM:CV_LNGAM + 16],
                                            consts[:, CV_GAM:CV_GAM + 16], AF.Ln))
        v1 = vop(lambda v: v.tensor_scalar(consts[:, CV_GT:CV_GT + 16],
                                           consts[:, CV_LNGAM:CV_LNGAM + 16],
                                           2048.0, None, ALU.mult),
                 waits=[("ss", ss_lng)])
        sop(lambda s: s.activation(consts[:, CV_G2048:CV_G2048 + 16],
                                   consts[:, CV_GT:CV_GT + 16], AF.Exp),
            waits=[("sv", v1)])
        vop(lambda v: v.memset(consts[:, CV_G512P:CV_G512P + 16], 1.0))
        prev_exp = 0
        for tau in range(1, NT):
            v2 = vop(lambda v, tau=tau: v.tensor_scalar(
                consts[:, CV_GT + 16:CV_GT + 32],
                consts[:, CV_LNGAM:CV_LNGAM + 16],
                512.0 * tau, None, ALU.mult),
                waits=[("ss", ss_lng), ("ss", prev_exp)])
            prev_exp = sop(lambda s, tau=tau: s.activation(
                consts[:, CV_G512P + 16 * tau:CV_G512P + 16 * (tau + 1)],
                consts[:, CV_GT + 16:CV_GT + 32], AF.Exp), waits=[("sv", v2)])
        ss_gam = q.now("ss")

        b_, psl, w_ = bk.acquire()
        tb = top(lambda t, psl=psl: t.matmul(psl, ones128[0:1, :],
                                             rowsb[0:1, 0:TT],
                                             start=True, stop=True),
                 waits=[("dsy", ldc), ("sv", sv_ones)] + w_)
        riot_sv = vop(lambda v, psl=psl: v.tensor_copy(riot[:, :], psl),
                      waits=[("st", tb)])
        bk.release(b_, "sv", riot_sv)

        split_units = []
        for src_t, hi_t, lo_t in ((w0T, w0hi_d, w0lo_d), (w1T, w1hi_d, w1lo_d)):
            for i in range(64):
                for hh in range(2):
                    c0, c1 = hh * 1024, (hh + 1) * 1024
                    split_units.append((src_t[i, :, c0:c1], hi_t[i, :, c0:c1],
                                        lo_t[i, :, c0:c1]))
        spl = {"i": 0, "store": [0, 0]}
        spl_in = [reg(O_SPL, 1024), reg(O_SPL + 1024, 1024)]
        spl_hi = [reg(O_SPLO, 512).bitcast(F16), reg(O_SPLO + 512, 512).bitcast(F16)]
        spl_lo = [reg(O_SPLO + 1024, 512).bitcast(F16),
                  reg(O_SPLO + 1536, 512).bitcast(F16)]

        def drip(n):
            for _ in range(n):
                i = spl["i"]
                if i >= len(split_units):
                    return
                spl["i"] += 1
                src, hd, ld_ = split_units[i]
                sl = i % 2
                d = q.dma("sync", spl_in[sl], src,
                          waits=[("dgp", spl["store"][sl])])
                s_ = sop(lambda s, sl=sl: s.copy(spl_hi[sl], spl_in[sl]),
                         waits=[("dsy", d)])
                v_ = vop(lambda v, sl=sl: v.tensor_tensor(
                    spl_lo[sl], spl_in[sl], spl_hi[sl], ALU.subtract),
                    waits=[("ss", s_)])
                q.dma("gpsimd", hd, spl_hi[sl], waits=[("sv", v_)])
                spl["store"][sl] = q.dma("gpsimd", ld_, spl_lo[sl])

        asp_in = [reg(O_RD, 2048), reg(O_RD + 2048, 2048)]
        asp_out = [reg(O_RD + 4096, 2048), reg(O_RD + 6144, 2048)]
        asp = {"i": 0, "store": [0, 0], "ready": {}}
        _amats = (wkT, wvT, wqT, woT)

        def adrip(n):
            for _ in range(n):
                i = asp["i"]
                if i >= 64:
                    return
                asp["i"] += 1
                mat, o = _amats[i // 16], i % 16
                sl = i % 2
                d = q.dma("sync", asp_in[sl], mat[o, :, :],
                          waits=[("dgp", asp["store"][sl])])
                hi = asp_out[sl][:, 0:1024].bitcast(F16)
                lo = asp_out[sl][:, 1024:2048].bitcast(F16)
                s_ = sop(lambda s, sl=sl, hi=hi: s.copy(hi, asp_in[sl]),
                         waits=[("dsy", d)])
                v_ = vop(lambda v, sl=sl, hi=hi, lo=lo: v.tensor_tensor(
                    lo, asp_in[sl], hi, ALU.subtract), waits=[("ss", s_)])
                asp["store"][sl] = q.dma("gpsimd", wapair_d[i, :, :],
                                         asp_out[sl].bitcast(F16),
                                         waits=[("sv", v_)])
                asp["ready"][i] = asp["store"][sl]

        wmbuf = reg3(O_RC, NCH, TT)
        NJ = MODW // TT
        ss_sil = s1
        prev_cp = 0
        wm_mm = [0, 0]
        for j in range(NJ):
            wb = wmbuf
            dj = q.dma("sync", wb[:, :, :],
                       wmods[:, j * TT:(j + 1) * TT].rearrange("(k p) t -> p k t", p=P),
                       waits=[("st", wm_mm[0])])
            b_, psl, w_ = bk.acquire()
            tm = None
            for k in range(NCH):
                tm = top(lambda t, psl=psl, k=k, wb=wb: t.matmul(
                    psl[0:1, :], consts[:, CV_SILC + k:CV_SILC + k + 1],
                    wb[:, k, :], start=(k == 0), stop=(k == NCH - 1)),
                    waits=([("dsy", dj), ("ss", ss_sil)] + w_) if k == 0 else [])
            wm_mm[0] = tm
            prev_cp = vop(lambda v, psl=psl, j=j: v.tensor_copy(
                modc[:, j * TT:(j + 1) * TT], psl[0:1, :]), waits=[("st", tm)])
            bk.release(b_, "sv", prev_cp)
        g1 = q.dma("gpsimd", ag1_in[:, :], modc[:, :], waits=[("sv", prev_cp)])
        cc1 = q.op("gpsimd", lambda g: g.collective_compute(
            "AllGather", ALU.bypass,
            replica_groups=[[0, 1, 2, 3], [4, 5, 6, 7]],
            ins=[ag1_in.opt()], outs=[ag1_out.opt()]),
            waits=[("dgp", g1)], inc="scc")
        gmod = 0
        for j in range(4):
            gmod = q.dma("gpsimd", consts[:, CV_MODF + 24 * j:CV_MODF + 24 * (j + 1)],
                         ag1_out[j, :].rearrange("(k p) -> p k", p=P),
                         waits=[("scc", cc1)] if j == 0 else [], slow=True)
        MF = CV_MODF
        vop(lambda v: v.tensor_copy(consts[:, CV_SA:CV_SA + 16],
                                    consts[:, MF:MF + 16]), waits=[("dgp", gmod)])
        vop(lambda v: v.tensor_scalar(consts[:, CV_ONEPCA:CV_ONEPCA + 16],
                                      consts[:, MF + 16:MF + 32], 1.0, None, ALU.add))
        vop(lambda v: v.tensor_copy(consts[:, CV_SB:CV_SB + 16],
                                    consts[:, MF + 48:MF + 64]))
        vop(lambda v: v.tensor_scalar(consts[:, CV_ONEPCB:CV_ONEPCB + 16],
                                      consts[:, MF + 64:MF + 80], 1.0, None, ALU.add))
        sv_e = 0
        for src, dst in ((MF + 32, CV_GA), (MF + 80, CV_GB)):
            sv_e = vop(lambda v, src=src: v.tensor_scalar(
                consts[:, CV_GT:CV_GT + 16], consts[:, src:src + 16],
                0.0, None, ALU.is_ge), waits=[("ss", ss_gam), ("sv", sv_e)])
            sv_e = vop(lambda v: v.tensor_scalar(
                consts[:, CV_GT:CV_GT + 16], consts[:, CV_GT:CV_GT + 16],
                2.0, -1.0, ALU.mult, ALU.add), waits=[("sv", sv_e)])
            sv_e = vop(lambda v, src=src: v.tensor_scalar(
                consts[:, CV_GT + 16:CV_GT + 32], consts[:, src:src + 16],
                -1.0, None, ALU.mult), waits=[("sv", sv_e)])
            sv_e = vop(lambda v, src=src: v.tensor_tensor(
                consts[:, CV_GT + 16:CV_GT + 32], consts[:, src:src + 16],
                consts[:, CV_GT + 16:CV_GT + 32], ALU.max), waits=[("sv", sv_e)])
            sv_e = vop(lambda v: v.tensor_scalar(
                consts[:, CV_GT + 16:CV_GT + 32], consts[:, CV_GT + 16:CV_GT + 32],
                0.25, None, ALU.max), waits=[("sv", sv_e)])
            sv_e = vop(lambda v, dst=dst: v.tensor_tensor(
                consts[:, dst:dst + 16], consts[:, CV_GT:CV_GT + 16],
                consts[:, CV_GT + 16:CV_GT + 32], ALU.mult), waits=[("sv", sv_e)])
        sv_mod = sv_e
        phA_st = q.now("st")

        def emit_ln(src3, n_t, dep_waits, xc_dst3=None):
            dst3 = src3 if xc_dst3 is None else xc_dst3
            b1_, ps1, w1_ = bk.acquire()
            b3_, ps3, w3_ = bk.acquire()
            tm = None
            tm3 = None
            prev_mm = 0
            for k in range(NCH):
                tm = top(lambda t, ps1=ps1, k=k: t.matmul(
                    ps1[0:1, 0:n_t], onesc[:, 0:1], src3[:, k, :],
                    start=(k == 0), stop=(k == NCH - 1)),
                    waits=(list(dep_waits) + w1_) if k == 0 else [])
                sq = reg(O_SQ, n_t) if k % 2 == 0 else reg(O_SQ + 512, n_t)
                vs = vop(lambda v, k=k, sq=sq: v.tensor_tensor(
                    sq, src3[:, k, :], src3[:, k, :], ALU.mult),
                    waits=(list(dep_waits) if k == 0 else []) +
                          ([("st", prev_mm)] if k >= 2 else []) +
                          ([sqg[k % 2]] if k < 2 else []))
                tm3 = top(lambda t, ps3=ps3, k=k, sq=sq: t.matmul(
                    ps3[0:1, 0:n_t], onesc[:, 0:1], sq,
                    start=(k == 0), stop=(k == NCH - 1)),
                    waits=[("sv", vs)] + (w3_ if k == 0 else []))
                prev_mm = tm3
            mu = rowsb[0:1, 0:n_t]
            e2 = rowsb[0:1, 512:512 + n_t]
            musq = reg(O_SQ, n_t)[0:1, :]
            vv = vop(lambda v, ps1=ps1: v.tensor_scalar(
                mu, ps1[0:1, 0:n_t], 1.0 / C, None, ALU.mult), waits=[("st", tm)])
            bk.release(b1_, "sv", vv)
            vv = vop(lambda v, ps3=ps3: v.tensor_scalar(
                e2, ps3[0:1, 0:n_t], 1.0 / C, None, ALU.mult),
                waits=[("st", tm3), ("sv", vv)])
            bk.release(b3_, "sv", vv)
            vv = vop(lambda v: v.tensor_tensor(musq, mu, mu, ALU.mult),
                     waits=[("sv", vv)])
            vv = vop(lambda v: v.tensor_tensor(e2, e2, musq, ALU.subtract),
                     waits=[("sv", vv)])
            sqg[0] = ("sv", vv)
            sqg[1] = ("st", tm3)
            sr = sop(lambda s: s.activation(e2, e2, AF.Sqrt, bias=eps5[0:1, 0:1]),
                     waits=[("sv", vv)])
            vv3 = vop(lambda v: v.reciprocal(e2, e2), waits=[("ss", sr)])
            vv3 = vop(lambda v: v.tensor_tensor(mu, mu, e2, ALU.mult),
                      waits=[("sv", vv3)])
            b4_, ps4, w4_ = bk.acquire()
            b5_, ps5, w5_ = bk.acquire()
            t4 = top(lambda t, ps4=ps4: t.matmul(ps4[:, 0:n_t], ones128[0:1, :], e2,
                                                 start=True, stop=True),
                     waits=[("sv", vv3)] + w4_)
            t5 = top(lambda t, ps5=ps5: t.matmul(ps5[:, 0:n_t], ones128[0:1, :], mu,
                                                 start=True, stop=True),
                     waits=[("sv", vv3)] + w5_)
            last = 0
            for k in range(NCH):
                last = vop(lambda v, k=k, ps4=ps4: v.tensor_tensor(
                    dst3[:, k, :], src3[:, k, :], ps4[:, 0:n_t], ALU.mult),
                    waits=[("st", t4)] if k == 0 else [])
            for k in range(NCH):
                last = vop(lambda v, k=k, ps5=ps5: v.tensor_tensor(
                    dst3[:, k, :], dst3[:, k, :], ps5[:, 0:n_t], ALU.subtract),
                    waits=[("st", t5)] if k == 0 else [])
            bk.release(b4_, "sv", last)
            bk.release(b5_, "sv", last)
            return last

        wba = [reg3(O_WBA, NCH, P), reg3(O_WBA + 2048, NCH, P)]
        wba_mm = [0, 0]

        def emit_attn_mm(wT_ap, rhs3, rhs_waits, evac, drip_n=0):
            for o in range(NCH):
                if drip_n:
                    drip(drip_n)
                sl = o % 2
                d = q.dma("sync", wba[sl][:, :, :],
                          wT_ap[o, :, :].rearrange("p (k o) -> p k o", k=NCH),
                          waits=[("st", wba_mm[sl])])
                b_, psl, w_ = bk.acquire()
                tm = None
                for k in range(NCH):
                    tm = top(lambda t, psl=psl, k=k, sl=sl: t.matmul(
                        psl, wba[sl][:, k, :], rhs3[:, k, :],
                        start=(k == 0), stop=(k == NCH - 1)),
                        waits=([("dsy", d)] + list(rhs_waits) + w_) if k == 0 else [])
                wba_mm[sl] = tm
                sem, val = evac(o, psl, tm)
                bk.release(b_, sem, val)

        wba_pair = [reg(O_WBA, 2048).bitcast(F16),
                    reg(O_WBA + 2048, 2048).bitcast(F16)]
        wba_h = [reg3h(O_WBA, NCH, P), reg3h(O_WBA + 2048, NCH, P)]
        wba_l = [reg3h(O_WBA + 1024, NCH, P), reg3h(O_WBA + 3072, NCH, P)]

        def emit_attn_mm_f16(mat_i, rhs_h, rhs_l, rhs_waits, evac, drip_n=0):
            for o in range(NCH):
                img = mat_i * 16 + o
                while img not in asp["ready"]:
                    adrip(1)
                if drip_n:
                    drip(drip_n)
                sl = o % 2
                d = q.dma("sync", wba_pair[sl], wapair_d[img, :, :],
                          waits=[("st", wba_mm[sl]), ("dgp", asp["ready"][img])])
                b_, psl, w_ = bk.acquire()
                tm = None
                for k in range(NCH):
                    w0 = ([("dsy", d)] + list(rhs_waits) + w_) if k == 0 else []
                    tm = top(lambda t, psl=psl, k=k, sl=sl: t.matmul(
                        psl, wba_h[sl][:, k, :], rhs_h[:, k, :],
                        start=(k == 0), stop=False), waits=w0)
                    tm = top(lambda t, psl=psl, k=k, sl=sl: t.matmul(
                        psl, wba_h[sl][:, k, :], rhs_l[:, k, :],
                        start=False, stop=False))
                    tm = top(lambda t, psl=psl, k=k, sl=sl: t.matmul(
                        psl, wba_l[sl][:, k, :], rhs_h[:, k, :],
                        start=False, stop=(k == NCH - 1)))
                wba_mm[sl] = tm
                sem, val = evac(o, psl, tm)
                bk.release(b_, sem, val)

        def emit_pairs(src3, dst_h, dst_l, waits):
            sv = 0
            for k in range(NCH):
                sv = vop(lambda v, k=k, src3=src3, dst_h=dst_h: v.tensor_copy(
                    dst_h[:, k, :], src3[:, k, :]),
                    waits=list(waits) if k == 0 else [])
            for k in range(NCH):
                sv = vop(lambda v, k=k, src3=src3, dst_h=dst_h, dst_l=dst_l:
                         v.tensor_tensor(dst_l[:, k, :], src3[:, k, :],
                                         dst_h[:, k, :], ALU.subtract))
            return sv

        def emit_norms(src3, ev_dep):
            prev_mm = [0, 0]
            last = 0
            for k in range(NCH):
                j = k % 2
                sq = reg(O_SQ, TT) if j == 0 else reg(O_SQ + 512, TT)
                vs = vop(lambda v, k=k, sq=sq, src3=src3: v.tensor_tensor(
                    sq, src3[:, k, :], src3[:, k, :], ALU.mult),
                    waits=([("ss", ev_dep)] if k == 0 else []) +
                          [("st", prev_mm[j]), sqg[j]])
                bn_, psn, wn_ = bk.acquire()
                tmn = top(lambda t, psn=psn, sq=sq: t.matmul(
                    psn, ones128[:, :], sq, start=True, stop=True),
                    waits=[("sv", vs)] + wn_)
                prev_mm[j] = tmn
                sq_s = sop(lambda s, psn=psn, sq=sq: s.activation(
                    sq, psn, AF.Sqrt), waits=[("st", tmn)])
                bk.release(bn_, "ss", sq_s)
                vn1 = vop(lambda v, sq=sq: v.tensor_scalar(
                    sq, sq, 1e-12, None, ALU.max), waits=[("ss", sq_s)])
                vn2 = vop(lambda v, sq=sq: v.reciprocal(sq, sq), waits=[("sv", vn1)])
                last = vop(lambda v, k=k, sq=sq, src3=src3: v.tensor_tensor(
                    src3[:, k, :], src3[:, k, :], sq, ALU.mult),
                    waits=[("sv", vn2)])
                sqg[j] = ("sv", last)
            return last

        XB = [reg3(O_X0, NCH, 3 + TT), reg3(O_X1, NCH, 3 + TT)]
        RB = reg3(O_RB, NCH, TT)
        RC = reg3(O_RC, NCH, TT)
        RD = reg3(O_RD, NCH, TT)
        ph_h = reg3h(O_RD, NCH, TT)
        ph_l = reg3h(O_RD + 4096, NCH, TT)
        powb = reg(O_SQ, TT)
        gstg = reg(O_POW, TT)
        tail3 = consts[:, CV_TAIL:CV_TAIL + 48].rearrange("p (k t) -> p k t", k=NCH)

        sin_cell = [0]

        def emit_ag2(scan_done):
            g2 = q.dma("gpsimd", ag2_in[:, :], consts[:, CV_CARRY:CV_CARRY + 16],
                       waits=[("sv", scan_done)])
            cc2 = q.op("gpsimd", lambda g: g.collective_compute(
                "AllGather", ALU.bypass,
                replica_groups=[[0, 1, 2, 3], [4, 5, 6, 7]],
                ins=[ag2_in.opt()], outs=[ag2_out.opt()]),
                waits=[("dgp", g2)], inc="scc")
            gc = 0
            for j in range(4):
                gc = q.dma("gpsimd", consts[:, CV_CJ + 16 * j:CV_CJ + 16 * (j + 1)],
                           ag2_out[j * P:(j + 1) * P, :],
                           waits=[("scc", cc2)] if j == 0 else [])
            sv_a = vop(lambda v: v.memset(consts[:, CV_SIN:CV_SIN + 16], 0.0),
                       waits=[("dgp", gc)])
            sv_a = vop(lambda v: v.tensor_copy(consts[:, CV_GT:CV_GT + 16],
                                               consts[:, CV_CJ:CV_CJ + 16]),
                       waits=[("sv", sv_a)])
            for j in range(4):
                sv_a = vop(lambda v, j=j: v.scalar_tensor_tensor(
                    consts[:, CV_SIN:CV_SIN + 16], consts[:, CV_GT:CV_GT + 16],
                    ccol(CV_MSEL + j), consts[:, CV_SIN:CV_SIN + 16],
                    ALU.mult, ALU.add), waits=[("sv", sv_a)])
                if j < 3:
                    sv_a = vop(lambda v: v.tensor_tensor(
                        consts[:, CV_GT:CV_GT + 16], consts[:, CV_GT:CV_GT + 16],
                        consts[:, CV_G2048:CV_G2048 + 16], ALU.mult),
                        waits=[("sv", sv_a)])
                    sv_a = vop(lambda v, j=j: v.tensor_tensor(
                        consts[:, CV_GT:CV_GT + 16], consts[:, CV_GT:CV_GT + 16],
                        consts[:, CV_CJ + 16 * (j + 1):CV_CJ + 16 * (j + 2)],
                        ALU.add), waits=[("sv", sv_a)])
            sin_cell[0] = sv_a

        x_pref = [0, 0]
        x_pref[0] = q.dma("gpsimd", XB[0][:, :, :],
                          xfm[:, 0:3 + TT].rearrange("(k p) t -> p k t", p=P))
        qmm_st = [0, 0]
        az_store = 0
        prev_gq = 0

        for tau in range(NT):
            buf = XB[tau % 2]
            bufm = buf[:, :, 3:3 + TT]
            if tau + 1 < NT:
                x_pref[(tau + 1) % 2] = q.dma(
                    "gpsimd", XB[(tau + 1) % 2][:, :, 3:3 + TT],
                    xfm[:, 3 + (tau + 1) * TT:3 + (tau + 2) * TT].rearrange(
                        "(k p) t -> p k t", p=P),
                    waits=[("st", qmm_st[(tau + 1) % 2]), ("scc", cc1)])
            if tau == 0:
                hsv = emit_ln(buf[:, :, 0:3], 3, [("dgp", x_pref[0]), ("sv", sv_mod)])
                for k in range(NCH):
                    hsv = vop(lambda v, k=k, buf=buf: v.tensor_scalar(
                        buf[:, k, 0:3], buf[:, k, 0:3],
                        ccol(CV_ONEPCA + k), ccol(CV_SA + k), ALU.mult, ALU.add),
                        waits=[("sv", hsv)] if k == 0 else [])
                hm = 0
                for k in range(NCH):
                    hm = vop(lambda v, k=k, buf=buf: v.tensor_scalar(
                        buf[:, k, 0:3], buf[:, k, 0:3], ccol(CV_HM), None, ALU.mult),
                        waits=[("sv", hsv)] if k == 0 else [])
                dep = [("sv", hm)]
            else:
                tl = vop(lambda v, buf=buf: v.tensor_copy(buf[:, :, 0:3], tail3),
                         waits=[("dgp", x_pref[tau % 2])])
                dep = [("sv", tl)]
            lsv = emit_ln(bufm, TT, dep + [("dgp", x_pref[tau % 2]), ("sv", sv_mod)])
            h1_sv = 0
            for k in range(NCH):
                h1_sv = vop(lambda v, k=k, bufm=bufm: v.tensor_scalar(
                    bufm[:, k, :], bufm[:, k, :],
                    ccol(CV_ONEPCA + k), ccol(CV_SA + k), ALU.mult, ALU.add),
                    waits=[("sv", lsv)] if k == 0 else [])
            tail_sv = vop(lambda v, buf=buf: v.tensor_copy(
                tail3, buf[:, :, TT:TT + 3]), waits=[("sv", h1_sv)])
            if tau == 0:
                adrip(64)
                asp_done = max(asp["store"])
                pair_dep = [("sv", h1_sv), ("dgp", asp_done)]
            else:
                pair_dep = [("sv", h1_sv), ("sv", prev_gq)]
            pair_sv = emit_pairs(bufm, ph_h, ph_l, pair_dep)

            def evac_to(dst3, extra):
                def f(o, psl, tm):
                    c = sop(lambda s, o=o, psl=psl, dst3=dst3: s.copy(
                        dst3[:, o, :], psl),
                        waits=[("st", tm)] + (extra if o == 0 else []))
                    return ("ss", c)
                return f

            emit_attn_mm_f16(0, ph_h, ph_l, [("sv", pair_sv)],
                             evac_to(RC, [("dgp", az_store), ("st", phA_st)]),
                             drip_n=1)
            k_ev = q.now("ss")

            csv = 0
            for k in range(NCH):
                csv = vop(lambda v, k=k, buf=buf: v.tensor_scalar(
                    RB[:, k, :], buf[:, k, 0:TT], ccol(CV_CONVW + 4 * k),
                    None, ALU.mult),
                    waits=[("sv", h1_sv), ("dgp", az_store)] if k == 0 else [])
                for kk in range(1, 4):
                    csv = vop(lambda v, k=k, kk=kk, buf=buf: v.scalar_tensor_tensor(
                        RB[:, k, :], buf[:, k, kk:kk + TT],
                        ccol(CV_CONVW + 4 * k + kk), RB[:, k, :], ALU.mult, ALU.add))
            ssv = 0
            for k in range(NCH):
                ssv = sop(lambda s, k=k: s.activation(
                    RB[:, k, :], RB[:, k, :], AF.Silu, bias=ccol(CV_CONVB + k)),
                    waits=[("sv", csv)] if k == 0 else [])
            kn_sv = emit_norms(RC, k_ev)

            def evac_u(o, psl, tm):
                c = vop(lambda v, o=o, psl=psl: v.tensor_tensor(
                    RC[:, o, :], RC[:, o, :], psl, ALU.mult),
                    waits=[("st", tm)] + ([("sv", kn_sv)] if o == 0 else []))
                return ("sv", c)

            emit_attn_mm_f16(1, ph_h, ph_l, [("sv", pair_sv)], evac_u, drip_n=1)
            u_sv = q.now("sv")

            gate_st = [0]

            def evac_gate(o, psl, tm):
                c = sop(lambda s, o=o, psl=psl: s.activation(
                    gstg, psl, AF.Sigmoid, bias=ccol(CV_GATEB + o)),
                    waits=[("st", tm), ("dgp", gate_st[0])])
                gate_st[0] = q.dma("gpsimd", gate_d[o * P:(o + 1) * P, :], gstg,
                                   waits=[("ss", c)])
                return ("ss", c)

            emit_attn_mm(wgT, RB, [("ss", ssv)], evac_gate, drip_n=1)
            gate_dma = gate_st[0]
            gate_mm_st = q.now("st")

            scan_k = []
            scan_sv = 0
            for k in range(NCH):
                scan_sv = vop(lambda v, k=k: v.tensor_tensor_scan(
                    RB[:, k, :], ccol(CV_GAM + k).broadcast_to([P, TT]),
                    RC[:, k, :], ccol(CV_CARRY + k), ALU.mult, ALU.add),
                    waits=([("sv", u_sv), ("st", gate_mm_st)] if k == 0 else []))
                scan_k.append(scan_sv)
                scan_sv = vop(lambda v, k=k: v.tensor_copy(
                    ccol(CV_CARRY + k), RB[:, k, TT - 1:TT]),
                    waits=[("sv", scan_sv)])
            if tau == NT - 1:
                emit_ag2(scan_sv)

            def evac_q(o, psl, tm):
                c = sop(lambda s, o=o, psl=psl: s.copy(RC[:, o, :], psl),
                        waits=[("st", tm), ("sv", scan_k[o])])
                return ("ss", c)

            emit_attn_mm_f16(2, ph_h, ph_l, [("sv", pair_sv)], evac_q, drip_n=1)
            q_ev = q.now("ss")
            qmm_st[tau % 2] = q.now("st")
            qn_sv = emit_norms(RC, q_ev)

            gq_sv = 0
            grel = [0, 0]
            for k in range(NCH):
                j = k % 2
                gb_buf = reg(O_SQ, TT) if j == 0 else reg(O_SQ + 512, TT)
                grel[j] = q.dma("gpsimd", gb_buf,
                                gate_d[k * P:(k + 1) * P, :],
                                waits=[("dgp", gate_dma), sqg[j]])
                gq_sv = vop(lambda v, k=k, gb_buf=gb_buf: v.tensor_tensor(
                    RC[:, k, :], RC[:, k, :], gb_buf, ALU.mult),
                    waits=[("dgp", grel[j])] +
                          ([("sv", qn_sv)] if k == 0 else []))
                sqg[j] = ("sv", gq_sv)
            prev_gq = gq_sv
            q.dma("gpsimd", S_d[:, tau * TT:(tau + 1) * TT].rearrange(
                "(k p) t -> p k t", p=P), RB[:, :, :], waits=[("sv", scan_sv)])
            az_store = q.dma("gpsimd", G_d[:, tau * TT:(tau + 1) * TT].rearrange(
                "(k p) t -> p k t", p=P), RC[:, :, :], waits=[("sv", gq_sv)])

        drip(len(split_units))
        assert spl["i"] == len(split_units)
        spl_done = q.now("dgp")
        phB_sv = q.now("sv")
        phB_ss = q.now("ss")
        phB_st = q.now("st")

        XPs = [XB[0][:, :, 0:TT], XB[1][:, :, 0:TT]]
        RBY = RB
        D16 = reg3h(O_RD, 32, TT)
        H2 = reg3h(O_H2, NCH, TT)
        wfflat = [arena[:, O_RC + 2048 * i:O_RC + 2048 * (i + 1)].bitcast(F16)
                  for i in range(4)]
        wf = [wfflat[i].rearrange("p (o k c) -> p o k c", o=2, k=NCH)
              for i in range(4)]
        wf_mm = [0, 0, 0, 0]
        e16 = [reg(O_POW, 256).bitcast(F16), reg(O_POW + 256, 256).bitcast(F16)]

        out_store = [0, 0]
        final_sv = phB_sv
        last_w0_mm = phB_st
        last_w1_mm = phB_st
        corr_sv = [0, 0]
        dZ = 0

        def emit_corr(tau, extra_waits):
            XP = XPs[tau % 2]
            dA = q.dma("gpsimd", XP[:, :, :],
                       S_d[:, tau * TT:(tau + 1) * TT].rearrange(
                           "(k p) t -> p k t", p=P),
                       waits=[("dgp", az_store), ("dgp", out_store[tau % 2])] +
                             list(extra_waits))
            wv = vop(lambda v, tau=tau: v.tensor_tensor(
                consts[:, CV_WTAU:CV_WTAU + 16], consts[:, CV_SIN:CV_SIN + 16],
                consts[:, CV_G512P + 16 * tau:CV_G512P + 16 * (tau + 1)], ALU.mult),
                waits=[("sv", sin_cell[0])])
            sv = 0
            pe = 0
            for k in range(NCH):
                pe = sop(lambda s, k=k: s.activation(
                    powb, riot[:, :], AF.Exp, scale=ccol(CV_LNGAM + k)),
                    waits=([("ss", ss_gam), ("sv", wv), ("dgp", dA), sqg[0]]
                           if k == 0 else [("sv", sv)]))
                sv = vop(lambda v, k=k, XP=XP: v.scalar_tensor_tensor(
                    XP[:, k, :], powb, ccol(CV_WTAU + k), XP[:, k, :],
                    ALU.mult, ALU.add),
                    waits=[("ss", pe)] + ([("sv", wv), ("dgp", dA)] if k == 0 else []))
            sqg[0] = ("sv", sv)
            corr_sv[tau % 2] = sv
            return sv

        for tau in range(NT):
            XP = XPs[tau % 2]
            if tau == 0:
                emit_corr(0, [("sv", final_sv), ("ss", phB_ss), ("st", phB_st)])
                dZ = q.dma("gpsimd", RBY[:, :, :],
                           G_d[:, 0:TT].rearrange("(k p) t -> p k t", p=P),
                           waits=[("dgp", az_store), ("sv", final_sv)])

            o_sv = 0
            for k in range(NCH):
                o_sv = vop(lambda v, k=k, XP=XP: v.tensor_tensor(
                    XP[:, k, :], XP[:, k, :], RBY[:, k, :], ALU.mult),
                    waits=[("dgp", dZ), ("sv", corr_sv[tau % 2])]
                    if k == 0 else [])

            pair_d = emit_pairs(XP, ph_h, ph_l,
                                [("sv", o_sv), ("st", last_w1_mm),
                                 ("st", qmm_st[0]), ("st", qmm_st[1])])

            def evac_y(o, psl, tm):
                c = sop(lambda s, o=o, psl=psl: s.copy(RBY[:, o, :], psl),
                        waits=[("st", tm)])
                return ("ss", c)

            emit_attn_mm_f16(3, ph_h, ph_l, [("sv", pair_d)], evac_y)
            wo_st = q.now("st")
            y_ev = q.now("ss")
            l2 = emit_ln(RBY, TT, [("ss", y_ev)])
            dx = q.dma("gpsimd", XP[:, :, :],
                       xfm[:, 3 + tau * TT:3 + (tau + 1) * TT].rearrange(
                           "(k p) t -> p k t", p=P),
                       waits=[("st", wo_st)])
            for k in range(NCH):
                l2 = vop(lambda v, k=k: v.tensor_scalar(
                    RBY[:, k, :], RBY[:, k, :], ccol(CV_LNW + k), ccol(CV_LNB + k),
                    ALU.mult, ALU.add), waits=[("sv", l2)] if k == 0 else [])
            xp_sv = 0
            for k in range(NCH):
                xp_sv = vop(lambda v, k=k, XP=XP: v.scalar_tensor_tensor(
                    XP[:, k, :], RBY[:, k, :], ccol(CV_GA + k), XP[:, k, :],
                    ALU.mult, ALU.add),
                    waits=[("sv", l2), ("dgp", dx)] if k == 0 else [])
            l3 = emit_ln(XP, TT, [("sv", xp_sv)], xc_dst3=RBY)
            h2_sv = 0
            h2_k = []
            for k in range(NCH):
                h2_sv = vop(lambda v, k=k: v.tensor_scalar(
                    H2[:, k, :], RBY[:, k, :], ccol(CV_ONEPCB + k), ccol(CV_SB + k),
                    ALU.mult, ALU.add),
                    waits=([("sv", l3)] + ([("st", last_w0_mm)] if tau > 0 else []) +
                           ([("dgp", spl_done)] if tau == 0 else []))
                    if k == 0 else [])
                h2_k.append(h2_sv)

            ge_k = [0] * 32
            for half in range(2):
                w0dep = [("dgp", spl_done)] + \
                    ([("st", last_w1_mm)] if (half == 1 or tau > 0) else [])
                for og in range(16):
                    o0 = half * 32 + og * 2
                    slh, sll = (og % 2) * 2, (og % 2) * 2 + 1
                    for oo in range(2):
                        q.dma("sync", wfflat[slh][:, oo * 2048:(oo + 1) * 2048],
                              w0hi_d[o0 + oo, :, :],
                              waits=[("st", wf_mm[slh]), ("dgp", spl_done),
                                     ("dgp", az_store)])
                        dwl = q.dma(
                            "sync", wfflat[sll][:, oo * 2048:(oo + 1) * 2048],
                            w0lo_d[o0 + oo, :, :],
                            waits=[("st", wf_mm[sll]), ("dgp", spl_done),
                                   ("dgp", az_store)])
                    for oo in range(2):
                        b_, psl, w_ = bk.acquire()
                        tm = None
                        for k in range(NCH):
                            tm = top(lambda t, psl=psl, k=k, slh=slh, oo=oo: t.matmul(
                                psl, wf[slh][:, oo, k, :],
                                H2[:, k, :], start=(k == 0), stop=False),
                                waits=([("dsy", dwl), ("sv", h2_k[k])] + w0dep + w_)
                                if (k == 0 and oo == 0) else
                                ([("dsy", dwl), ("sv", h2_k[k])] if k == 0 else
                                 [("sv", h2_k[k])] if oo == 0 else []))
                            tm = top(lambda t, psl=psl, k=k, sll=sll, oo=oo: t.matmul(
                                psl, wf[sll][:, oo, k, :],
                                H2[:, k, :], start=False, stop=(k == NCH - 1)))
                        wf_mm[slh] = tm
                        wf_mm[sll] = tm
                        last_w0_mm = tm
                        ge = sop(lambda s, psl=psl, og=og, oo=oo, o0=o0: s.activation(
                            D16[:, og * 2 + oo, :], psl, AF.Gelu,
                            bias=ccol(CV_B0 + o0 + oo)), waits=[("st", tm)])
                        ge_k[og * 2 + oo] = ge
                        bk.release(b_, "ss", ge)
                d16_ss = q.now("ss")
                if half == 1 and tau + 1 < NT:
                    emit_corr(tau + 1, [])
                for op_ in range(8):
                    banks = []
                    for oo in range(2):
                        banks.append(bk.acquire())
                    dmas = []
                    for kb in range(2):
                        kbg = half * 2 + kb
                        slh, sll = (kb % 2) * 2, (kb % 2) * 2 + 1
                        for oo in range(2):
                            q.dma("sync", wfflat[slh][:, oo * 2048:(oo + 1) * 2048],
                                  w1hi_d[kbg * 16 + op_ * 2 + oo, :, :],
                                  waits=[("st", wf_mm[slh]), ("dgp", spl_done)])
                            dl = q.dma(
                                "sync", wfflat[sll][:, oo * 2048:(oo + 1) * 2048],
                                w1lo_d[kbg * 16 + op_ * 2 + oo, :, :],
                                waits=[("st", wf_mm[sll]), ("dgp", spl_done)])
                        dmas.append((dl, slh, sll))
                    kb_last = [0, 0]
                    for kb in range(2):
                        dl, slh, sll = dmas[kb]
                        for k in range(NCH):
                            for oo in range(2):
                                bix, psl, wb_ = banks[oo]
                                w01 = [("dsy", dl)] if k == 0 else []
                                w01 = w01 + ([("ss", ge_k[kb * 16 + k])]
                                             if oo == 0 else [])
                                if k == 0 and kb == 0:
                                    w01 = w01 + wb_
                                tm = top(lambda t, psl=psl, k=k, kb=kb, oo=oo,
                                         slh=slh: t.matmul(
                                    psl, wf[slh][:, oo, k, :],
                                    D16[:, kb * 16 + k, :],
                                    start=(kb == 0 and k == 0), stop=False),
                                    waits=w01)
                                tm = top(lambda t, psl=psl, k=k, kb=kb, oo=oo,
                                         sll=sll: t.matmul(
                                    psl, wf[sll][:, oo, k, :],
                                    D16[:, kb * 16 + k, :],
                                    start=False, stop=(kb == 1 and k == NCH - 1)))
                                kb_last[kb] = tm
                                if kb == 1 and k == NCH - 1:
                                    last_w1_mm = tm
                        wf_mm[slh] = kb_last[kb]
                        wf_mm[sll] = kb_last[kb]
                    for oo in range(2):
                        bix, psl, wb_ = banks[oo]
                        e = op_ * 2 + oo
                        if half == 0:
                            ev = sop(lambda s, psl=psl, e=e: s.activation(
                                RBY[:, e, :], psl, AF.Identity, bias=ccol(CV_B1 + e)),
                                waits=[("st", last_w1_mm)])
                            bk.release(bix, "ss", ev)
                        else:
                            ev = vop(lambda v, psl=psl, e=e: v.tensor_tensor(
                                e16[e % 2], RBY[:, e, :], psl, ALU.add),
                                waits=[("st", last_w1_mm)] +
                                      ([("sv", final_sv)] if e >= 2 else []))
                            bk.release(bix, "sv", ev)
                            final_sv = vop(lambda v, e=e, XP=XP: v.scalar_tensor_tensor(
                                XP[:, e, :], e16[e % 2], ccol(CV_GB + e),
                                XP[:, e, :], ALU.mult, ALU.add),
                                waits=[("sv", ev)])
            out_store[tau % 2] = q.dma(
                "gpsimd", out_fm[:, tau * TT:(tau + 1) * TT].rearrange(
                    "(k p) t -> p k t", p=P), XP[:, :, :], waits=[("sv", final_sv)])
            if tau + 1 < NT:
                dZ = q.dma("gpsimd", RBY[:, :, :],
                           G_d[:, (tau + 1) * TT:(tau + 2) * TT].rearrange(
                               "(k p) t -> p k t", p=P),
                           waits=[("sv", final_sv)])

        q.op("sync", lambda e: None, waits=[("dgp", q.now("dgp")),
                                            ("dsy", q.now("dsy"))])

        block = stack.enter_context(nc.Block())
        q.finalize(block)
    return nc
def kernel(**inputs):
    x = np.ascontiguousarray(inputs["L0_attn_x"], dtype=np.float32)
    def attn_img(w):
        return np.ascontiguousarray(
            np.asarray(w, np.float32).T.reshape(16, 128, 16, 128)
            .transpose(2, 1, 0, 3).reshape(16, 128, 2048))

    WqT = attn_img(inputs["L0_attn_Wq"])
    WkT = attn_img(inputs["L0_attn_Wk"])
    WvT = attn_img(inputs["L0_attn_Wv"])
    WgT = attn_img(inputs["L0_attn_gate_w"])
    WoT = attn_img(inputs["L0_attn_Wo"])
    W0T = np.ascontiguousarray(
        np.asarray(inputs["L0_ffn_W0"], np.float32).T.reshape(16, 128, 64, 128)
        .transpose(2, 1, 0, 3).reshape(64, 128, 2048))
    W1T = np.ascontiguousarray(
        np.asarray(inputs["L0_ffn_W1"], np.float32).T.reshape(4, 16, 128, 16, 128)
        .transpose(0, 3, 2, 1, 4).reshape(64, 128, 2048))
    wmod = np.asarray(inputs["L0_wmod"], dtype=np.float32)
    cin = np.asarray(inputs["L0_c"], dtype=np.float32)
    conv_w = np.ascontiguousarray(inputs["L0_attn_conv_w"][:, 0, :], dtype=np.float32)
    convb = np.ascontiguousarray(inputs["L0_attn_conv_b"], dtype=np.float32)
    gateb = np.ascontiguousarray(inputs["L0_attn_gate_b"], dtype=np.float32)
    lnw = np.ascontiguousarray(inputs["L0_attn_ln_w"], dtype=np.float32)
    lnb = np.ascontiguousarray(inputs["L0_attn_ln_b"], dtype=np.float32)
    gam = np.ascontiguousarray(inputs["L0_attn_gamma"].reshape(H), dtype=np.float32)
    b0 = np.ascontiguousarray(inputs["L0_ffn_bias0"], dtype=np.float32)
    b1 = np.ascontiguousarray(inputs["L0_ffn_bias1"], dtype=np.float32)

    iota = np.arange(1, TT + 1, dtype=np.float32).reshape(1, TT)

    nc = build_nc()
    in_maps = []
    for r in range(8):
        b, pos = r // 4, r % 4
        t0 = TC * pos
        if t0 == 0:
            halo = np.zeros((C, 3), np.float32)
        else:
            halo = np.ascontiguousarray(x[b, t0 - 3:t0].T)
        xfm = np.ascontiguousarray(
            np.concatenate([halo, x[b, t0:t0 + TC].T], axis=1))
        ms = np.zeros(4, np.float32)
        if pos > 0:
            ms[pos - 1] = 1.0
        in_maps.append({
            "xfm": xfm, "wqT": WqT, "wkT": WkT, "wvT": WvT, "wgT": WgT,
            "woT": WoT, "w0T": W0T, "w1T": W1T,
            "wmods": np.ascontiguousarray(wmod[:, pos * MODW:(pos + 1) * MODW]),
            "cbv": np.ascontiguousarray(cin[b]),
            "convw": conv_w, "convb": convb, "gateb": gateb,
            "lnw": lnw, "lnb": lnb, "gam16": gam, "b0v": b0, "b1v": b1,
            "msel": ms, "hmask": np.array([0.0 if pos == 0 else 1.0], np.float32),
            "iotar": iota,
        })
    res = run_bass_kernel_spmd(nc, in_maps, core_ids=list(range(8)))
    out = np.empty((2, 8192, C), np.float32)
    for r in range(8):
        b, pos = r // 4, r % 4
        out[b, TC * pos:TC * (pos + 1)] = res.results[r]["out_fm"].T
    return out


# version 29
# speedup vs baseline: 4.4981x; 1.0037x over previous
import numpy as np
import ml_dtypes
from contextlib import ExitStack

import concourse.bass as bass
import concourse.mybir as mybir
from concourse.bass_utils import run_bass_kernel_spmd

F32 = mybir.dt.float32
F16 = mybir.dt.float16
AF = mybir.ActivationFunctionType
ALU = mybir.AluOpType

C = 2048
H = 16
NCH = 16
TC = 2048
TT = 512
NT = TC // TT
NFF = 8192
P = 128
LN_EPS = 1e-5
MODW = 12288 // 4


SERIAL = False
SERIAL_DMA = []
SER_GROUPS = set()


class Q:
    ENGINES = ("sync", "gpsimd", "vector", "scalar", "tensor")

    def __init__(self, nc, stack):
        self.nc = nc
        self.q = {e: [] for e in self.ENGINES}
        self.sems = {}
        self.cnt = {}
        for name in ("st", "sv", "ss", "dsy", "dgp", "scc"):
            self.sems[name] = stack.enter_context(nc.semaphore(name))
            self.cnt[name] = 0

    ser_mark = 0

    def op(self, eng, fn, waits=(), inc=None, amt=1):
        if Q.ser_mark:
            waits = list(waits) + [("dgp", Q.ser_mark)]
        if SERIAL:
            waits = list(waits) + [(s, self.cnt[s]) for s in
                                   ("st", "sv", "ss", "dsy", "dgp")]
        elif SERIAL_DMA:
            ser = [x for x in SERIAL_DMA if True]
            if inc in ser:
                waits = list(waits) + [(s, self.cnt[s]) for s in
                                       ("st", "sv", "ss", "dsy", "dgp")]
            else:
                waits = list(waits) + [(s2, self.cnt[s2]) for s2 in ser]
        waits = [(s, v) for (s, v) in waits if v and v > 0]
        for s, v in waits:
            assert v <= self.cnt[s], f"wait {s}>={v} > emitted {self.cnt[s]} ({eng})"
        if inc is not None:
            self.cnt[inc] += amt
        sems = self.sems

        def body(e, fn=fn, waits=tuple(waits), inc=inc, amt=amt):
            for s, v in waits:
                e.wait_ge(sems[s], v)
            r = fn(e)
            if inc is not None:
                ins = r[-1] if isinstance(r, (list, tuple)) else r
                ins.then_inc(sems[inc], amt)
        self.q[eng].append(body)
        return self.cnt[inc] if inc is not None else None

    def dma(self, eng, out, in_, waits=(), slow=False, group=None):
        sem = "dsy" if eng == "sync" else "dgp"
        if group in SER_GROUPS:
            waits = list(waits) + [(s2, self.cnt[s2]) for s2 in
                                   ("st", "sv", "ss", "dsy", "dgp")]
        r = self.op(eng, lambda e, o=out, i=in_, s=slow: e.dma_start(
            out=o, in_=i, allow_slow_non_contiguous=s), waits, sem, 16)
        if group in SER_GROUPS:
            Q.ser_mark = r
        return r

    def now(self, s):
        return self.cnt[s]

    def finalize(self, block):
        for eng in self.ENGINES:
            fns = self.q[eng]
            if fns:
                getattr(block, eng)(lambda e, fns=fns: [f(e) for f in fns])


class Banks:

    def __init__(self, pA, pB):
        self.sl = [pA[:, i * 512:(i + 1) * 512] for i in range(4)] + \
                  [pB[:, i * 512:(i + 1) * 512] for i in range(4)]
        self.state = [("st", 0)] * 8
        self.i = 0

    def acquire(self):
        b = self.i % 8
        self.i += 1
        return b, self.sl[b], [self.state[b]]

    def release(self, b, sem, val):
        self.state[b] = (sem, val)


def build_nc():
    nc = bass.Bass("TRN2", target_bir_lowering=False, debug=False)

    xfm = nc.dram_tensor("xfm", [C, 3 + TC], F32, kind="ExternalInput").ap()
    wqT = nc.dram_tensor("wqT", [NCH, P, C], F32, kind="ExternalInput").ap()
    wkT = nc.dram_tensor("wkT", [NCH, P, C], F32, kind="ExternalInput").ap()
    wvT = nc.dram_tensor("wvT", [NCH, P, C], F32, kind="ExternalInput").ap()
    wgT = nc.dram_tensor("wgT", [NCH, P, C], F32, kind="ExternalInput").ap()
    woT = nc.dram_tensor("woT", [NCH, P, C], F32, kind="ExternalInput").ap()
    w0T = nc.dram_tensor("w0T", [64, P, C], F32, kind="ExternalInput").ap()
    w1T = nc.dram_tensor("w1T", [64, P, C], F32, kind="ExternalInput").ap()
    wmods = nc.dram_tensor("wmods", [C, MODW], F32, kind="ExternalInput").ap()
    cbv = nc.dram_tensor("cbv", [C], F32, kind="ExternalInput").ap()
    convw = nc.dram_tensor("convw", [C, 4], F32, kind="ExternalInput").ap()
    convb = nc.dram_tensor("convb", [C], F32, kind="ExternalInput").ap()
    gateb = nc.dram_tensor("gateb", [C], F32, kind="ExternalInput").ap()
    lnw = nc.dram_tensor("lnw", [C], F32, kind="ExternalInput").ap()
    lnb = nc.dram_tensor("lnb", [C], F32, kind="ExternalInput").ap()
    gam16 = nc.dram_tensor("gam16", [H], F32, kind="ExternalInput").ap()
    b0v = nc.dram_tensor("b0v", [NFF], F32, kind="ExternalInput").ap()
    b1v = nc.dram_tensor("b1v", [C], F32, kind="ExternalInput").ap()
    msel = nc.dram_tensor("msel", [4], F32, kind="ExternalInput").ap()
    hmask = nc.dram_tensor("hmask", [1], F32, kind="ExternalInput").ap()
    iotar = nc.dram_tensor("iotar", [1, TT], F32, kind="ExternalInput").ap()
    out_fm = nc.dram_tensor("out_fm", [C, TC], F32, kind="ExternalOutput").ap()

    wapair_d = nc.dram_tensor("wapair_d", [64, P, 2 * C], F16).ap()
    w0hi_d = nc.dram_tensor("w0hi_d", [64, P, C], F16).ap()
    w0lo_d = nc.dram_tensor("w0lo_d", [64, P, C], F16).ap()
    w1hi_d = nc.dram_tensor("w1hi_d", [64, P, C], F16).ap()
    w1lo_d = nc.dram_tensor("w1lo_d", [64, P, C], F16).ap()
    gate_d = nc.dram_tensor("gate_d", [C, TT], F32).ap()
    S_d = nc.dram_tensor("S_d", [C, TC], F32).ap()
    G_d = nc.dram_tensor("G_d", [C, TC], F32).ap()
    ag1_in = nc.dram_tensor("ag1_in", [1, MODW], F32).ap()
    ag1_out = nc.dram_tensor("ag1_out", [4, MODW], F32).ap()
    ag2_in = nc.dram_tensor("ag2_in", [P, H], F32).ap()
    ag2_out = nc.dram_tensor("ag2_out", [4 * P, H], F32).ap()

    stack = ExitStack()
    with stack:
        sb = lambda n_, s, d=F32: stack.enter_context(nc.sbuf_tensor(n_, s, d))

        ASZ = 50784
        arena = sb("arena", [P, ASZ])

        def reg(off, sz):
            return arena[:, off:off + sz]

        def reg3(off, k, t):
            return arena[:, off:off + k * t].rearrange("p (k t) -> p k t", k=k)

        def reg3h(off, k, t):
            return arena[:, off:off + (k * t) // 2].bitcast(F16).rearrange(
                "p (k t) -> p k t", k=k)

        O_X0 = 0
        O_X1 = 8240
        O_RB = 16480
        O_RC = 24672
        O_RD = 32864
        O_WBA = 41056
        O_SQ = 45152
        O_POW = 46176
        O_SPL = 46688
        O_SPLO = 48736
        O_H2 = O_SPL
        assert O_SPLO + 2048 <= ASZ

        consts = sb("consts", [P, 736])
        CV_ONEPCA, CV_SA, CV_GA = 0, 16, 32
        CV_ONEPCB, CV_SB, CV_GB = 48, 64, 80
        CV_LNW, CV_LNB, CV_CONVB, CV_GATEB, CV_B1 = 96, 112, 128, 144, 160
        CV_GAM, CV_LNGAM, CV_G2048 = 176, 192, 208
        CV_SIN, CV_WTAU = 224, 240
        CV_CARRY = 256
        CV_TAIL = 272
        CV_CJ = 320
        CV_MSEL = 384
        CV_HM = 388
        CV_CONVW = 392
        CV_B0 = 456
        CV_SILC = 520
        CV_MODF = 536
        CV_GT = 632
        CV_G512P = 664

        rowsb = sb("rowsb", [H, 1024])
        ones128 = sb("ones128", [P, P])
        onesc = sb("onesc", [P, 1])
        riot = sb("riot", [P, TT])
        eps5 = sb("eps5", [1, 4])
        modc = arena[0:1, O_RB:O_RB + MODW]

        pA = stack.enter_context(nc.psum_tensor("pA", [P, 2048], F32))
        pB = stack.enter_context(nc.psum_tensor("pB", [P, 2048], F32))

        q = Q(nc, stack)
        bk = Banks(pA, pB)
        sqg = [("sv", 0), ("sv", 0)]

        def vop(fn, waits=()):
            return q.op("vector", fn, waits, "sv")

        def sop(fn, waits=()):
            return q.op("scalar", fn, waits, "ss")

        def top(fn, waits=()):
            return q.op("tensor", fn, waits, "st")

        def ccol(i):
            return consts[:, i:i + 1]

        q.dma("sync", rowsb[0:1, 0:TT], iotar[:, :])
        q.dma("sync", consts[:, CV_CONVW:CV_CONVW + 64].rearrange(
            "p (k t) -> p k t", k=NCH),
              convw.rearrange("(k p) t -> p k t", p=P))
        for ap_, cv, n in ((convb, CV_CONVB, 16), (gateb, CV_GATEB, 16),
                           (lnw, CV_LNW, 16), (lnb, CV_LNB, 16),
                           (b1v, CV_B1, 16), (b0v, CV_B0, 64),
                           (cbv, CV_SILC, 16)):
            q.dma("sync", consts[:, cv:cv + n], ap_.rearrange("(k p) -> p k", p=P),
                  slow=True)
        for h in range(H):
            q.dma("sync", consts[:, CV_GAM + h:CV_GAM + h + 1],
                  gam16[h:h + 1].unsqueeze(0).broadcast_to([P, 1]))
        for j in range(4):
            q.dma("sync", consts[:, CV_MSEL + j:CV_MSEL + j + 1],
                  msel[j:j + 1].unsqueeze(0).broadcast_to([P, 1]))
        q.dma("sync", consts[:, CV_HM:CV_HM + 1],
              hmask[0:1].unsqueeze(0).broadcast_to([P, 1]))
        ldc = q.now("dsy")

        vop(lambda v: v.memset(ones128[:, :], 1.0))
        vop(lambda v: v.memset(onesc[:, :], 1.0))
        vop(lambda v: v.memset(eps5[:, 0:1], LN_EPS))
        sv_ones = vop(lambda v: v.memset(consts[:, CV_CARRY:CV_CARRY + 16], 0.0))

        s1 = sop(lambda s: s.activation(consts[:, CV_SILC:CV_SILC + 16],
                                        consts[:, CV_SILC:CV_SILC + 16], AF.Silu),
                 waits=[("dsy", ldc)])
        ss_lng = sop(lambda s: s.activation(consts[:, CV_LNGAM:CV_LNGAM + 16],
                                            consts[:, CV_GAM:CV_GAM + 16], AF.Ln))
        v1 = vop(lambda v: v.tensor_scalar(consts[:, CV_GT:CV_GT + 16],
                                           consts[:, CV_LNGAM:CV_LNGAM + 16],
                                           2048.0, None, ALU.mult),
                 waits=[("ss", ss_lng)])
        sop(lambda s: s.activation(consts[:, CV_G2048:CV_G2048 + 16],
                                   consts[:, CV_GT:CV_GT + 16], AF.Exp),
            waits=[("sv", v1)])
        vop(lambda v: v.memset(consts[:, CV_G512P:CV_G512P + 16], 1.0))
        prev_exp = 0
        for tau in range(1, NT):
            v2 = vop(lambda v, tau=tau: v.tensor_scalar(
                consts[:, CV_GT + 16:CV_GT + 32],
                consts[:, CV_LNGAM:CV_LNGAM + 16],
                512.0 * tau, None, ALU.mult),
                waits=[("ss", ss_lng), ("ss", prev_exp)])
            prev_exp = sop(lambda s, tau=tau: s.activation(
                consts[:, CV_G512P + 16 * tau:CV_G512P + 16 * (tau + 1)],
                consts[:, CV_GT + 16:CV_GT + 32], AF.Exp), waits=[("sv", v2)])
        ss_gam = q.now("ss")

        b_, psl, w_ = bk.acquire()
        tb = top(lambda t, psl=psl: t.matmul(psl, ones128[0:1, :],
                                             rowsb[0:1, 0:TT],
                                             start=True, stop=True),
                 waits=[("dsy", ldc), ("sv", sv_ones)] + w_)
        riot_sv = vop(lambda v, psl=psl: v.tensor_copy(riot[:, :], psl),
                      waits=[("st", tb)])
        bk.release(b_, "sv", riot_sv)

        split_units = []
        for src_t, hi_t, lo_t in ((w0T, w0hi_d, w0lo_d), (w1T, w1hi_d, w1lo_d)):
            for i in range(64):
                for hh in range(2):
                    c0, c1 = hh * 1024, (hh + 1) * 1024
                    split_units.append((src_t[i, :, c0:c1], hi_t[i, :, c0:c1],
                                        lo_t[i, :, c0:c1]))
        spl = {"i": 0, "store": [0, 0]}
        spl_in = [reg(O_SPL, 1024), reg(O_SPL + 1024, 1024)]
        spl_hi = [reg(O_SPLO, 512).bitcast(F16), reg(O_SPLO + 512, 512).bitcast(F16)]
        spl_lo = [reg(O_SPLO + 1024, 512).bitcast(F16),
                  reg(O_SPLO + 1536, 512).bitcast(F16)]

        def drip(n):
            for _ in range(n):
                i = spl["i"]
                if i >= len(split_units):
                    return
                spl["i"] += 1
                src, hd, ld_ = split_units[i]
                sl = i % 2
                d = q.dma("sync", spl_in[sl], src,
                          waits=[("dgp", spl["store"][sl])])
                s_ = sop(lambda s, sl=sl: s.copy(spl_hi[sl], spl_in[sl]),
                         waits=[("dsy", d)])
                v_ = vop(lambda v, sl=sl: v.tensor_tensor(
                    spl_lo[sl], spl_in[sl], spl_hi[sl], ALU.subtract),
                    waits=[("ss", s_)])
                q.dma("gpsimd", hd, spl_hi[sl], waits=[("sv", v_)])
                spl["store"][sl] = q.dma("gpsimd", ld_, spl_lo[sl])

        asp_in = [reg(O_RD, 2048), reg(O_RD + 2048, 2048)]
        asp_out = [reg(O_RD + 4096, 2048), reg(O_RD + 6144, 2048)]
        asp = {"i": 0, "store": [0, 0], "ready": {}}
        _amats = (wkT, wvT, wqT, woT)

        def adrip(n):
            for _ in range(n):
                i = asp["i"]
                if i >= 64:
                    return
                asp["i"] += 1
                mat, o = _amats[i // 16], i % 16
                sl = i % 2
                d = q.dma("sync", asp_in[sl], mat[o, :, :],
                          waits=[("dgp", asp["store"][sl])])
                hi = asp_out[sl][:, 0:1024].bitcast(F16)
                lo = asp_out[sl][:, 1024:2048].bitcast(F16)
                s_ = sop(lambda s, sl=sl, hi=hi: s.copy(hi, asp_in[sl]),
                         waits=[("dsy", d)])
                v_ = vop(lambda v, sl=sl, hi=hi, lo=lo: v.tensor_tensor(
                    lo, asp_in[sl], hi, ALU.subtract), waits=[("ss", s_)])
                asp["store"][sl] = q.dma("gpsimd", wapair_d[i, :, :],
                                         asp_out[sl].bitcast(F16),
                                         waits=[("sv", v_)])
                asp["ready"][i] = asp["store"][sl]

        wmbuf = reg3(O_RC, NCH, TT)
        NJ = MODW // TT
        ss_sil = s1
        prev_cp = 0
        wm_mm = [0, 0]
        for j in range(NJ):
            adrip(11)
            wb = wmbuf
            dj = q.dma("sync", wb[:, :, :],
                       wmods[:, j * TT:(j + 1) * TT].rearrange("(k p) t -> p k t", p=P),
                       waits=[("st", wm_mm[0])])
            b_, psl, w_ = bk.acquire()
            tm = None
            for k in range(NCH):
                tm = top(lambda t, psl=psl, k=k, wb=wb: t.matmul(
                    psl[0:1, :], consts[:, CV_SILC + k:CV_SILC + k + 1],
                    wb[:, k, :], start=(k == 0), stop=(k == NCH - 1)),
                    waits=([("dsy", dj), ("ss", ss_sil)] + w_) if k == 0 else [])
            wm_mm[0] = tm
            prev_cp = vop(lambda v, psl=psl, j=j: v.tensor_copy(
                modc[:, j * TT:(j + 1) * TT], psl[0:1, :]), waits=[("st", tm)])
            bk.release(b_, "sv", prev_cp)
        g1 = q.dma("gpsimd", ag1_in[:, :], modc[:, :], waits=[("sv", prev_cp)])
        cc1 = q.op("gpsimd", lambda g: g.collective_compute(
            "AllGather", ALU.bypass,
            replica_groups=[[0, 1, 2, 3], [4, 5, 6, 7]],
            ins=[ag1_in.opt()], outs=[ag1_out.opt()]),
            waits=[("dgp", g1)], inc="scc")
        gmod = 0
        for j in range(4):
            gmod = q.dma("gpsimd", consts[:, CV_MODF + 24 * j:CV_MODF + 24 * (j + 1)],
                         ag1_out[j, :].rearrange("(k p) -> p k", p=P),
                         waits=[("scc", cc1)] if j == 0 else [], slow=True)
        MF = CV_MODF
        vop(lambda v: v.tensor_copy(consts[:, CV_SA:CV_SA + 16],
                                    consts[:, MF:MF + 16]), waits=[("dgp", gmod)])
        vop(lambda v: v.tensor_scalar(consts[:, CV_ONEPCA:CV_ONEPCA + 16],
                                      consts[:, MF + 16:MF + 32], 1.0, None, ALU.add))
        vop(lambda v: v.tensor_copy(consts[:, CV_SB:CV_SB + 16],
                                    consts[:, MF + 48:MF + 64]))
        vop(lambda v: v.tensor_scalar(consts[:, CV_ONEPCB:CV_ONEPCB + 16],
                                      consts[:, MF + 64:MF + 80], 1.0, None, ALU.add))
        sv_e = 0
        for src, dst in ((MF + 32, CV_GA), (MF + 80, CV_GB)):
            sv_e = vop(lambda v, src=src: v.tensor_scalar(
                consts[:, CV_GT:CV_GT + 16], consts[:, src:src + 16],
                0.0, None, ALU.is_ge), waits=[("ss", ss_gam), ("sv", sv_e)])
            sv_e = vop(lambda v: v.tensor_scalar(
                consts[:, CV_GT:CV_GT + 16], consts[:, CV_GT:CV_GT + 16],
                2.0, -1.0, ALU.mult, ALU.add), waits=[("sv", sv_e)])
            sv_e = vop(lambda v, src=src: v.tensor_scalar(
                consts[:, CV_GT + 16:CV_GT + 32], consts[:, src:src + 16],
                -1.0, None, ALU.mult), waits=[("sv", sv_e)])
            sv_e = vop(lambda v, src=src: v.tensor_tensor(
                consts[:, CV_GT + 16:CV_GT + 32], consts[:, src:src + 16],
                consts[:, CV_GT + 16:CV_GT + 32], ALU.max), waits=[("sv", sv_e)])
            sv_e = vop(lambda v: v.tensor_scalar(
                consts[:, CV_GT + 16:CV_GT + 32], consts[:, CV_GT + 16:CV_GT + 32],
                0.25, None, ALU.max), waits=[("sv", sv_e)])
            sv_e = vop(lambda v, dst=dst: v.tensor_tensor(
                consts[:, dst:dst + 16], consts[:, CV_GT:CV_GT + 16],
                consts[:, CV_GT + 16:CV_GT + 32], ALU.mult), waits=[("sv", sv_e)])
        sv_mod = sv_e
        phA_st = q.now("st")

        def emit_ln(src3, n_t, dep_waits, xc_dst3=None):
            dst3 = src3 if xc_dst3 is None else xc_dst3
            b1_, ps1, w1_ = bk.acquire()
            b3_, ps3, w3_ = bk.acquire()
            tm = None
            tm3 = None
            prev_mm = 0
            for k in range(NCH):
                tm = top(lambda t, ps1=ps1, k=k: t.matmul(
                    ps1[0:1, 0:n_t], onesc[:, 0:1], src3[:, k, :],
                    start=(k == 0), stop=(k == NCH - 1)),
                    waits=(list(dep_waits) + w1_) if k == 0 else [])
                sq = reg(O_SQ, n_t) if k % 2 == 0 else reg(O_SQ + 512, n_t)
                vs = vop(lambda v, k=k, sq=sq: v.tensor_tensor(
                    sq, src3[:, k, :], src3[:, k, :], ALU.mult),
                    waits=(list(dep_waits) if k == 0 else []) +
                          ([("st", prev_mm)] if k >= 2 else []) +
                          ([sqg[k % 2]] if k < 2 else []))
                tm3 = top(lambda t, ps3=ps3, k=k, sq=sq: t.matmul(
                    ps3[0:1, 0:n_t], onesc[:, 0:1], sq,
                    start=(k == 0), stop=(k == NCH - 1)),
                    waits=[("sv", vs)] + (w3_ if k == 0 else []))
                prev_mm = tm3
            mu = rowsb[0:1, 0:n_t]
            e2 = rowsb[0:1, 512:512 + n_t]
            musq = reg(O_SQ, n_t)[0:1, :]
            vv = vop(lambda v, ps1=ps1: v.tensor_scalar(
                mu, ps1[0:1, 0:n_t], 1.0 / C, None, ALU.mult), waits=[("st", tm)])
            bk.release(b1_, "sv", vv)
            vv = vop(lambda v, ps3=ps3: v.tensor_scalar(
                e2, ps3[0:1, 0:n_t], 1.0 / C, None, ALU.mult),
                waits=[("st", tm3), ("sv", vv)])
            bk.release(b3_, "sv", vv)
            vv = vop(lambda v: v.tensor_tensor(musq, mu, mu, ALU.mult),
                     waits=[("sv", vv)])
            vv = vop(lambda v: v.tensor_tensor(e2, e2, musq, ALU.subtract),
                     waits=[("sv", vv)])
            sqg[0] = ("sv", vv)
            sqg[1] = ("st", tm3)
            sr = sop(lambda s: s.activation(e2, e2, AF.Sqrt, bias=eps5[0:1, 0:1]),
                     waits=[("sv", vv)])
            vv3 = vop(lambda v: v.reciprocal(e2, e2), waits=[("ss", sr)])
            vv3 = vop(lambda v: v.tensor_tensor(mu, mu, e2, ALU.mult),
                      waits=[("sv", vv3)])
            b4_, ps4, w4_ = bk.acquire()
            b5_, ps5, w5_ = bk.acquire()
            t4 = top(lambda t, ps4=ps4: t.matmul(ps4[:, 0:n_t], ones128[0:1, :], e2,
                                                 start=True, stop=True),
                     waits=[("sv", vv3)] + w4_)
            t5 = top(lambda t, ps5=ps5: t.matmul(ps5[:, 0:n_t], ones128[0:1, :], mu,
                                                 start=True, stop=True),
                     waits=[("sv", vv3)] + w5_)
            last = 0
            for k in range(NCH):
                last = vop(lambda v, k=k, ps4=ps4: v.tensor_tensor(
                    dst3[:, k, :], src3[:, k, :], ps4[:, 0:n_t], ALU.mult),
                    waits=[("st", t4)] if k == 0 else [])
            for k in range(NCH):
                last = vop(lambda v, k=k, ps5=ps5: v.tensor_tensor(
                    dst3[:, k, :], dst3[:, k, :], ps5[:, 0:n_t], ALU.subtract),
                    waits=[("st", t5)] if k == 0 else [])
            bk.release(b4_, "sv", last)
            bk.release(b5_, "sv", last)
            return last

        wba = [reg3(O_WBA, NCH, P), reg3(O_WBA + 2048, NCH, P)]
        wba_mm = [0, 0]

        def emit_attn_mm(wT_ap, rhs3, rhs_waits, evac, drip_n=0):
            for o in range(NCH):
                if drip_n:
                    drip(drip_n)
                sl = o % 2
                d = q.dma("sync", wba[sl][:, :, :],
                          wT_ap[o, :, :].rearrange("p (k o) -> p k o", k=NCH),
                          waits=[("st", wba_mm[sl])])
                b_, psl, w_ = bk.acquire()
                tm = None
                for k in range(NCH):
                    tm = top(lambda t, psl=psl, k=k, sl=sl: t.matmul(
                        psl, wba[sl][:, k, :], rhs3[:, k, :],
                        start=(k == 0), stop=(k == NCH - 1)),
                        waits=([("dsy", d)] + list(rhs_waits) + w_) if k == 0 else [])
                wba_mm[sl] = tm
                sem, val = evac(o, psl, tm)
                bk.release(b_, sem, val)

        wba_pair = [reg(O_WBA, 2048).bitcast(F16),
                    reg(O_WBA + 2048, 2048).bitcast(F16)]
        wba_h = [reg3h(O_WBA, NCH, P), reg3h(O_WBA + 2048, NCH, P)]
        wba_l = [reg3h(O_WBA + 1024, NCH, P), reg3h(O_WBA + 3072, NCH, P)]

        def emit_attn_mm_f16(mat_i, rhs_h, rhs_l, rhs_waits, evac, drip_n=0):
            for o in range(NCH):
                img = mat_i * 16 + o
                while img not in asp["ready"]:
                    adrip(1)
                if drip_n:
                    drip(drip_n)
                sl = o % 2
                d = q.dma("sync", wba_pair[sl], wapair_d[img, :, :],
                          waits=[("st", wba_mm[sl]), ("dgp", asp["ready"][img])])
                b_, psl, w_ = bk.acquire()
                tm = None
                for k in range(NCH):
                    w0 = ([("dsy", d)] + list(rhs_waits) + w_) if k == 0 else []
                    tm = top(lambda t, psl=psl, k=k, sl=sl: t.matmul(
                        psl, wba_h[sl][:, k, :], rhs_h[:, k, :],
                        start=(k == 0), stop=False), waits=w0)
                    tm = top(lambda t, psl=psl, k=k, sl=sl: t.matmul(
                        psl, wba_h[sl][:, k, :], rhs_l[:, k, :],
                        start=False, stop=False))
                    tm = top(lambda t, psl=psl, k=k, sl=sl: t.matmul(
                        psl, wba_l[sl][:, k, :], rhs_h[:, k, :],
                        start=False, stop=(k == NCH - 1)))
                wba_mm[sl] = tm
                sem, val = evac(o, psl, tm)
                bk.release(b_, sem, val)

        def emit_pairs(src3, dst_h, dst_l, waits):
            sv = 0
            for k in range(NCH):
                sv = vop(lambda v, k=k, src3=src3, dst_h=dst_h: v.tensor_copy(
                    dst_h[:, k, :], src3[:, k, :]),
                    waits=list(waits) if k == 0 else [])
            for k in range(NCH):
                sv = vop(lambda v, k=k, src3=src3, dst_h=dst_h, dst_l=dst_l:
                         v.tensor_tensor(dst_l[:, k, :], src3[:, k, :],
                                         dst_h[:, k, :], ALU.subtract))
            return sv

        def emit_norms(src3, ev_dep):
            prev_mm = [0, 0]
            last = 0
            for k in range(NCH):
                j = k % 2
                sq = reg(O_SQ, TT) if j == 0 else reg(O_SQ + 512, TT)
                vs = vop(lambda v, k=k, sq=sq, src3=src3: v.tensor_tensor(
                    sq, src3[:, k, :], src3[:, k, :], ALU.mult),
                    waits=([("ss", ev_dep)] if k == 0 else []) +
                          [("st", prev_mm[j]), sqg[j]])
                bn_, psn, wn_ = bk.acquire()
                tmn = top(lambda t, psn=psn, sq=sq: t.matmul(
                    psn, ones128[:, :], sq, start=True, stop=True),
                    waits=[("sv", vs)] + wn_)
                prev_mm[j] = tmn
                sq_s = sop(lambda s, psn=psn, sq=sq: s.activation(
                    sq, psn, AF.Sqrt), waits=[("st", tmn)])
                bk.release(bn_, "ss", sq_s)
                vn1 = vop(lambda v, sq=sq: v.tensor_scalar(
                    sq, sq, 1e-12, None, ALU.max), waits=[("ss", sq_s)])
                vn2 = vop(lambda v, sq=sq: v.reciprocal(sq, sq), waits=[("sv", vn1)])
                last = vop(lambda v, k=k, sq=sq, src3=src3: v.tensor_tensor(
                    src3[:, k, :], src3[:, k, :], sq, ALU.mult),
                    waits=[("sv", vn2)])
                sqg[j] = ("sv", last)
            return last

        XB = [reg3(O_X0, NCH, 3 + TT), reg3(O_X1, NCH, 3 + TT)]
        RB = reg3(O_RB, NCH, TT)
        RC = reg3(O_RC, NCH, TT)
        RD = reg3(O_RD, NCH, TT)
        ph_h = reg3h(O_RD, NCH, TT)
        ph_l = reg3h(O_RD + 4096, NCH, TT)
        powb = reg(O_SQ, TT)
        gstg = reg(O_POW, TT)
        tail3 = consts[:, CV_TAIL:CV_TAIL + 48].rearrange("p (k t) -> p k t", k=NCH)

        sin_cell = [0]

        def emit_ag2(scan_done):
            g2 = q.dma("gpsimd", ag2_in[:, :], consts[:, CV_CARRY:CV_CARRY + 16],
                       waits=[("sv", scan_done)])
            cc2 = q.op("gpsimd", lambda g: g.collective_compute(
                "AllGather", ALU.bypass,
                replica_groups=[[0, 1, 2, 3], [4, 5, 6, 7]],
                ins=[ag2_in.opt()], outs=[ag2_out.opt()]),
                waits=[("dgp", g2)], inc="scc")
            gc = 0
            for j in range(4):
                gc = q.dma("gpsimd", consts[:, CV_CJ + 16 * j:CV_CJ + 16 * (j + 1)],
                           ag2_out[j * P:(j + 1) * P, :],
                           waits=[("scc", cc2)] if j == 0 else [])
            sv_a = vop(lambda v: v.memset(consts[:, CV_SIN:CV_SIN + 16], 0.0),
                       waits=[("dgp", gc)])
            sv_a = vop(lambda v: v.tensor_copy(consts[:, CV_GT:CV_GT + 16],
                                               consts[:, CV_CJ:CV_CJ + 16]),
                       waits=[("sv", sv_a)])
            for j in range(4):
                sv_a = vop(lambda v, j=j: v.scalar_tensor_tensor(
                    consts[:, CV_SIN:CV_SIN + 16], consts[:, CV_GT:CV_GT + 16],
                    ccol(CV_MSEL + j), consts[:, CV_SIN:CV_SIN + 16],
                    ALU.mult, ALU.add), waits=[("sv", sv_a)])
                if j < 3:
                    sv_a = vop(lambda v: v.tensor_tensor(
                        consts[:, CV_GT:CV_GT + 16], consts[:, CV_GT:CV_GT + 16],
                        consts[:, CV_G2048:CV_G2048 + 16], ALU.mult),
                        waits=[("sv", sv_a)])
                    sv_a = vop(lambda v, j=j: v.tensor_tensor(
                        consts[:, CV_GT:CV_GT + 16], consts[:, CV_GT:CV_GT + 16],
                        consts[:, CV_CJ + 16 * (j + 1):CV_CJ + 16 * (j + 2)],
                        ALU.add), waits=[("sv", sv_a)])
            sin_cell[0] = sv_a

        x_pref = [0, 0]
        x_pref[0] = q.dma("gpsimd", XB[0][:, :, :],
                          xfm[:, 0:3 + TT].rearrange("(k p) t -> p k t", p=P))
        qmm_st = [0, 0]
        az_store = 0
        prev_gq = 0

        for tau in range(NT):
            buf = XB[tau % 2]
            bufm = buf[:, :, 3:3 + TT]
            if tau + 1 < NT:
                x_pref[(tau + 1) % 2] = q.dma(
                    "gpsimd", XB[(tau + 1) % 2][:, :, 3:3 + TT],
                    xfm[:, 3 + (tau + 1) * TT:3 + (tau + 2) * TT].rearrange(
                        "(k p) t -> p k t", p=P),
                    waits=[("st", qmm_st[(tau + 1) % 2]), ("scc", cc1)])
            if tau == 0:
                hsv = emit_ln(buf[:, :, 0:3], 3, [("dgp", x_pref[0]), ("sv", sv_mod)])
                for k in range(NCH):
                    hsv = vop(lambda v, k=k, buf=buf: v.tensor_scalar(
                        buf[:, k, 0:3], buf[:, k, 0:3],
                        ccol(CV_ONEPCA + k), ccol(CV_SA + k), ALU.mult, ALU.add),
                        waits=[("sv", hsv)] if k == 0 else [])
                hm = 0
                for k in range(NCH):
                    hm = vop(lambda v, k=k, buf=buf: v.tensor_scalar(
                        buf[:, k, 0:3], buf[:, k, 0:3], ccol(CV_HM), None, ALU.mult),
                        waits=[("sv", hsv)] if k == 0 else [])
                dep = [("sv", hm)]
            else:
                tl = vop(lambda v, buf=buf: v.tensor_copy(buf[:, :, 0:3], tail3),
                         waits=[("dgp", x_pref[tau % 2])])
                dep = [("sv", tl)]
            lsv = emit_ln(bufm, TT, dep + [("dgp", x_pref[tau % 2]), ("sv", sv_mod)])
            h1_sv = 0
            for k in range(NCH):
                h1_sv = vop(lambda v, k=k, bufm=bufm: v.tensor_scalar(
                    bufm[:, k, :], bufm[:, k, :],
                    ccol(CV_ONEPCA + k), ccol(CV_SA + k), ALU.mult, ALU.add),
                    waits=[("sv", lsv)] if k == 0 else [])
            tail_sv = vop(lambda v, buf=buf: v.tensor_copy(
                tail3, buf[:, :, TT:TT + 3]), waits=[("sv", h1_sv)])
            if tau == 0:
                adrip(64)
                asp_done = max(asp["store"])
                pair_dep = [("sv", h1_sv), ("dgp", asp_done)]
            else:
                pair_dep = [("sv", h1_sv), ("sv", prev_gq)]
            pair_sv = emit_pairs(bufm, ph_h, ph_l, pair_dep)

            def evac_to(dst3, extra):
                def f(o, psl, tm):
                    c = sop(lambda s, o=o, psl=psl, dst3=dst3: s.copy(
                        dst3[:, o, :], psl),
                        waits=[("st", tm)] + (extra if o == 0 else []))
                    return ("ss", c)
                return f

            emit_attn_mm_f16(0, ph_h, ph_l, [("sv", pair_sv)],
                             evac_to(RC, [("dgp", az_store), ("st", phA_st)]),
                             drip_n=1)
            k_ev = q.now("ss")

            csv = 0
            for k in range(NCH):
                csv = vop(lambda v, k=k, buf=buf: v.tensor_scalar(
                    RB[:, k, :], buf[:, k, 0:TT], ccol(CV_CONVW + 4 * k),
                    None, ALU.mult),
                    waits=[("sv", h1_sv), ("dgp", az_store)] if k == 0 else [])
                for kk in range(1, 4):
                    csv = vop(lambda v, k=k, kk=kk, buf=buf: v.scalar_tensor_tensor(
                        RB[:, k, :], buf[:, k, kk:kk + TT],
                        ccol(CV_CONVW + 4 * k + kk), RB[:, k, :], ALU.mult, ALU.add))
            ssv = 0
            for k in range(NCH):
                ssv = sop(lambda s, k=k: s.activation(
                    RB[:, k, :], RB[:, k, :], AF.Silu, bias=ccol(CV_CONVB + k)),
                    waits=[("sv", csv)] if k == 0 else [])
            kn_sv = emit_norms(RC, k_ev)

            def evac_u(o, psl, tm):
                c = vop(lambda v, o=o, psl=psl: v.tensor_tensor(
                    RC[:, o, :], RC[:, o, :], psl, ALU.mult),
                    waits=[("st", tm)] + ([("sv", kn_sv)] if o == 0 else []))
                return ("sv", c)

            emit_attn_mm_f16(1, ph_h, ph_l, [("sv", pair_sv)], evac_u, drip_n=1)
            u_sv = q.now("sv")

            gate_st = [0]

            def evac_gate(o, psl, tm):
                c = sop(lambda s, o=o, psl=psl: s.activation(
                    gstg, psl, AF.Sigmoid, bias=ccol(CV_GATEB + o)),
                    waits=[("st", tm), ("dgp", gate_st[0])])
                gate_st[0] = q.dma("gpsimd", gate_d[o * P:(o + 1) * P, :], gstg,
                                   waits=[("ss", c)])
                return ("ss", c)

            emit_attn_mm(wgT, RB, [("ss", ssv)], evac_gate, drip_n=1)
            gate_dma = gate_st[0]
            gate_mm_st = q.now("st")

            scan_k = []
            scan_sv = 0
            for k in range(NCH):
                scan_sv = vop(lambda v, k=k: v.tensor_tensor_scan(
                    RB[:, k, :], ccol(CV_GAM + k).broadcast_to([P, TT]),
                    RC[:, k, :], ccol(CV_CARRY + k), ALU.mult, ALU.add),
                    waits=([("sv", u_sv), ("st", gate_mm_st)] if k == 0 else []))
                scan_k.append(scan_sv)
                scan_sv = vop(lambda v, k=k: v.tensor_copy(
                    ccol(CV_CARRY + k), RB[:, k, TT - 1:TT]),
                    waits=[("sv", scan_sv)])
            if tau == NT - 1:
                emit_ag2(scan_sv)

            def evac_q(o, psl, tm):
                c = sop(lambda s, o=o, psl=psl: s.copy(RC[:, o, :], psl),
                        waits=[("st", tm), ("sv", scan_k[o])])
                return ("ss", c)

            emit_attn_mm_f16(2, ph_h, ph_l, [("sv", pair_sv)], evac_q, drip_n=1)
            q_ev = q.now("ss")
            qmm_st[tau % 2] = q.now("st")
            qn_sv = emit_norms(RC, q_ev)

            gq_sv = 0
            grel = [0, 0]
            for k in range(NCH):
                j = k % 2
                gb_buf = reg(O_SQ, TT) if j == 0 else reg(O_SQ + 512, TT)
                grel[j] = q.dma("gpsimd", gb_buf,
                                gate_d[k * P:(k + 1) * P, :],
                                waits=[("dgp", gate_dma), sqg[j]])
                gq_sv = vop(lambda v, k=k, gb_buf=gb_buf: v.tensor_tensor(
                    RC[:, k, :], RC[:, k, :], gb_buf, ALU.mult),
                    waits=[("dgp", grel[j])] +
                          ([("sv", qn_sv)] if k == 0 else []))
                sqg[j] = ("sv", gq_sv)
            prev_gq = gq_sv
            q.dma("gpsimd", S_d[:, tau * TT:(tau + 1) * TT].rearrange(
                "(k p) t -> p k t", p=P), RB[:, :, :], waits=[("sv", scan_sv)])
            az_store = q.dma("gpsimd", G_d[:, tau * TT:(tau + 1) * TT].rearrange(
                "(k p) t -> p k t", p=P), RC[:, :, :], waits=[("sv", gq_sv)])

        drip(len(split_units))
        assert spl["i"] == len(split_units)
        spl_done = q.now("dgp")
        phB_sv = q.now("sv")
        phB_ss = q.now("ss")
        phB_st = q.now("st")

        XPs = [XB[0][:, :, 0:TT], XB[1][:, :, 0:TT]]
        RBY = RB
        D16 = reg3h(O_RD, 32, TT)
        H2 = reg3h(O_H2, NCH, TT)
        wfflat = [arena[:, O_RC + 2048 * i:O_RC + 2048 * (i + 1)].bitcast(F16)
                  for i in range(4)]
        wf = [wfflat[i].rearrange("p (o k c) -> p o k c", o=2, k=NCH)
              for i in range(4)]
        wf_mm = [0, 0, 0, 0]
        e16 = [reg(O_POW, 256).bitcast(F16), reg(O_POW + 256, 256).bitcast(F16)]

        out_store = [0, 0]
        final_sv = phB_sv
        last_w0_mm = phB_st
        last_w1_mm = phB_st
        corr_sv = [0, 0]
        dZ = 0

        def emit_corr(tau, extra_waits):
            XP = XPs[tau % 2]
            dA = q.dma("gpsimd", XP[:, :, :],
                       S_d[:, tau * TT:(tau + 1) * TT].rearrange(
                           "(k p) t -> p k t", p=P),
                       waits=[("dgp", az_store), ("dgp", out_store[tau % 2])] +
                             list(extra_waits))
            wv = vop(lambda v, tau=tau: v.tensor_tensor(
                consts[:, CV_WTAU:CV_WTAU + 16], consts[:, CV_SIN:CV_SIN + 16],
                consts[:, CV_G512P + 16 * tau:CV_G512P + 16 * (tau + 1)], ALU.mult),
                waits=[("sv", sin_cell[0])])
            sv = 0
            pe = 0
            for k in range(NCH):
                pe = sop(lambda s, k=k: s.activation(
                    powb, riot[:, :], AF.Exp, scale=ccol(CV_LNGAM + k)),
                    waits=([("ss", ss_gam), ("sv", wv), ("dgp", dA), sqg[0]]
                           if k == 0 else [("sv", sv)]))
                sv = vop(lambda v, k=k, XP=XP: v.scalar_tensor_tensor(
                    XP[:, k, :], powb, ccol(CV_WTAU + k), XP[:, k, :],
                    ALU.mult, ALU.add),
                    waits=[("ss", pe)] + ([("sv", wv), ("dgp", dA)] if k == 0 else []))
            sqg[0] = ("sv", sv)
            corr_sv[tau % 2] = sv
            return sv

        for tau in range(NT):
            XP = XPs[tau % 2]
            if tau == 0:
                emit_corr(0, [("sv", final_sv), ("ss", phB_ss), ("st", phB_st)])
                dZ = q.dma("gpsimd", RBY[:, :, :],
                           G_d[:, 0:TT].rearrange("(k p) t -> p k t", p=P),
                           waits=[("dgp", az_store), ("sv", final_sv)])

            o_sv = 0
            for k in range(NCH):
                o_sv = vop(lambda v, k=k, XP=XP: v.tensor_tensor(
                    XP[:, k, :], XP[:, k, :], RBY[:, k, :], ALU.mult),
                    waits=[("dgp", dZ), ("sv", corr_sv[tau % 2])]
                    if k == 0 else [])

            pair_d = emit_pairs(XP, ph_h, ph_l,
                                [("sv", o_sv), ("st", last_w1_mm),
                                 ("st", qmm_st[0]), ("st", qmm_st[1])])

            def evac_y(o, psl, tm):
                c = sop(lambda s, o=o, psl=psl: s.copy(RBY[:, o, :], psl),
                        waits=[("st", tm)])
                return ("ss", c)

            emit_attn_mm_f16(3, ph_h, ph_l, [("sv", pair_d)], evac_y)
            wo_st = q.now("st")
            y_ev = q.now("ss")
            l2 = emit_ln(RBY, TT, [("ss", y_ev)])
            dx = q.dma("gpsimd", XP[:, :, :],
                       xfm[:, 3 + tau * TT:3 + (tau + 1) * TT].rearrange(
                           "(k p) t -> p k t", p=P),
                       waits=[("st", wo_st)])
            for k in range(NCH):
                l2 = vop(lambda v, k=k: v.tensor_scalar(
                    RBY[:, k, :], RBY[:, k, :], ccol(CV_LNW + k), ccol(CV_LNB + k),
                    ALU.mult, ALU.add), waits=[("sv", l2)] if k == 0 else [])
            xp_sv = 0
            for k in range(NCH):
                xp_sv = vop(lambda v, k=k, XP=XP: v.scalar_tensor_tensor(
                    XP[:, k, :], RBY[:, k, :], ccol(CV_GA + k), XP[:, k, :],
                    ALU.mult, ALU.add),
                    waits=[("sv", l2), ("dgp", dx)] if k == 0 else [])
            l3 = emit_ln(XP, TT, [("sv", xp_sv)], xc_dst3=RBY)
            h2_sv = 0
            h2_k = []
            for k in range(NCH):
                h2_sv = vop(lambda v, k=k: v.tensor_scalar(
                    H2[:, k, :], RBY[:, k, :], ccol(CV_ONEPCB + k), ccol(CV_SB + k),
                    ALU.mult, ALU.add),
                    waits=([("sv", l3)] + ([("st", last_w0_mm)] if tau > 0 else []) +
                           ([("dgp", spl_done)] if tau == 0 else []))
                    if k == 0 else [])
                h2_k.append(h2_sv)

            ge_k = [0] * 32
            for half in range(2):
                w0dep = [("dgp", spl_done)] + \
                    ([("st", last_w1_mm)] if (half == 1 or tau > 0) else [])
                for og in range(16):
                    o0 = half * 32 + og * 2
                    slh, sll = (og % 2) * 2, (og % 2) * 2 + 1
                    for oo in range(2):
                        q.dma("sync", wfflat[slh][:, oo * 2048:(oo + 1) * 2048],
                              w0hi_d[o0 + oo, :, :],
                              waits=[("st", wf_mm[slh]), ("dgp", spl_done),
                                     ("dgp", az_store)])
                        dwl = q.dma(
                            "sync", wfflat[sll][:, oo * 2048:(oo + 1) * 2048],
                            w0lo_d[o0 + oo, :, :],
                            waits=[("st", wf_mm[sll]), ("dgp", spl_done),
                                   ("dgp", az_store)])
                    for oo in range(2):
                        b_, psl, w_ = bk.acquire()
                        tm = None
                        for k in range(NCH):
                            tm = top(lambda t, psl=psl, k=k, slh=slh, oo=oo: t.matmul(
                                psl, wf[slh][:, oo, k, :],
                                H2[:, k, :], start=(k == 0), stop=False),
                                waits=([("dsy", dwl), ("sv", h2_k[k])] + w0dep + w_)
                                if (k == 0 and oo == 0) else
                                ([("dsy", dwl), ("sv", h2_k[k])] if k == 0 else
                                 [("sv", h2_k[k])] if oo == 0 else []))
                            tm = top(lambda t, psl=psl, k=k, sll=sll, oo=oo: t.matmul(
                                psl, wf[sll][:, oo, k, :],
                                H2[:, k, :], start=False, stop=(k == NCH - 1)))
                        wf_mm[slh] = tm
                        wf_mm[sll] = tm
                        last_w0_mm = tm
                        ge = sop(lambda s, psl=psl, og=og, oo=oo, o0=o0: s.activation(
                            D16[:, og * 2 + oo, :], psl, AF.Gelu,
                            bias=ccol(CV_B0 + o0 + oo)), waits=[("st", tm)])
                        ge_k[og * 2 + oo] = ge
                        bk.release(b_, "ss", ge)
                d16_ss = q.now("ss")
                if half == 1 and tau + 1 < NT:
                    emit_corr(tau + 1, [])
                for op_ in range(8):
                    banks = []
                    for oo in range(2):
                        banks.append(bk.acquire())
                    dmas = []
                    for kb in range(2):
                        kbg = half * 2 + kb
                        slh, sll = (kb % 2) * 2, (kb % 2) * 2 + 1
                        for oo in range(2):
                            q.dma("sync", wfflat[slh][:, oo * 2048:(oo + 1) * 2048],
                                  w1hi_d[kbg * 16 + op_ * 2 + oo, :, :],
                                  waits=[("st", wf_mm[slh]), ("dgp", spl_done)])
                            dl = q.dma(
                                "sync", wfflat[sll][:, oo * 2048:(oo + 1) * 2048],
                                w1lo_d[kbg * 16 + op_ * 2 + oo, :, :],
                                waits=[("st", wf_mm[sll]), ("dgp", spl_done)])
                        dmas.append((dl, slh, sll))
                    kb_last = [0, 0]
                    for kb in range(2):
                        dl, slh, sll = dmas[kb]
                        for k in range(NCH):
                            for oo in range(2):
                                bix, psl, wb_ = banks[oo]
                                w01 = [("dsy", dl)] if k == 0 else []
                                w01 = w01 + ([("ss", ge_k[kb * 16 + k])]
                                             if oo == 0 else [])
                                if k == 0 and kb == 0:
                                    w01 = w01 + wb_
                                tm = top(lambda t, psl=psl, k=k, kb=kb, oo=oo,
                                         slh=slh: t.matmul(
                                    psl, wf[slh][:, oo, k, :],
                                    D16[:, kb * 16 + k, :],
                                    start=(kb == 0 and k == 0), stop=False),
                                    waits=w01)
                                tm = top(lambda t, psl=psl, k=k, kb=kb, oo=oo,
                                         sll=sll: t.matmul(
                                    psl, wf[sll][:, oo, k, :],
                                    D16[:, kb * 16 + k, :],
                                    start=False, stop=(kb == 1 and k == NCH - 1)))
                                kb_last[kb] = tm
                                if kb == 1 and k == NCH - 1:
                                    last_w1_mm = tm
                        wf_mm[slh] = kb_last[kb]
                        wf_mm[sll] = kb_last[kb]
                    for oo in range(2):
                        bix, psl, wb_ = banks[oo]
                        e = op_ * 2 + oo
                        if half == 0:
                            ev = sop(lambda s, psl=psl, e=e: s.activation(
                                RBY[:, e, :], psl, AF.Identity, bias=ccol(CV_B1 + e)),
                                waits=[("st", last_w1_mm)])
                            bk.release(bix, "ss", ev)
                        else:
                            ev = vop(lambda v, psl=psl, e=e: v.tensor_tensor(
                                e16[e % 2], RBY[:, e, :], psl, ALU.add),
                                waits=[("st", last_w1_mm)] +
                                      ([("sv", final_sv)] if e >= 2 else []))
                            bk.release(bix, "sv", ev)
                            final_sv = vop(lambda v, e=e, XP=XP: v.scalar_tensor_tensor(
                                XP[:, e, :], e16[e % 2], ccol(CV_GB + e),
                                XP[:, e, :], ALU.mult, ALU.add),
                                waits=[("sv", ev)])
            out_store[tau % 2] = q.dma(
                "gpsimd", out_fm[:, tau * TT:(tau + 1) * TT].rearrange(
                    "(k p) t -> p k t", p=P), XP[:, :, :], waits=[("sv", final_sv)])
            if tau + 1 < NT:
                dZ = q.dma("gpsimd", RBY[:, :, :],
                           G_d[:, (tau + 1) * TT:(tau + 2) * TT].rearrange(
                               "(k p) t -> p k t", p=P),
                           waits=[("sv", final_sv)])

        q.op("sync", lambda e: None, waits=[("dgp", q.now("dgp")),
                                            ("dsy", q.now("dsy"))])

        block = stack.enter_context(nc.Block())
        q.finalize(block)
    return nc
def kernel(**inputs):
    x = np.ascontiguousarray(inputs["L0_attn_x"], dtype=np.float32)
    def attn_img(w):
        return np.ascontiguousarray(
            np.asarray(w, np.float32).T.reshape(16, 128, 16, 128)
            .transpose(2, 1, 0, 3).reshape(16, 128, 2048))

    WqT = attn_img(inputs["L0_attn_Wq"])
    WkT = attn_img(inputs["L0_attn_Wk"])
    WvT = attn_img(inputs["L0_attn_Wv"])
    WgT = attn_img(inputs["L0_attn_gate_w"])
    WoT = attn_img(inputs["L0_attn_Wo"])
    W0T = np.ascontiguousarray(
        np.asarray(inputs["L0_ffn_W0"], np.float32).T.reshape(16, 128, 64, 128)
        .transpose(2, 1, 0, 3).reshape(64, 128, 2048))
    W1T = np.ascontiguousarray(
        np.asarray(inputs["L0_ffn_W1"], np.float32).T.reshape(4, 16, 128, 16, 128)
        .transpose(0, 3, 2, 1, 4).reshape(64, 128, 2048))
    wmod = np.asarray(inputs["L0_wmod"], dtype=np.float32)
    cin = np.asarray(inputs["L0_c"], dtype=np.float32)
    conv_w = np.ascontiguousarray(inputs["L0_attn_conv_w"][:, 0, :], dtype=np.float32)
    convb = np.ascontiguousarray(inputs["L0_attn_conv_b"], dtype=np.float32)
    gateb = np.ascontiguousarray(inputs["L0_attn_gate_b"], dtype=np.float32)
    lnw = np.ascontiguousarray(inputs["L0_attn_ln_w"], dtype=np.float32)
    lnb = np.ascontiguousarray(inputs["L0_attn_ln_b"], dtype=np.float32)
    gam = np.ascontiguousarray(inputs["L0_attn_gamma"].reshape(H), dtype=np.float32)
    b0 = np.ascontiguousarray(inputs["L0_ffn_bias0"], dtype=np.float32)
    b1 = np.ascontiguousarray(inputs["L0_ffn_bias1"], dtype=np.float32)

    iota = np.arange(1, TT + 1, dtype=np.float32).reshape(1, TT)

    nc = build_nc()
    in_maps = []
    for r in range(8):
        b, pos = r // 4, r % 4
        t0 = TC * pos
        if t0 == 0:
            halo = np.zeros((C, 3), np.float32)
        else:
            halo = np.ascontiguousarray(x[b, t0 - 3:t0].T)
        xfm = np.ascontiguousarray(
            np.concatenate([halo, x[b, t0:t0 + TC].T], axis=1))
        ms = np.zeros(4, np.float32)
        if pos > 0:
            ms[pos - 1] = 1.0
        in_maps.append({
            "xfm": xfm, "wqT": WqT, "wkT": WkT, "wvT": WvT, "wgT": WgT,
            "woT": WoT, "w0T": W0T, "w1T": W1T,
            "wmods": np.ascontiguousarray(wmod[:, pos * MODW:(pos + 1) * MODW]),
            "cbv": np.ascontiguousarray(cin[b]),
            "convw": conv_w, "convb": convb, "gateb": gateb,
            "lnw": lnw, "lnb": lnb, "gam16": gam, "b0v": b0, "b1v": b1,
            "msel": ms, "hmask": np.array([0.0 if pos == 0 else 1.0], np.float32),
            "iotar": iota,
        })
    res = run_bass_kernel_spmd(nc, in_maps, core_ids=list(range(8)))
    out = np.empty((2, 8192, C), np.float32)
    for r in range(8):
        b, pos = r // 4, r % 4
        out[b, TC * pos:TC * (pos + 1)] = res.results[r]["out_fm"].T
    return out
